# version 04
# baseline (speedup 1.0000x reference)
import numpy as np
import ml_dtypes
from contextlib import ExitStack

import concourse.bass as bass
import concourse.mybir as mybir
from concourse.bass_utils import run_bass_kernel_spmd

F32 = mybir.dt.float32
BF16 = mybir.dt.bfloat16
I32 = mybir.dt.int32
AF = mybir.ActivationFunctionType
ALU = mybir.AluOpType
AX = mybir.AxisListType

D = 2048
S = 8192
B = 2
NCORE = 8
TPC = 2 * S // NCORE
HB = TPC // 2
KT = D // 128
CH = 512
NCH = TPC // CH

BF = ml_dtypes.bfloat16


class SC:

    def __init__(self, ctx, nc, name):
        self.h = ctx.enter_context(nc.semaphore(name))
        self.n = 0

    def inc(self, inst, k=1):
        inst.then_inc(self.h, k)
        self.n += k
        return self.n


def _r16(ap):
    return ap.rearrange("(kt p) t -> p kt t", p=128)


def build_l0qkv():
    nc = bass.Bass("TRN2", target_bir_lowering=False)
    xs = nc.dram_tensor("xs", [D, TPC], F32, kind="ExternalInput").ap()
    w = nc.dram_tensor("w", [D, 3 * D], F32, kind="ExternalInput").ap()
    bvec = nc.dram_tensor("bvec", [1, 3 * D], F32, kind="ExternalInput").ap()
    qT = nc.dram_tensor("qT", [D, TPC], F32, kind="ExternalOutput").ap()
    part = nc.dram_tensor("part", [128, 96], F32, kind="ExternalOutput").ap()

    with ExitStack() as ctx:
        e = ctx.enter_context
        xbf = e(nc.sbuf_tensor("xbf", [128, KT, TPC], BF16))
        wbuf = [e(nc.sbuf_tensor(f"wbuf{p}", [128, KT, 128], BF16)) for p in range(2)]
        brow = e(nc.sbuf_tensor("brow", [1, 3 * D], F32))
        bcol = e(nc.sbuf_tensor("bcol", [128, 48], F32))
        one_sb = e(nc.sbuf_tensor("one_sb", [1, 1], F32))
        qsb = [e(nc.sbuf_tensor(f"qsb{p}", [128, TPC], F32)) for p in range(2)]
        ksb = e(nc.sbuf_tensor("ksb", [128, TPC], F32))
        dumpS = e(nc.sbuf_tensor("dumpS", [128, CH], F32))
        dumpV = e(nc.sbuf_tensor("dumpV", [128, CH], F32))
        sq_acc = e(nc.sbuf_tensor("sq_acc", [128, 64], F32))
        sk_acc = e(nc.sbuf_tensor("sk_acc", [128, 64], F32))
        skv_acc = e(nc.sbuf_tensor("skv_acc", [128, 64], F32))
        part_sb = e(nc.sbuf_tensor("part_sb", [128, 96], F32))
        mmps = [e(nc.psum_tensor(f"mmps{p}", [128, CH], F32)) for p in range(2)]
        bps = e(nc.psum_tensor("bps", [128, 48], F32))

        ld = SC(ctx, nc, "ld")
        xld = SC(ctx, nc, "xld")
        wc = [SC(ctx, nc, "wc0"), SC(ctx, nc, "wc1")]
        mm = SC(ctx, nc, "mm")
        ev = SC(ctx, nc, "ev")
        sc = SC(ctx, nc, "sc")
        st = [SC(ctx, nc, "st0"), SC(ctx, nc, "st1")]
        stp = SC(ctx, nc, "stp")
        block = e(nc.Block())

        ocs = list(range(16))
        for j in range(16):
            ocs += [16 + j, 32 + j]

        @block.gpsimd
        def _(g):
            T, V, SCA, G, SY = nc.tensor, nc.vector, nc.scalar, nc.gpsimd, nc.sync

            ld.inc(SY.dma_start(out=brow[:, :], in_=bvec[:, :]), 16)
            xld.inc(G.dma_start(out=xbf[:, :, :], in_=_r16(xs)), 16)
            ev.inc(V.memset(one_sb[:, :], 1.0), 1)
            ev_ones = ev.n

            wcast_at = []
            for i in range(2):
                oc = ocs[i]
                wc[i % 2].inc(G.dma_start(out=wbuf[i % 2][:, :, :],
                                          in_=_r16(w)[:, :, oc * 128:(oc + 1) * 128]), 16)
                wcast_at.append((i % 2, wc[i % 2].n))

            T.wait_ge(ld.h, 16)
            T.wait_ge(ev.h, ev_ones)
            for j in range(48):
                ins = T.matmul(bps[:, j:j + 1], brow[:, j * 128:(j + 1) * 128],
                               one_sb[:, :], start=True, stop=True)
            mm.inc(ins, 1)
            mm_bias = mm.n
            V.wait_ge(mm.h, mm_bias)
            ev.inc(V.tensor_copy(bcol[:, :], bps[:, :]), 1)
            ev_bias = ev.n
            V.wait_ge(ev.h, ev_bias)

            bank_free_ev = [0, 0]
            qsb_free = [(0, 0)] * 2
            ksb_free_sc = 0
            n_q_sq = 0
            n_k_sq = 0

            T.wait_ge(xld.h, 16)

            for i, oc in enumerate(ocs):
                wp = i % 2
                T.wait_ge(wc[wcast_at[i][0]].h, wcast_at[i][1])
                is_q = oc < 16
                is_k = 16 <= oc < 32
                is_v = oc >= 32
                j = oc - 16 if is_k else (oc - 32 if is_v else oc)

                if is_q:
                    need_st, need_sc = qsb_free[oc % 2]
                    if need_st or need_sc:
                        V.wait_ge(st[oc % 2].h, need_st)
                        V.wait_ge(sc.h, need_sc)
                if is_k and j > 0:
                    V.wait_ge(sc.h, ksb_free_sc)
                if is_k:
                    ksb_w_ev = [0] * NCH

                for c in range(NCH):
                    bk = (i * NCH + c) % 2
                    if bank_free_ev[bk]:
                        T.wait_ge(ev.h, bank_free_ev[bk])
                    for kt in range(KT):
                        ins = T.matmul(mmps[bk][:, :], wbuf[wp][:, kt, :],
                                       xbf[:, kt, c * CH:(c + 1) * CH],
                                       start=(kt == 0), stop=(kt == KT - 1))
                    mm.inc(ins, 1)
                    V.wait_ge(mm.h, mm.n)
                    b01 = c // 2
                    if is_q:
                        ev.inc(V.tensor_scalar(
                            out=qsb[oc % 2][:, c * CH:(c + 1) * CH], in0=mmps[bk][:, :],
                            scalar1=bcol[:, oc:oc + 1], scalar2=None, op0=ALU.add), 1)
                        bank_free_ev[bk] = ev.n
                        SCA.wait_ge(ev.h, ev.n)
                        SCA.wait_ge(sc.h, sc.n)
                        sc.inc(SCA.activation(
                            out=dumpS[:, :], in_=qsb[oc % 2][:, c * CH:(c + 1) * CH],
                            func=AF.Square, accum_out=sq_acc[:, oc * 4 + c:oc * 4 + c + 1]), 1)
                        n_q_sq += 1
                    elif is_k:
                        ev.inc(V.tensor_scalar(
                            out=ksb[:, c * CH:(c + 1) * CH], in0=mmps[bk][:, :],
                            scalar1=bcol[:, oc:oc + 1], scalar2=None, op0=ALU.add), 1)
                        bank_free_ev[bk] = ev.n
                        ksb_w_ev[c] = ev.n
                        SCA.wait_ge(ev.h, ev.n)
                        SCA.wait_ge(sc.h, sc.n)
                        sc.inc(SCA.activation(
                            out=dumpS[:, :], in_=ksb[:, c * CH:(c + 1) * CH],
                            func=AF.Square, accum_out=sk_acc[:, j * 4 + c:j * 4 + c + 1]), 1)
                        n_k_sq += 1
                    else:
                        V.wait_ge(ev.h, ksb_w_ev[c])
                        V.wait_ge(ev.h, ev.n)
                        ev.inc(V.scalar_tensor_tensor(
                            out=dumpV[:, :], in0=mmps[bk][:, :],
                            scalar=bcol[:, oc:oc + 1], in1=ksb[:, c * CH:(c + 1) * CH],
                            op0=ALU.add, op1=ALU.mult,
                            accum_out=skv_acc[:, j * 4 + c:j * 4 + c + 1]), 1)
                        bank_free_ev[bk] = ev.n

                if is_q:
                    SY.wait_ge(ev.h, ev.n)
                    st[oc % 2].inc(SY.dma_start(out=qT[oc * 128:(oc + 1) * 128, :],
                                                in_=qsb[oc % 2][:, :]), 16)
                    qsb_free[oc % 2] = (st[oc % 2].n, sc.n)
                if is_v:
                    ksb_free_sc = 64 + 4 * (j + 1)

                if i + 2 < len(ocs):
                    ocn = ocs[i + 2]
                    G.wait_ge(mm.h, mm.n)
                    wc[i % 2].inc(G.dma_start(out=wbuf[(i + 2) % 2][:, :, :],
                                              in_=_r16(w)[:, :, ocn * 128:(ocn + 1) * 128]), 16)
                    wcast_at.append((i % 2, wc[i % 2].n))

            V.wait_ge(sc.h, sc.n)
            V.wait_ge(ev.h, ev.n)
            grp = lambda t: t.rearrange("p (j b two) -> p (j b) two", two=2, b=2)
            ev.inc(V.tensor_reduce(out=part_sb[:, 0:32], in_=grp(sq_acc),
                                   axis=AX.X, op=ALU.add), 1)
            ev.inc(V.tensor_reduce(out=part_sb[:, 32:64], in_=grp(sk_acc),
                                   axis=AX.X, op=ALU.add), 1)
            ev.inc(V.tensor_reduce(out=part_sb[:, 64:96], in_=grp(skv_acc),
                                   axis=AX.X, op=ALU.add), 1)
            G.wait_ge(ev.h, ev.n)
            stp.inc(G.dma_start(out=part[:, :], in_=part_sb[:, :]), 16)
            G.wait_ge(st[0].h, st[0].n)
            G.wait_ge(st[1].h, st[1].n)
            G.wait_ge(stp.h, stp.n)

    return nc


def build_l0x1():
    nc = bass.Bass("TRN2", target_bir_lowering=False)
    xs = nc.dram_tensor("xs", [D, TPC], F32, kind="ExternalInput").ap()
    qTi = nc.dram_tensor("qTi", [D, TPC], F32, kind="ExternalInput").ap()
    parts = nc.dram_tensor("parts", [NCORE, 128, 96], F32, kind="ExternalInput").ap()
    gw = nc.dram_tensor("gw", [D, D], F32, kind="ExternalInput").ap()
    gb = nc.dram_tensor("gb", [1, D], F32, kind="ExternalInput").ap()
    ing = nc.dram_tensor("ing", [1, D], F32, kind="ExternalInput").ap()
    inb = nc.dram_tensor("inb", [1, D], F32, kind="ExternalInput").ap()
    x1T = nc.dram_tensor("x1T", [D, TPC], F32, kind="ExternalOutput").ap()
    gateT = nc.dram_tensor("gateT", [D, TPC], F32, kind="ExternalOutput").ap()
    xnl = nc.dram_tensor("xnl", [D, 2], F32, kind="ExternalOutput").ap()
    r_scr = nc.dram_tensor("r_scr", [D, TPC], F32).ap()

    EPS1 = 1e-5
    EPS2 = 1e-6

    with ExitStack() as ctx:
        e = ctx.enter_context
        psb = e(nc.sbuf_tensor("psb", [128, NCORE, 96], F32))
        sums = e(nc.sbuf_tensor("sums", [128, 96], F32))
        scr32 = e(nc.sbuf_tensor("scr32", [128, 32], F32))
        s_sb = e(nc.sbuf_tensor("s_sb", [128, 32], F32))
        qtile = [e(nc.sbuf_tensor(f"qtile{p}", [128, TPC], F32)) for p in range(2)]
        xtile = [e(nc.sbuf_tensor(f"xtile{p}", [128, TPC], F32)) for p in range(2)]
        sqt = [e(nc.sbuf_tensor(f"sqt{p}", [128, TPC], BF16)) for p in range(2)]
        onesb = e(nc.sbuf_tensor("onesb", [128, 1], BF16))
        x1bf = e(nc.sbuf_tensor("x1bf", [128, KT, TPC], BF16))
        x1t = [e(nc.sbuf_tensor(f"x1t{p}", [128, TPC], F32)) for p in range(2)]
        wbuf = [e(nc.sbuf_tensor(f"wbuf{p}", [128, KT, 128], BF16)) for p in range(2)]
        gsb = [e(nc.sbuf_tensor(f"gsb{p}", [128, CH], F32)) for p in range(2)]
        ones_c = e(nc.sbuf_tensor("ones_c", [128, 1], F32))
        ones_r = e(nc.sbuf_tensor("ones_r", [1, 128], F32))
        murow = e(nc.sbuf_tensor("murow", [1, TPC], F32))
        q2row = e(nc.sbuf_tensor("q2row", [1, TPC], F32))
        Asb = e(nc.sbuf_tensor("Asb", [128, TPC], F32))
        Bsb = e(nc.sbuf_tensor("Bsb", [128, TPC], F32))
        xl = e(nc.sbuf_tensor("xl", [128, KT, 2], F32))
        xlsq = e(nc.sbuf_tensor("xlsq", [128, KT, 2], F32))
        xn = e(nc.sbuf_tensor("xn", [128, KT, 2], F32))
        prow = e(nc.sbuf_tensor("prow", [1, D], F32))
        gcol = e(nc.sbuf_tensor("gcol", [128, KT], F32))
        bcol = e(nc.sbuf_tensor("bcol", [128, KT], F32))
        gbcol = e(nc.sbuf_tensor("gbcol", [128, KT], F32))
        mrow = e(nc.sbuf_tensor("mrow", [1, 2], F32))
        vrow = e(nc.sbuf_tensor("vrow", [1, 2], F32))
        trow = e(nc.sbuf_tensor("trow", [1, 2], F32))
        mu2b = e(nc.sbuf_tensor("mu2b", [128, 2], F32))
        inv2b = e(nc.sbuf_tensor("inv2b", [128, 2], F32))
        t2 = e(nc.sbuf_tensor("t2", [128, 2], F32))

        stats_ctx = ExitStack()
        stats = [stats_ctx.enter_context(nc.psum_tensor(f"stats{c}", [1, CH], F32))
                 for c in range(NCH)]
        statq = [stats_ctx.enter_context(nc.psum_tensor(f"statq{c}", [1, CH], F32))
                 for c in range(NCH)]
        bps_holder = []

        ld = SC(ctx, nc, "ld")
        rowld = SC(ctx, nc, "rowld")
        qld = [SC(ctx, nc, f"qld{p}") for p in range(2)]
        xld = [SC(ctx, nc, f"xld{p}") for p in range(2)]
        rld = [SC(ctx, nc, f"rld{p}") for p in range(2)]
        wc = [SC(ctx, nc, f"wc{p}") for p in range(2)]
        mm = SC(ctx, nc, "mm")
        ev = SC(ctx, nc, "ev")
        sc = SC(ctx, nc, "sc")
        str_ = [SC(ctx, nc, f"str{p}") for p in range(2)]
        stx = [SC(ctx, nc, f"stx{p}") for p in range(2)]
        stg = [SC(ctx, nc, f"stg{p}") for p in range(2)]
        sto = SC(ctx, nc, "sto")
        block = e(nc.Block())

        @block.gpsimd
        def _(g):
            T, V, SCA, G, SY = nc.tensor, nc.vector, nc.scalar, nc.gpsimd, nc.sync

            ld.inc(SY.dma_start(out=psb[:, :, :], in_=parts.rearrange("r p c -> p r c")), 16)
            ev.inc(V.memset(ones_c[:, :], 1.0), 1)
            ev.inc(V.memset(ones_r[:, :], 1.0), 1)
            ev.inc(V.memset(onesb[:, :], 1.0), 1)
            ev_ones = ev.n

            V.wait_ge(ld.h, 16)
            ev.inc(V.tensor_reduce(out=sums[:, :], in_=psb.rearrange("p r c -> p c r"),
                                   axis=AX.X, op=ALU.add), 1)
            V.wait_ge(ev.h, ev.n)
            ev.inc(V.tensor_tensor(out=scr32[:, :], in0=sums[:, 0:32],
                                   in1=sums[:, 32:64], op=ALU.mult), 1)
            SCA.wait_ge(ev.h, ev.n)
            sc.inc(SCA.activation(out=scr32[:, :], in_=scr32[:, :], func=AF.Sqrt), 1)
            V.wait_ge(sc.h, sc.n)
            ev.inc(V.reciprocal(out=scr32[:, :], in_=scr32[:, :]), 1)
            V.wait_ge(ev.h, ev.n)
            ev.inc(V.tensor_tensor(out=s_sb[:, :], in0=sums[:, 64:96],
                                   in1=scr32[:, :], op=ALU.mult), 1)
            ev_s = ev.n

            q_used_ev = [0, 0]
            r_used_mm = [0, 0]
            r_used_sc = [0, 0]
            V.wait_ge(ev.h, ev_s)
            for j in range(KT):
                p = j % 2
                if j >= 2:
                    SY.wait_ge(ev.h, q_used_ev[p])
                    SY.wait_ge(str_[p].h, str_[p].n)
                    SY.wait_ge(mm.h, r_used_mm[p])
                    SY.wait_ge(sc.h, r_used_sc[p])
                qld[p].inc(SY.dma_start(out=qtile[p][:, :],
                                        in_=qTi[j * 128:(j + 1) * 128, :]), 16)
                xld[p].inc(SY.dma_start(out=xtile[p][:, :],
                                        in_=xs[j * 128:(j + 1) * 128, :]), 16)
                V.wait_ge(qld[p].h, qld[p].n)
                V.wait_ge(xld[p].h, xld[p].n)
                for b in range(2):
                    ev.inc(V.scalar_tensor_tensor(
                        out=xtile[p][:, b * HB:(b + 1) * HB],
                        in0=qtile[p][:, b * HB:(b + 1) * HB],
                        scalar=s_sb[:, j * 2 + b:j * 2 + b + 1],
                        in1=xtile[p][:, b * HB:(b + 1) * HB],
                        op0=ALU.mult, op1=ALU.add), 1)
                ev_r = ev.n
                q_used_ev[p] = ev.n
                SCA.wait_ge(ev.h, ev_r)
                SCA.wait_ge(sc.h, sc.n)
                if j >= 2:
                    SCA.wait_ge(mm.h, r_used_mm[p])
                sc.inc(SCA.activation(out=sqt[p][:, :], in_=xtile[p][:, :],
                                      func=AF.Square), 1)
                T.wait_ge(ev.h, ev_r)
                T.wait_ge(sc.h, sc.n)
                for c in range(NCH):
                    T.matmul(stats[c][:, :], ones_c[:, :], xtile[p][:, c * CH:(c + 1) * CH],
                             start=(j == 0), stop=(j == KT - 1))
                    ins = T.matmul(statq[c][:, :], onesb[:, :], sqt[p][:, c * CH:(c + 1) * CH],
                                   start=(j == 0), stop=(j == KT - 1))
                mm.inc(ins, 1)
                r_used_mm[p] = mm.n
                r_used_sc[p] = sc.n
                SY.wait_ge(ev.h, ev_r)
                str_[p].inc(SY.dma_start(out=r_scr[j * 128:(j + 1) * 128, :],
                                         in_=xtile[p][:, :]), 16)
            mm_stats = mm.n

            V.wait_ge(mm.h, mm_stats)
            for c in range(NCH):
                ev.inc(V.tensor_scalar(out=murow[:, c * CH:(c + 1) * CH], in0=stats[c][:, :],
                                       scalar1=1.0 / D, scalar2=None, op0=ALU.mult), 1)
                ev.inc(V.tensor_scalar(out=q2row[:, c * CH:(c + 1) * CH], in0=statq[c][:, :],
                                       scalar1=1.0 / D, scalar2=None, op0=ALU.mult), 1)
            V.wait_ge(ev.h, ev.n)
            ev.inc(V.tensor_tensor(out=Bsb[0:1, 0:TPC], in0=murow[:, :], in1=murow[:, :],
                                   op=ALU.mult), 1)
            V.wait_ge(ev.h, ev.n)
            ev.inc(V.scalar_tensor_tensor(out=q2row[:, :], in0=q2row[:, :], scalar=EPS1,
                                          in1=Bsb[0:1, 0:TPC], op0=ALU.add,
                                          op1=ALU.subtract), 1)
            SCA.wait_ge(ev.h, ev.n)
            SCA.wait_ge(sc.h, sc.n)
            sc.inc(SCA.activation(out=q2row[:, :], in_=q2row[:, :], func=AF.Sqrt), 1)
            V.wait_ge(sc.h, sc.n)
            ev.inc(V.reciprocal(out=q2row[:, :], in_=q2row[:, :]), 1)
            V.wait_ge(ev.h, ev.n)
            ev.inc(V.scalar_tensor_tensor(out=murow[:, :], in0=murow[:, :], scalar=-1.0,
                                          in1=q2row[:, :], op0=ALU.mult, op1=ALU.mult), 1)
            ev_rows = ev.n
            stats_ctx.close()
            bps = [e(nc.psum_tensor(f"bps{p}", [128, CH], F32)) for p in range(2)]
            bps_holder.append(bps)

            T.wait_ge(ev.h, ev_rows)
            for name, src_ap, dst in ((0, ing, gcol), (1, inb, bcol), (2, gb, gbcol)):
                if name > 0:
                    SY.wait_ge(mm.h, mm.n)
                rowld.inc(SY.dma_start(out=prow[:, :], in_=src_ap[:, :]), 16)
                T.wait_ge(rowld.h, rowld.n)
                if name > 0:
                    T.wait_ge(ev.h, ev.n)
                for j in range(KT):
                    ins = T.matmul(bps[0][:, j:j + 1], prow[:, j * 128:(j + 1) * 128],
                                   ones_c[0:1, :], start=True, stop=True)
                mm.inc(ins, 1)
                V.wait_ge(mm.h, mm.n)
                ev.inc(V.tensor_copy(dst[:, :], bps[0][:, 0:KT]), 1)
            ev_cols = ev.n

            T.wait_ge(ev.h, ev_cols)
            for c in range(NCH):
                if c > 0:
                    T.wait_ge(ev.h, ev.n)
                T.matmul(bps[0][:, :], ones_r[:, :], q2row[:, c * CH:(c + 1) * CH],
                         start=True, stop=True)
                ins = T.matmul(bps[1][:, :], ones_r[:, :], murow[:, c * CH:(c + 1) * CH],
                               start=True, stop=True)
                mm.inc(ins, 1)
                V.wait_ge(mm.h, mm.n)
                ev.inc(V.tensor_copy(Asb[:, c * CH:(c + 1) * CH], bps[0][:, :]), 1)
                ev.inc(V.tensor_copy(Bsb[:, c * CH:(c + 1) * CH], bps[1][:, :]), 1)
            ev_AB = ev.n

            r_bufev = [0, 0]
            for j in range(KT):
                p = j % 2
                if j >= 2:
                    SY.wait_ge(ev.h, r_bufev[p])
                else:
                    SY.wait_ge(str_[p].h, str_[p].n)
                    SY.wait_ge(mm.h, mm_stats)
                    SY.wait_ge(sc.h, r_used_sc[p])
                rld[p].inc(SY.dma_start(out=xtile[p][:, :],
                                        in_=r_scr[j * 128:(j + 1) * 128, :]), 16)
                V.wait_ge(rld[p].h, rld[p].n)
                V.wait_ge(ev.h, ev_AB)
                if j >= 2:
                    V.wait_ge(stx[p].h, stx[p].n)
                ev.inc(V.tensor_tensor(out=x1t[p][:, :], in0=xtile[p][:, :],
                                       in1=Asb[:, :], op=ALU.mult), 1)
                r_bufev[p] = ev.n
                V.wait_ge(ev.h, ev.n)
                ev.inc(V.tensor_tensor(out=x1t[p][:, :], in0=x1t[p][:, :],
                                       in1=Bsb[:, :], op=ALU.add), 1)
                ev_x1 = ev.n
                V.wait_ge(ev.h, ev_x1)
                ev.inc(V.tensor_copy(x1bf[:, j, :], x1t[p][:, :]), 1)
                ev.inc(V.tensor_copy(xl[:, j, 0:1], x1t[p][:, HB - 1:HB]), 1)
                ev.inc(V.tensor_copy(xl[:, j, 1:2], x1t[p][:, TPC - 1:TPC]), 1)
                SY.wait_ge(ev.h, ev.n)
                stx[p].inc(SY.dma_start(out=x1T[j * 128:(j + 1) * 128, :],
                                        in_=x1t[p][:, :]), 16)
            ev_passB = ev.n
            sc_passB = sc.n

            SCA.wait_ge(ev.h, ev_passB)
            SCA.wait_ge(sc.h, sc_passB)
            sc.inc(SCA.activation(out=xlsq[:, :, :], in_=xl[:, :, :], func=AF.Square), 1)
            T.wait_ge(ev.h, ev_passB)
            T.wait_ge(sc.h, sc.n)
            T.wait_ge(ev.h, ev_AB)
            for j in range(KT):
                T.matmul(bps[0][0:1, 0:2], ones_c[:, :], xl[:, j, :],
                         start=(j == 0), stop=(j == KT - 1))
                ins = T.matmul(bps[1][0:1, 0:2], ones_c[:, :], xlsq[:, j, :],
                               start=(j == 0), stop=(j == KT - 1))
            mm.inc(ins, 1)
            V.wait_ge(mm.h, mm.n)
            ev.inc(V.tensor_scalar(out=mrow[:, :], in0=bps[0][0:1, 0:2],
                                   scalar1=1.0 / D, scalar2=None, op0=ALU.mult), 1)
            ev.inc(V.tensor_scalar(out=vrow[:, :], in0=bps[1][0:1, 0:2],
                                   scalar1=1.0 / D, scalar2=None, op0=ALU.mult), 1)
            V.wait_ge(ev.h, ev.n)
            ev.inc(V.tensor_tensor(out=trow[:, :], in0=mrow[:, :], in1=mrow[:, :],
                                   op=ALU.mult), 1)
            V.wait_ge(ev.h, ev.n)
            ev.inc(V.scalar_tensor_tensor(out=vrow[:, :], in0=vrow[:, :], scalar=EPS2,
                                          in1=trow[:, :], op0=ALU.add, op1=ALU.subtract), 1)
            SCA.wait_ge(ev.h, ev.n)
            SCA.wait_ge(sc.h, sc.n)
            sc.inc(SCA.activation(out=vrow[:, :], in_=vrow[:, :], func=AF.Sqrt), 1)
            V.wait_ge(sc.h, sc.n)
            ev.inc(V.reciprocal(out=vrow[:, :], in_=vrow[:, :]), 1)
            ev_nrm = ev.n
            T.wait_ge(ev.h, ev_nrm)
            T.matmul(bps[0][:, 0:2], ones_r[:, :], mrow[:, :], start=True, stop=True)
            ins = T.matmul(bps[1][:, 0:2], ones_r[:, :], vrow[:, :], start=True, stop=True)
            mm.inc(ins, 1)
            V.wait_ge(mm.h, mm.n)
            ev.inc(V.tensor_copy(mu2b[:, :], bps[0][:, 0:2]), 1)
            ev.inc(V.tensor_copy(inv2b[:, :], bps[1][:, 0:2]), 1)
            V.wait_ge(ev.h, ev.n)
            for j in range(KT):
                ev.inc(V.tensor_tensor(out=t2[:, :], in0=xl[:, j, :], in1=mu2b[:, :],
                                       op=ALU.subtract), 1)
                V.wait_ge(ev.h, ev.n)
                ev.inc(V.tensor_tensor(out=t2[:, :], in0=t2[:, :], in1=inv2b[:, :],
                                       op=ALU.mult), 1)
                V.wait_ge(ev.h, ev.n)
                ev.inc(V.tensor_scalar(out=xn[:, j, :], in0=t2[:, :],
                                       scalar1=gcol[:, j:j + 1], scalar2=bcol[:, j:j + 1],
                                       op0=ALU.mult, op1=ALU.add), 1)
                V.wait_ge(ev.h, ev.n)
            G.wait_ge(ev.h, ev.n)
            sto.inc(G.dma_start(out=_r16(xnl), in_=xn[:, :, :]), 16)

            wcast_at = []
            for i in range(2):
                wc[i % 2].inc(G.dma_start(out=wbuf[i % 2][:, :, :],
                                          in_=_r16(gw)[:, :, i * 128:(i + 1) * 128]), 16)
                wcast_at.append((i % 2, wc[i % 2].n))
            bank_free = [0, 0]
            gsb_free = [0, 0]
            for oc in range(KT):
                wp = oc % 2
                T.wait_ge(wc[wcast_at[oc][0]].h, wcast_at[oc][1])
                T.wait_ge(ev.h, ev_passB)
                for c in range(NCH):
                    bk = (oc * NCH + c) % 2
                    if bank_free[bk]:
                        T.wait_ge(sc.h, bank_free[bk])
                    for kt in range(KT):
                        ins = T.matmul(bps[bk][:, :], wbuf[wp][:, kt, :],
                                       x1bf[:, kt, c * CH:(c + 1) * CH],
                                       start=(kt == 0), stop=(kt == KT - 1))
                    mm.inc(ins, 1)
                    SCA.wait_ge(mm.h, mm.n)
                    SCA.wait_ge(stg[bk].h, gsb_free[bk])
                    sc.inc(SCA.activation(out=gsb[bk][:, :],
                                          in_=bps[bk][:, :], func=AF.Sigmoid,
                                          bias=gbcol[:, oc:oc + 1], scale=1.0), 1)
                    bank_free[bk] = sc.n
                    SY.wait_ge(sc.h, sc.n)
                    stg[bk].inc(SY.dma_start(
                        out=gateT[oc * 128:(oc + 1) * 128, c * CH:(c + 1) * CH],
                        in_=gsb[bk][:, :]), 16)
                    gsb_free[bk] = stg[bk].n
                if oc + 2 < KT:
                    G.wait_ge(mm.h, mm.n)
                    wc[oc % 2].inc(G.dma_start(
                        out=wbuf[oc % 2][:, :, :],
                        in_=_r16(gw)[:, :, (oc + 2) * 128:(oc + 3) * 128]), 16)
                    wcast_at.append((oc % 2, wc[oc % 2].n))

            for p in range(2):
                G.wait_ge(stx[p].h, stx[p].n)
                G.wait_ge(stg[p].h, stg[p].n)
            G.wait_ge(sto.h, sto.n)

    return nc


def build_tstep(first):
    nc = bass.Bass("TRN2", target_bir_lowering=False)
    if first:
        xnli = nc.dram_tensor("xnli", [D, 2], F32, kind="ExternalInput").ap()
        aggw = nc.dram_tensor("aggw", [D, D], F32, kind="ExternalInput").ap()
        aggb = nc.dram_tensor("aggb", [1, D], F32, kind="ExternalInput").ap()
    else:
        thpi = nc.dram_tensor("thpi", [D, 2], F32, kind="ExternalInput").ap()
        updsi = nc.dram_tensor("updsi", [128, NCORE * KT * 2], F32, kind="ExternalInput").ap()
        d2br = nc.dram_tensor("d2br", [1, D], F32, kind="ExternalInput").ap()
    png = nc.dram_tensor("png", [1, D], F32, kind="ExternalInput").ap()
    pnb = nc.dram_tensor("pnb", [1, D], F32, kind="ExternalInput").ap()
    d1ws = nc.dram_tensor("d1ws", [D, 1024], F32, kind="ExternalInput").ap()
    d1bs = nc.dram_tensor("d1bs", [1, 1024], F32, kind="ExternalInput").ap()
    d2ws = nc.dram_tensor("d2ws", [1024, D], F32, kind="ExternalInput").ap()
    thv = nc.dram_tensor("thv", [D, 2], F32, kind="ExternalOutput").ap()
    updp = nc.dram_tensor("updp", [D, 2], F32, kind="ExternalOutput").ap()

    EPS2 = 1e-6
    H8 = 1024 // 128

    with ExitStack() as ctx:
        e = ctx.enter_context
        th = e(nc.sbuf_tensor("th", [128, KT, 2], F32))
        thsq = e(nc.sbuf_tensor("thsq", [128, KT, 2], F32))
        tlnb = e(nc.sbuf_tensor("tlnb", [128, KT, 2], BF16))
        tln = e(nc.sbuf_tensor("tln", [128, 2], F32))
        hb = e(nc.sbuf_tensor("hb", [128, H8, 2], BF16))
        w1 = e(nc.sbuf_tensor("w1", [128, KT, 1024], BF16))
        w2 = e(nc.sbuf_tensor("w2", [128, H8, D], BF16))
        wag = e(nc.sbuf_tensor("wag", [128, KT, D], BF16)) if first else None
        upds_sb = e(nc.sbuf_tensor("upds_sb", [128, NCORE, KT * 2], F32)) if not first else None
        thp_sb = e(nc.sbuf_tensor("thp_sb", [128, KT, 2], F32)) if not first else None
        usum = e(nc.sbuf_tensor("usum", [128, KT * 2], F32)) if not first else None
        prow = e(nc.sbuf_tensor("prow", [1, D], F32))
        pgcol = e(nc.sbuf_tensor("pgcol", [128, KT], F32))
        pbcol = e(nc.sbuf_tensor("pbcol", [128, KT], F32))
        acol = e(nc.sbuf_tensor("acol", [128, KT], F32)) if first else None
        d2col = e(nc.sbuf_tensor("d2col", [128, KT], F32)) if not first else None
        d1col = e(nc.sbuf_tensor("d1col", [128, H8], F32))
        xnb = e(nc.sbuf_tensor("xnb", [128, KT, 2], BF16)) if first else None
        ones_c = e(nc.sbuf_tensor("ones_c", [128, 1], F32))
        ones_r = e(nc.sbuf_tensor("ones_r", [1, 128], F32))
        mrow = e(nc.sbuf_tensor("mrow", [1, 2], F32))
        vrow = e(nc.sbuf_tensor("vrow", [1, 2], F32))
        trow = e(nc.sbuf_tensor("trow", [1, 2], F32))
        mu2b = e(nc.sbuf_tensor("mu2b", [128, 2], F32))
        inv2b = e(nc.sbuf_tensor("inv2b", [128, 2], F32))
        outv = e(nc.sbuf_tensor("outv", [128, KT, 2], F32))

        mps = [e(nc.psum_tensor(f"mps{p}", [128, 2], F32)) for p in range(2)]
        sps = [e(nc.psum_tensor(f"sps{p}", [1, 2], F32)) for p in range(2)]
        bcp = [e(nc.psum_tensor(f"bcp{p}", [128, 2], F32)) for p in range(2)]
        tps = e(nc.psum_tensor("tps", [128, KT], F32))

        ld = SC(ctx, nc, "ld")
        rowld = SC(ctx, nc, "rowld")
        wl = SC(ctx, nc, "wl")
        mm = SC(ctx, nc, "mm")
        ev = SC(ctx, nc, "ev")
        sc = SC(ctx, nc, "sc")
        sto = SC(ctx, nc, "sto")
        block = e(nc.Block())

        @block.gpsimd
        def _(g):
            T, V, SCA, G, SY = nc.tensor, nc.vector, nc.scalar, nc.gpsimd, nc.sync

            ev.inc(V.memset(ones_c[:, :], 1.0), 1)
            ev.inc(V.memset(ones_r[:, :], 1.0), 1)
            ev_ones = ev.n
            wl.inc(G.dma_start(out=w1[:, :, :], in_=_r16(d1ws)), 16)
            wl.inc(G.dma_start(out=w2[:, :, :], in_=d2ws.rearrange("(kt p) t -> p kt t", p=128)), 16)
            if first:
                wl.inc(G.dma_start(out=wag[:, :, :], in_=_r16(aggw)), 16)
                ld.inc(G.dma_start(out=xnb[:, :, :], in_=_r16(xnli)), 16)
            else:
                ld.inc(SY.dma_start(out=upds_sb[:, :, :],
                                    in_=updsi.rearrange("p (r m) -> p r m", r=NCORE)), 16)
                ld.inc(SY.dma_start(out=thp_sb[:, :, :], in_=_r16(thpi)), 16)
            wl_n = wl.n
            ld_n = ld.n

            T.wait_ge(ev.h, ev_ones)
            cols = [(png, pgcol, KT), (pnb, pbcol, KT), (d1bs, d1col, H8)]
            cols.append((aggb, acol, KT) if first else (d2br, d2col, KT))
            for src_ap, dst, nt in cols:
                SY.wait_ge(mm.h, mm.n)
                rowld.inc(SY.dma_start(out=prow[:, 0:nt * 128], in_=src_ap[:, :]), 16)
                T.wait_ge(rowld.h, rowld.n)
                T.wait_ge(ev.h, ev.n)
                for j in range(nt):
                    ins = T.matmul(tps[:, j:j + 1], prow[:, j * 128:(j + 1) * 128],
                                   ones_c[0:1, :], start=True, stop=True)
                mm.inc(ins, 1)
                V.wait_ge(mm.h, mm.n)
                ev.inc(V.tensor_copy(dst[:, 0:nt], tps[:, 0:nt]), 1)
            ev_cols = ev.n

            if first:
                T.wait_ge(wl.h, wl_n)
                T.wait_ge(ld.h, ld_n)
                bank_ev = [0, 0]
                for oc in range(KT):
                    bk = oc % 2
                    if bank_ev[bk]:
                        T.wait_ge(ev.h, bank_ev[bk])
                    for kt in range(KT):
                        ins = T.matmul(mps[bk][:, :], wag[:, kt, oc * 128:(oc + 1) * 128],
                                       xnb[:, kt, :], start=(kt == 0), stop=(kt == KT - 1))
                    mm.inc(ins, 1)
                    V.wait_ge(mm.h, mm.n)
                    V.wait_ge(ev.h, ev_cols)
                    ev.inc(V.tensor_scalar(out=th[:, oc, :], in0=mps[bk][:, :],
                                           scalar1=acol[:, oc:oc + 1], scalar2=None,
                                           op0=ALU.add), 1)
                    bank_ev[bk] = ev.n
            else:
                V.wait_ge(ld.h, ld_n)
                V.wait_ge(ev.h, ev_cols)
                ev.inc(V.tensor_reduce(out=usum[:, :],
                                       in_=upds_sb.rearrange("p r m -> p m r"),
                                       axis=AX.X, op=ALU.add), 1)
                V.wait_ge(ev.h, ev.n)
                ev.inc(V.tensor_tensor(out=usum[:, :], in0=usum[:, :],
                                       in1=thp_sb.rearrange("p j c -> p (j c)"),
                                       op=ALU.add), 1)
                V.wait_ge(ev.h, ev.n)
                ev.inc(V.tensor_tensor(
                    out=th[:, :, :],
                    in0=usum[:, :].rearrange("p (j c) -> p j c", c=2),
                    in1=d2col[:, :, None].to_broadcast((128, KT, 2)),
                    op=ALU.add), 1)
            ev_th = ev.n
            SY.wait_ge(ev.h, ev_th)
            sto.inc(SY.dma_start(out=_r16(thv), in_=th[:, :, :]), 16)

            SCA.wait_ge(ev.h, ev_th)
            SCA.wait_ge(sc.h, sc.n)
            sc.inc(SCA.activation(out=thsq[:, :, :], in_=th[:, :, :], func=AF.Square), 1)
            T.wait_ge(ev.h, ev_th)
            T.wait_ge(sc.h, sc.n)
            for j in range(KT):
                T.matmul(sps[0][:, :], ones_c[:, :], th[:, j, :],
                         start=(j == 0), stop=(j == KT - 1))
                ins = T.matmul(sps[1][:, :], ones_c[:, :], thsq[:, j, :],
                               start=(j == 0), stop=(j == KT - 1))
            mm.inc(ins, 1)
            V.wait_ge(mm.h, mm.n)
            ev.inc(V.tensor_scalar(out=mrow[:, :], in0=sps[0][:, :], scalar1=1.0 / D,
                                   scalar2=None, op0=ALU.mult), 1)
            ev.inc(V.tensor_scalar(out=vrow[:, :], in0=sps[1][:, :], scalar1=1.0 / D,
                                   scalar2=None, op0=ALU.mult), 1)
            V.wait_ge(ev.h, ev.n)
            ev.inc(V.tensor_tensor(out=trow[:, :], in0=mrow[:, :], in1=mrow[:, :],
                                   op=ALU.mult), 1)
            V.wait_ge(ev.h, ev.n)
            ev.inc(V.scalar_tensor_tensor(out=vrow[:, :], in0=vrow[:, :], scalar=EPS2,
                                          in1=trow[:, :], op0=ALU.add, op1=ALU.subtract), 1)
            SCA.wait_ge(ev.h, ev.n)
            SCA.wait_ge(sc.h, sc.n)
            sc.inc(SCA.activation(out=vrow[:, :], in_=vrow[:, :], func=AF.Sqrt), 1)
            V.wait_ge(sc.h, sc.n)
            ev.inc(V.reciprocal(out=vrow[:, :], in_=vrow[:, :]), 1)
            ev_nrm = ev.n
            T.wait_ge(ev.h, ev_nrm)
            T.matmul(bcp[0][:, :], ones_r[:, :], mrow[:, :], start=True, stop=True)
            ins = T.matmul(bcp[1][:, :], ones_r[:, :], vrow[:, :], start=True, stop=True)
            mm.inc(ins, 1)
            V.wait_ge(mm.h, mm.n)
            ev.inc(V.tensor_copy(mu2b[:, :], bcp[0][:, :]), 1)
            ev.inc(V.tensor_copy(inv2b[:, :], bcp[1][:, :]), 1)
            V.wait_ge(ev.h, ev.n)
            for j in range(KT):
                ev.inc(V.tensor_tensor(out=tln[:, :], in0=th[:, j, :], in1=mu2b[:, :],
                                       op=ALU.subtract), 1)
                V.wait_ge(ev.h, ev.n)
                ev.inc(V.tensor_tensor(out=tln[:, :], in0=tln[:, :], in1=inv2b[:, :],
                                       op=ALU.mult), 1)
                V.wait_ge(ev.h, ev.n)
                ev.inc(V.tensor_scalar(out=tlnb[:, j, :], in0=tln[:, :],
                                       scalar1=pgcol[:, j:j + 1], scalar2=pbcol[:, j:j + 1],
                                       op0=ALU.mult, op1=ALU.add), 1)
                V.wait_ge(ev.h, ev.n)
            ev_tln = ev.n

            if first:
                pass
            T.wait_ge(wl.h, wl_n)
            bank_sc = [0, 0]
            for oc in range(H8):
                T.wait_ge(ev.h, ev_tln)
                bk = oc % 2
                if bank_sc[bk]:
                    T.wait_ge(sc.h, bank_sc[bk])
                for kt in range(KT):
                    ins = T.matmul(mps[bk][:, :], w1[:, kt, oc * 128:(oc + 1) * 128],
                                   tlnb[:, kt, :], start=(kt == 0), stop=(kt == KT - 1))
                mm.inc(ins, 1)
                SCA.wait_ge(mm.h, mm.n)
                SCA.wait_ge(sc.h, sc.n)
                sc.inc(SCA.activation(out=hb[:, oc, :], in_=mps[bk][:, :],
                                      func=AF.Gelu, bias=d1col[:, oc:oc + 1], scale=1.0), 1)
                bank_sc[bk] = sc.n
            sc_h = sc.n

            bank_ev2 = [0, 0]
            for oc in range(KT):
                T.wait_ge(sc.h, sc_h)
                bk = oc % 2
                if bank_ev2[bk]:
                    T.wait_ge(ev.h, bank_ev2[bk])
                for kt in range(H8):
                    ins = T.matmul(mps[bk][:, :], w2[:, kt, oc * 128:(oc + 1) * 128],
                                   hb[:, kt, :], start=(kt == 0), stop=(kt == H8 - 1))
                mm.inc(ins, 1)
                V.wait_ge(mm.h, mm.n)
                ev.inc(V.tensor_copy(outv[:, oc, :], mps[bk][:, :]), 1)
                bank_ev2[bk] = ev.n
            SY.wait_ge(ev.h, ev.n)
            sto.inc(SY.dma_start(out=_r16(updp), in_=outv[:, :, :]), 16)
            G.wait_ge(sto.h, sto.n)

    return nc


def build_l0tail():
    nc = bass.Bass("TRN2", target_bir_lowering=False)
    x1i = nc.dram_tensor("x1i", [D, TPC], F32, kind="ExternalInput").ap()
    gti = nc.dram_tensor("gti", [D, TPC], F32, kind="ExternalInput").ap()
    thpi = nc.dram_tensor("thpi", [D, 2], F32, kind="ExternalInput").ap()
    updsi = nc.dram_tensor("updsi", [128, NCORE * KT * 2], F32, kind="ExternalInput").ap()
    d2br = nc.dram_tensor("d2br", [1, D], F32, kind="ExternalInput").ap()
    bcw = nc.dram_tensor("bcw", [D, D], F32, kind="ExternalInput").ap()
    bcbr = nc.dram_tensor("bcbr", [1, D], F32, kind="ExternalInput").ap()
    outg = nc.dram_tensor("outg", [1, D], F32, kind="ExternalInput").ap()
    outb = nc.dram_tensor("outb", [1, D], F32, kind="ExternalInput").ap()
    x2T = nc.dram_tensor("x2T", [D, TPC], F32, kind="ExternalOutput").ap()
    x2bf = nc.dram_tensor("x2bf", [D, TPC], BF16, kind="ExternalOutput").ap()
    u_scr = nc.dram_tensor("u_scr", [D, TPC], F32).ap()
    z_scr = nc.dram_tensor("z_scr", [D, TPC], F32).ap()

    EPS1 = 1e-5
    EPS2 = 1e-6

    with ExitStack() as ctx:
        e = ctx.enter_context
        atile = [e(nc.sbuf_tensor(f"atile{p}", [128, TPC], F32)) for p in range(2)]
        btile = [e(nc.sbuf_tensor(f"btile{p}", [128, TPC], F32)) for p in range(2)]
        ctile = [e(nc.sbuf_tensor(f"ctile{p}", [128, TPC], F32)) for p in range(2)]
        sqt = [e(nc.sbuf_tensor(f"sqt{p}", [128, TPC], BF16)) for p in range(2)]
        obf = [e(nc.sbuf_tensor(f"obf{p}", [128, TPC], BF16)) for p in range(2)]
        bcc = e(nc.sbuf_tensor("bcc", [128, KT, 2], F32))
        upds_sb = e(nc.sbuf_tensor("upds_sb", [128, NCORE, KT * 2], F32))
        thp_sb = e(nc.sbuf_tensor("thp_sb", [128, KT, 2], F32))
        usum = e(nc.sbuf_tensor("usum", [128, KT * 2], F32))
        thbf = e(nc.sbuf_tensor("thbf", [128, KT, 2], BF16))
        d2col = e(nc.sbuf_tensor("d2col", [128, KT], F32))
        bccol = e(nc.sbuf_tensor("bccol", [128, KT], F32))
        wbc = e(nc.sbuf_tensor("wbc", [128, KT, D], BF16))
        ones_c = e(nc.sbuf_tensor("ones_c", [128, 1], F32))
        onesb = e(nc.sbuf_tensor("onesb", [128, 1], BF16))
        ones_r = e(nc.sbuf_tensor("ones_r", [1, 128], F32))
        murow = e(nc.sbuf_tensor("murow", [1, TPC], F32))
        q2row = e(nc.sbuf_tensor("q2row", [1, TPC], F32))
        Asb = e(nc.sbuf_tensor("Asb", [128, TPC], F32))
        Bsb = e(nc.sbuf_tensor("Bsb", [128, TPC], F32))
        prow = e(nc.sbuf_tensor("prow", [1, D], F32))
        gcol = e(nc.sbuf_tensor("gcol", [128, KT], F32))
        bcol = e(nc.sbuf_tensor("bcol", [128, KT], F32))

        ld = SC(ctx, nc, "ld")
        rowld = SC(ctx, nc, "rowld")
        wcb = SC(ctx, nc, "wcb")
        ald = [SC(ctx, nc, f"ald{p}") for p in range(2)]
        bld = [SC(ctx, nc, f"bld{p}") for p in range(2)]
        mm = SC(ctx, nc, "mm")
        ev = SC(ctx, nc, "ev")
        sc = SC(ctx, nc, "sc")
        stu = [SC(ctx, nc, f"stu{p}") for p in range(2)]
        stz = [SC(ctx, nc, f"stz{p}") for p in range(2)]
        stx = [SC(ctx, nc, f"stx{p}") for p in range(2)]
        stb = [SC(ctx, nc, f"stb{p}") for p in range(2)]
        block = e(nc.Block())

        @block.gpsimd
        def _(g):
            T, V, SCA, G, SY = nc.tensor, nc.vector, nc.scalar, nc.gpsimd, nc.sync

            ld.inc(SY.dma_start(out=upds_sb[:, :, :],
                                in_=updsi.rearrange("p (r m) -> p r m", r=NCORE)), 16)
            ld.inc(SY.dma_start(out=thp_sb[:, :, :], in_=_r16(thpi)), 16)
            ld_th = ld.n
            ev.inc(V.memset(ones_c[:, :], 1.0), 1)
            ev.inc(V.memset(onesb[:, :], 1.0), 1)
            ev.inc(V.memset(ones_r[:, :], 1.0), 1)
            ev_ones = ev.n

            with nc.psum_tensor("tps", [128, KT], F32) as tps:
                T.wait_ge(ev.h, ev_ones)
                for src_ap, dst in ((outg, gcol), (outb, bcol), (d2br, d2col),
                                    (bcbr, bccol)):
                    SY.wait_ge(mm.h, mm.n)
                    rowld.inc(SY.dma_start(out=prow[:, :], in_=src_ap[:, :]), 16)
                    T.wait_ge(rowld.h, rowld.n)
                    T.wait_ge(ev.h, ev.n)
                    for j in range(KT):
                        ins = T.matmul(tps[:, j:j + 1], prow[:, j * 128:(j + 1) * 128],
                                       ones_c[0:1, :], start=True, stop=True)
                    mm.inc(ins, 1)
                    V.wait_ge(mm.h, mm.n)
                    ev.inc(V.tensor_copy(dst[:, :], tps[:, 0:KT]), 1)
            ev_cols = ev.n

            V.wait_ge(ld.h, ld_th)
            V.wait_ge(ev.h, ev_cols)
            ev.inc(V.tensor_reduce(out=usum[:, :],
                                   in_=upds_sb.rearrange("p r m -> p m r"),
                                   axis=AX.X, op=ALU.add), 1)
            V.wait_ge(ev.h, ev.n)
            ev.inc(V.tensor_tensor(out=usum[:, :], in0=usum[:, :],
                                   in1=thp_sb.rearrange("p j c -> p (j c)"),
                                   op=ALU.add), 1)
            V.wait_ge(ev.h, ev.n)
            ev.inc(V.tensor_tensor(out=thbf[:, :, :],
                                   in0=usum[:, :].rearrange("p (j c) -> p j c", c=2),
                                   in1=d2col[:, :, None].to_broadcast((128, KT, 2)),
                                   op=ALU.add), 1)
            ev_thb = ev.n
            wcb.inc(G.dma_start(out=wbc[:, :, :], in_=_r16(bcw)), 16)
            with nc.psum_tensor("bps_bc", [128, 2], F32) as bps_bc:
                T.wait_ge(wcb.h, wcb.n)
                T.wait_ge(ev.h, ev_thb)
                for oc in range(KT):
                    if oc > 0:
                        T.wait_ge(ev.h, ev.n)
                    for kt in range(KT):
                        ins = T.matmul(bps_bc[:, :], wbc[:, kt, oc * 128:(oc + 1) * 128],
                                       thbf[:, kt, :], start=(kt == 0), stop=(kt == KT - 1))
                    mm.inc(ins, 1)
                    V.wait_ge(mm.h, mm.n)
                    ev.inc(V.tensor_scalar(out=bcc[:, oc, :], in0=bps_bc[:, :],
                                           scalar1=bccol[:, oc:oc + 1], scalar2=None,
                                           op0=ALU.add), 1)
                T.wait_ge(ev.h, ev.n)
            ev_bcc = ev.n

            def pbar():
                for eng in (SY, V):
                    eng.wait_ge(ev.h, ev.n)
                    eng.wait_ge(mm.h, mm.n)
                    eng.wait_ge(sc.h, sc.n)
                    for s in (stu, stz, stx, stb, ald, bld):
                        for p in range(2):
                            eng.wait_ge(s[p].h, s[p].n)

            def stream_ln_pass(in_ld_fns, combine_fn, out_scr, store_sems, with_stats,
                               stats, statq, first_wait_ev):
                used = {"mm": [0, 0], "sc": [0, 0], "ev": [0, 0]}
                for j in range(KT):
                    p = j % 2
                    if j >= 2:
                        SY.wait_ge(ev.h, used["ev"][p])
                        SY.wait_ge(store_sems[p].h, store_sems[p].n)
                        if with_stats:
                            SY.wait_ge(mm.h, used["mm"][p])
                            SY.wait_ge(sc.h, used["sc"][p])
                    for fn in in_ld_fns:
                        fn(j, p)
                    combine_fn(j, p)
                    used["ev"][p] = ev.n
                    if with_stats:
                        SCA.wait_ge(ev.h, ev.n)
                        SCA.wait_ge(sc.h, sc.n)
                        if j >= 2:
                            SCA.wait_ge(mm.h, used["mm"][p])
                        sc.inc(SCA.activation(out=sqt[p][:, :], in_=ctile[p][:, :],
                                              func=AF.Square), 1)
                        T.wait_ge(ev.h, ev.n)
                        T.wait_ge(sc.h, sc.n)
                        if j == 0:
                            T.wait_ge(ev.h, first_wait_ev)
                        for c in range(NCH):
                            T.matmul(stats[c][:, :], ones_c[:, :],
                                     ctile[p][:, c * CH:(c + 1) * CH],
                                     start=(j == 0), stop=(j == KT - 1))
                            ins = T.matmul(statq[c][:, :], onesb[:, :],
                                           sqt[p][:, c * CH:(c + 1) * CH],
                                           start=(j == 0), stop=(j == KT - 1))
                        mm.inc(ins, 1)
                        used["mm"][p] = mm.n
                        used["sc"][p] = sc.n
                    if out_scr is not None:
                        SY.wait_ge(ev.h, ev.n)
                        store_sems[p].inc(SY.dma_start(
                            out=out_scr[j * 128:(j + 1) * 128, :],
                            in_=ctile[p][:, :]), 16)
                return used

            def stats_to_AB(stats, statq, eps):
                V.wait_ge(mm.h, mm.n)
                for c in range(NCH):
                    ev.inc(V.tensor_scalar(out=murow[:, c * CH:(c + 1) * CH],
                                           in0=stats[c][:, :], scalar1=1.0 / D,
                                           scalar2=None, op0=ALU.mult), 1)
                    ev.inc(V.tensor_scalar(out=q2row[:, c * CH:(c + 1) * CH],
                                           in0=statq[c][:, :], scalar1=1.0 / D,
                                           scalar2=None, op0=ALU.mult), 1)
                V.wait_ge(ev.h, ev.n)
                ev.inc(V.tensor_tensor(out=Bsb[0:1, 0:TPC], in0=murow[:, :],
                                       in1=murow[:, :], op=ALU.mult), 1)
                V.wait_ge(ev.h, ev.n)
                ev.inc(V.scalar_tensor_tensor(out=q2row[:, :], in0=q2row[:, :],
                                              scalar=eps, in1=Bsb[0:1, 0:TPC],
                                              op0=ALU.add, op1=ALU.subtract), 1)
                SCA.wait_ge(ev.h, ev.n)
                SCA.wait_ge(sc.h, sc.n)
                sc.inc(SCA.activation(out=q2row[:, :], in_=q2row[:, :], func=AF.Sqrt), 1)
                V.wait_ge(sc.h, sc.n)
                ev.inc(V.reciprocal(out=q2row[:, :], in_=q2row[:, :]), 1)
                V.wait_ge(ev.h, ev.n)
                ev.inc(V.scalar_tensor_tensor(out=murow[:, :], in0=murow[:, :],
                                              scalar=-1.0, in1=q2row[:, :],
                                              op0=ALU.mult, op1=ALU.mult), 1)
                return ev.n

            def bcast_AB(bps):
                for c in range(NCH):
                    T.wait_ge(ev.h, ev.n)
                    T.matmul(bps[0][:, :], ones_r[:, :], q2row[:, c * CH:(c + 1) * CH],
                             start=True, stop=True)
                    ins = T.matmul(bps[1][:, :], ones_r[:, :],
                                   murow[:, c * CH:(c + 1) * CH], start=True, stop=True)
                    mm.inc(ins, 1)
                    V.wait_ge(mm.h, mm.n)
                    ev.inc(V.tensor_copy(Asb[:, c * CH:(c + 1) * CH], bps[0][:, :]), 1)
                    ev.inc(V.tensor_copy(Bsb[:, c * CH:(c + 1) * CH], bps[1][:, :]), 1)
                return ev.n

            def ld_x1(j, p):
                ald[p].inc(SY.dma_start(out=atile[p][:, :],
                                        in_=x1i[j * 128:(j + 1) * 128, :]), 16)
            def ld_gt(j, p):
                bld[p].inc(SY.dma_start(out=btile[p][:, :],
                                        in_=gti[j * 128:(j + 1) * 128, :]), 16)

            def comb_u(j, p):
                V.wait_ge(ald[p].h, ald[p].n)
                V.wait_ge(bld[p].h, bld[p].n)
                V.wait_ge(ev.h, ev_bcc)
                for b in range(2):
                    ev.inc(V.scalar_tensor_tensor(
                        out=ctile[p][:, b * HB:(b + 1) * HB],
                        in0=btile[p][:, b * HB:(b + 1) * HB],
                        scalar=bcc[:, j, b:b + 1],
                        in1=atile[p][:, b * HB:(b + 1) * HB],
                        op0=ALU.mult, op1=ALU.add), 1)

            stats_ctx = ExitStack()
            stats = [stats_ctx.enter_context(nc.psum_tensor(f"s1_{c}", [1, CH], F32))
                     for c in range(NCH)]
            statq = [stats_ctx.enter_context(nc.psum_tensor(f"q1_{c}", [1, CH], F32))
                     for c in range(NCH)]
            stream_ln_pass([ld_x1, ld_gt], comb_u, u_scr, stu, True, stats, statq,
                           ev_cols)
            aba = stats_to_AB(stats, statq, EPS2)
            stats_ctx.close()
            with nc.psum_tensor("bp0", [128, CH], F32) as b0, \
                 nc.psum_tensor("bp1", [128, CH], F32) as b1:
                ev_ab1 = bcast_AB([b0, b1])

            def ld_u(j, p):
                ald[p].inc(SY.dma_start(out=atile[p][:, :],
                                        in_=u_scr[j * 128:(j + 1) * 128, :]), 16)

            def comb_z(j, p):
                V.wait_ge(ald[p].h, ald[p].n)
                V.wait_ge(bld[p].h, bld[p].n)
                V.wait_ge(ev.h, ev_ab1)
                ev.inc(V.tensor_tensor(out=ctile[p][:, :], in0=atile[p][:, :],
                                       in1=Asb[:, :], op=ALU.mult), 1)
                V.wait_ge(ev.h, ev.n)
                ev.inc(V.tensor_tensor(out=ctile[p][:, :], in0=ctile[p][:, :],
                                       in1=Bsb[:, :], op=ALU.add), 1)
                V.wait_ge(ev.h, ev.n)
                ev.inc(V.tensor_scalar(out=ctile[p][:, :], in0=ctile[p][:, :],
                                       scalar1=gcol[:, j:j + 1], scalar2=bcol[:, j:j + 1],
                                       op0=ALU.mult, op1=ALU.add), 1)
                V.wait_ge(ev.h, ev.n)
                ev.inc(V.tensor_tensor(out=ctile[p][:, :], in0=ctile[p][:, :],
                                       in1=btile[p][:, :], op=ALU.add), 1)

            def ld_x1b(j, p):
                bld[p].inc(SY.dma_start(out=btile[p][:, :],
                                        in_=x1i[j * 128:(j + 1) * 128, :]), 16)

            pbar()
            stats_ctx2 = ExitStack()
            stats2 = [stats_ctx2.enter_context(nc.psum_tensor(f"s2_{c}", [1, CH], F32))
                      for c in range(NCH)]
            statq2 = [stats_ctx2.enter_context(nc.psum_tensor(f"q2_{c}", [1, CH], F32))
                      for c in range(NCH)]
            stream_ln_pass([ld_u, ld_x1b], comb_z, z_scr, stz, True, stats2, statq2,
                           ev_ab1)
            stats_to_AB(stats2, statq2, EPS1)
            stats_ctx2.close()
            with nc.psum_tensor("bp2", [128, CH], F32) as b0, \
                 nc.psum_tensor("bp3", [128, CH], F32) as b1:
                ev_ab2 = bcast_AB([b0, b1])

            def ld_z(j, p):
                ald[p].inc(SY.dma_start(out=atile[p][:, :],
                                        in_=z_scr[j * 128:(j + 1) * 128, :]), 16)

            def comb_x2(j, p):
                V.wait_ge(ald[p].h, ald[p].n)
                V.wait_ge(ev.h, ev_ab2)
                if j >= 2:
                    V.wait_ge(stb[p].h, stb[p].n)
                ev.inc(V.tensor_tensor(out=ctile[p][:, :], in0=atile[p][:, :],
                                       in1=Asb[:, :], op=ALU.mult), 1)
                V.wait_ge(ev.h, ev.n)
                ev.inc(V.tensor_tensor(out=ctile[p][:, :], in0=ctile[p][:, :],
                                       in1=Bsb[:, :], op=ALU.add), 1)
                V.wait_ge(ev.h, ev.n)
                ev.inc(V.tensor_copy(obf[p][:, :], ctile[p][:, :]), 1)

            pbar()
            passc_used = [0, 0]
            for j in range(KT):
                p = j % 2
                if j >= 2:
                    SY.wait_ge(ev.h, passc_used[p])
                    SY.wait_ge(stx[p].h, stx[p].n)
                    SY.wait_ge(stb[p].h, stb[p].n)
                ld_z(j, p)
                comb_x2(j, p)
                passc_used[p] = ev.n
                SY.wait_ge(ev.h, ev.n)
                stx[p].inc(SY.dma_start(out=x2T[j * 128:(j + 1) * 128, :],
                                        in_=ctile[p][:, :]), 16)
                stb[p].inc(SY.dma_start(out=x2bf[j * 128:(j + 1) * 128, :],
                                        in_=obf[p][:, :]), 16)

            for p in range(2):
                G.wait_ge(stu[p].h, stu[p].n)
                G.wait_ge(stz[p].h, stz[p].n)
                G.wait_ge(stx[p].h, stx[p].n)
                G.wait_ge(stb[p].h, stb[p].n)

    return nc


def build_l1a():
    nc = bass.Bass("TRN2", target_bir_lowering=False)
    x2f = nc.dram_tensor("x2f", [D, 2 * S], BF16, kind="ExternalInput").ap()
    wvs = nc.dram_tensor("wvs", [D, 256], F32, kind="ExternalInput").ap()
    wqss = nc.dram_tensor("wqss", [D, 256], F32, kind="ExternalInput").ap()
    wkss = nc.dram_tensor("wkss", [D, 256], F32, kind="ExternalInput").ap()
    qTs = nc.dram_tensor("qTs", [256, 2 * S], BF16, kind="ExternalOutput").ap()
    kTs = nc.dram_tensor("kTs", [256, 2 * S], BF16, kind="ExternalOutput").ap()
    vs = nc.dram_tensor("vs", [2 * S, 256], BF16, kind="ExternalOutput").ap()

    NCHB = S // CH

    with ExitStack() as ctx:
        e = ctx.enter_context
        ident = e(nc.sbuf_tensor("ident", [128, 128], BF16))
        iot = e(nc.sbuf_tensor("iot", [128, 128], I32))
        wslc = e(nc.sbuf_tensor("wslc", [128, KT, 256], BF16))
        wkslc = e(nc.sbuf_tensor("wkslc", [128, KT, 256], BF16))
        wvslc = e(nc.sbuf_tensor("wvslc", [128, KT, 256], BF16))
        xch = [e(nc.sbuf_tensor(f"xch{p}", [128, KT, CH], BF16)) for p in range(2)]
        qout = [e(nc.sbuf_tensor(f"qout{p}", [128, 2, CH], BF16)) for p in range(2)]
        kout = [e(nc.sbuf_tensor(f"kout{p}", [128, 2, CH], BF16)) for p in range(2)]
        vT_sb = [e(nc.sbuf_tensor(f"vT{p}", [128, 2, CH], BF16)) for p in range(2)]
        vnat = [e(nc.sbuf_tensor(f"vnat{p}", [128, 4, 256], BF16)) for p in range(2)]

        sld = SC(ctx, nc, "sld")
        xl = [SC(ctx, nc, f"xl{p}") for p in range(2)]
        mm = SC(ctx, nc, "mm")
        ev = SC(ctx, nc, "ev")
        gp = SC(ctx, nc, "gp")
        stq = [SC(ctx, nc, f"stq{p}") for p in range(2)]
        stk = [SC(ctx, nc, f"stk{p}") for p in range(2)]
        stv = [SC(ctx, nc, f"stv{p}") for p in range(2)]
        block = e(nc.Block())

        @block.gpsimd
        def _(g):
            T, V, SCA, G, SY = nc.tensor, nc.vector, nc.scalar, nc.gpsimd, nc.sync

            gp.inc(G.iota(iot[:, :], pattern=[[1, 128]], base=0, channel_multiplier=-1), 1)
            V.wait_ge(gp.h, gp.n)
            ev.inc(V.tensor_scalar(out=ident[:, :], in0=iot[:, :], scalar1=0.0,
                                   scalar2=None, op0=ALU.is_equal), 1)
            ev_id = ev.n
            sld.inc(G.dma_start(out=wvslc[:, :, :], in_=_r16(wvs)), 16)
            sld.inc(G.dma_start(out=wslc[:, :, :], in_=_r16(wqss)), 16)
            sld.inc(G.dma_start(out=wkslc[:, :, :], in_=_r16(wkss)), 16)
            sld_n = sld.n

            xused = [0, 0]
            vused = [0, 0]
            for b in range(2):
                for cc in range(NCHB):
                    p = cc % 2
                    tok0 = b * S + cc * CH
                    if xused[p]:
                        SY.wait_ge(mm.h, xused[p])
                    xl[p].inc(SY.dma_start(
                        out=xch[p][:, :, :],
                        in_=x2f.rearrange("(kt q) t -> q kt t", q=128)[:, :, tok0:tok0 + CH]), 16)
                    T.wait_ge(xl[p].h, xl[p].n)
                    if cc == 0 and b == 0:
                        T.wait_ge(sld.h, sld_n)
                        T.wait_ge(ev.h, ev_id)
                    with ExitStack() as c3:
                        mb = [c3.enter_context(nc.psum_tensor(f"mb{b}_{cc}_{i}", [128, CH], F32))
                              for i in range(6)]
                        tp2 = [c3.enter_context(nc.psum_tensor(f"t2_{b}_{cc}_{i}", [128, 128], BF16))
                               for i in range(2)]
                        specs = [(wslc, 0, "q"), (wslc, 1, "q"), (wkslc, 0, "k"),
                                 (wkslc, 1, "k"), (wvslc, 0, "v"), (wvslc, 1, "v")]
                        bank_ev2 = [0] * 6
                        for si, (wsb, et, kind) in enumerate(specs):
                            bk = si % 6
                            if bank_ev2[bk]:
                                T.wait_ge(ev.h, bank_ev2[bk])
                            for mt in range(KT):
                                ins = T.matmul(mb[bk][:, :],
                                               wsb[:, mt, et * 128:(et + 1) * 128],
                                               xch[p][:, mt, :],
                                               start=(mt == 0), stop=(mt == KT - 1))
                            mm.inc(ins, 1)
                            if si == len(specs) - 1:
                                xused[p] = mm.n
                            V.wait_ge(mm.h, mm.n)
                            if kind == "q":
                                if et == 0:
                                    V.wait_ge(stq[p].h, stq[p].n)
                                ev.inc(V.tensor_copy(qout[p][:, et, :], mb[bk][:, :]), 1)
                            elif kind == "k":
                                if et == 0:
                                    V.wait_ge(stk[p].h, stk[p].n)
                                ev.inc(V.tensor_copy(kout[p][:, et, :], mb[bk][:, :]), 1)
                            else:
                                ev.inc(V.tensor_copy(vT_sb[p][:, et, :], mb[bk][:, :]), 1)
                            bank_ev2[bk] = ev.n
                        ev_vT = ev.n
                        if vused[p]:
                            V.wait_ge(stv[p].h, stv[p].n)
                        tb_ev = [0, 0]
                        for et in range(2):
                            for tt in range(4):
                                tb = (et * 4 + tt) % 2
                                T.wait_ge(ev.h, ev_vT if tb_ev[tb] == 0 else tb_ev[tb])
                                ins = T.transpose(tp2[tb][:, :],
                                                  vT_sb[p][:, et, tt * 128:(tt + 1) * 128],
                                                  ident[:, :])
                                mm.inc(ins, 1)
                                V.wait_ge(mm.h, mm.n)
                                ev.inc(V.tensor_copy(
                                    vnat[p][:, tt, et * 128:(et + 1) * 128],
                                    tp2[tb][:, :]), 1)
                                tb_ev[tb] = ev.n
                        vused[p] = ev.n
                        T.wait_ge(ev.h, ev.n)
                    SY.wait_ge(ev.h, ev.n)
                    stq[p].inc(SY.dma_start(
                        out=qTs.rearrange("(et q) t -> q et t", q=128)[:, :, tok0:tok0 + CH],
                        in_=qout[p][:, :, :]), 16)
                    stk[p].inc(SY.dma_start(
                        out=kTs.rearrange("(et q) t -> q et t", q=128)[:, :, tok0:tok0 + CH],
                        in_=kout[p][:, :, :]), 16)
                    stv[p].inc(SY.dma_start(
                        out=vs.rearrange("(c tt q) e -> q c tt e", q=128, tt=4)[:, b * NCHB + cc, :, :],
                        in_=vnat[p][:, :, :]), 16)
            for p in range(2):
                G.wait_ge(stq[p].h, stq[p].n)
                G.wait_ge(stk[p].h, stk[p].n)
                G.wait_ge(stv[p].h, stv[p].n)

    return nc


def build_l1ind():
    nc = bass.Bass("TRN2", target_bir_lowering=False)
    qlf = nc.dram_tensor("qlf", [D, 2 * S], BF16, kind="ExternalInput").ap()
    klf = nc.dram_tensor("klf", [D, 2 * S], BF16, kind="ExternalInput").ap()
    wqis = nc.dram_tensor("wqis", [D, 256], F32, kind="ExternalInput").ap()
    wkis = nc.dram_tensor("wkis", [D, 256], F32, kind="ExternalInput").ap()
    hwme = nc.dram_tensor("hwme", [1, 1], F32, kind="ExternalInput").ap()
    temp = nc.dram_tensor("temp", [1, 1], F32, kind="ExternalInput").ap()
    relq = nc.dram_tensor("relq", [128, 128], F32, kind="ExternalOutput").ap()

    NCHB = S // CH
    QT = S // 128

    with ExitStack() as ctx:
        e = ctx.enter_context
        ones_r = e(nc.sbuf_tensor("ones_r", [1, 128], F32))
        wqi_sb = e(nc.sbuf_tensor("wqi_sb", [128, KT, 256], BF16))
        wki_sb = e(nc.sbuf_tensor("wki_sb", [128, KT, 256], BF16))
        xch = [e(nc.sbuf_tensor(f"xch{p}", [128, KT, CH], BF16)) for p in range(2)]
        kch = [e(nc.sbuf_tensor(f"kch{p}", [128, KT, CH], BF16)) for p in range(2)]
        qpT = e(nc.sbuf_tensor("qpT", [128, 2, S], BF16))
        kpT = e(nc.sbuf_tensor("kpT", [128, 2, S], BF16))
        relbuf = e(nc.sbuf_tensor("relbuf", [128, 16], F32))
        relsb = e(nc.sbuf_tensor("relsb", [128, 128], F32))
        fac = e(nc.sbuf_tensor("fac", [1, 1], F32))
        hwsb = e(nc.sbuf_tensor("hwsb", [1, 1], F32))
        tsb = e(nc.sbuf_tensor("tsb", [1, 1], F32))
        fcol = e(nc.sbuf_tensor("fcol", [128, 1], F32))
        dmp = e(nc.sbuf_tensor("dmp", [128, CH], BF16))
        dmp2 = e(nc.sbuf_tensor("dmp2", [128, CH], BF16))

        ld = SC(ctx, nc, "ld")
        sld = SC(ctx, nc, "sld")
        xl = [SC(ctx, nc, f"xl{p}") for p in range(2)]
        kl = [SC(ctx, nc, f"kl{p}") for p in range(2)]
        mm = SC(ctx, nc, "mm")
        ev = SC(ctx, nc, "ev")
        sc = SC(ctx, nc, "sc")
        str_ = SC(ctx, nc, "str")
        block = e(nc.Block())

        @block.gpsimd
        def _(g):
            T, V, SCA, G, SY = nc.tensor, nc.vector, nc.scalar, nc.gpsimd, nc.sync

            ev.inc(V.memset(ones_r[:, :], 1.0), 1)
            ev_ones = ev.n
            ld.inc(SY.dma_start(out=hwsb[:, :], in_=hwme[:, :]), 16)
            ld.inc(SY.dma_start(out=tsb[:, :], in_=temp[:, :]), 16)
            SCA.wait_ge(ld.h, 32)
            sc.inc(SCA.activation(out=fac[:, :], in_=tsb[:, :], func=AF.Exp,
                                  scale=-1.0), 1)
            V.wait_ge(sc.h, sc.n)
            ev.inc(V.tensor_tensor(out=fac[:, :], in0=fac[:, :], in1=hwsb[:, :],
                                   op=ALU.mult), 1)
            ev_fac = ev.n
            T.wait_ge(ev.h, ev_fac)
            with nc.psum_tensor("fps", [128, 1], F32) as fps:
                ins = T.matmul(fps[:, :], ones_r[:, :], fac[:, :], start=True, stop=True)
                mm.inc(ins, 1)
                V.wait_ge(mm.h, mm.n)
                ev.inc(V.tensor_copy(fcol[:, :], fps[:, :]), 1)
            ev_fcol = ev.n
            sld.inc(G.dma_start(out=wqi_sb[:, :, :], in_=_r16(wqis)), 16)
            sld.inc(G.dma_start(out=wki_sb[:, :, :], in_=_r16(wkis)), 16)
            sld_n = sld.n

            for b in range(2):
                xused = [0, 0]
                for cc in range(NCHB):
                    p = cc % 2
                    tok0 = b * S + cc * CH
                    if b > 0 or cc >= 2:
                        SY.wait_ge(mm.h, xused[p])
                    xl[p].inc(SY.dma_start(
                        out=xch[p][:, :, :],
                        in_=qlf.rearrange("(kt q) t -> q kt t", q=128)[:, :, tok0:tok0 + CH]), 16)
                    kl[p].inc(SY.dma_start(
                        out=kch[p][:, :, :],
                        in_=klf.rearrange("(kt q) t -> q kt t", q=128)[:, :, tok0:tok0 + CH]), 16)
                    T.wait_ge(xl[p].h, xl[p].n)
                    T.wait_ge(kl[p].h, kl[p].n)
                    if cc == 0 and b == 0:
                        T.wait_ge(sld.h, sld_n)
                    if b > 0 and cc < 2:
                        T.wait_ge(mm.h, mm.n)
                    with ExitStack() as c3:
                        mb = [c3.enter_context(
                            nc.psum_tensor(f"mb{b}_{cc}_{i}", [128, CH], F32))
                            for i in range(4)]
                        specs = [(wqi_sb, xch, qpT, 0), (wqi_sb, xch, qpT, 1),
                                 (wki_sb, kch, kpT, 0), (wki_sb if False else wki_sb, kch, kpT, 1)]
                        bank_ev2 = [0] * 4
                        for si, (wsb, src, dst, et) in enumerate(specs):
                            bk = si % 4
                            if bank_ev2[bk]:
                                T.wait_ge(ev.h, bank_ev2[bk])
                            for mt in range(KT):
                                ins = T.matmul(mb[bk][:, :],
                                               wsb[:, mt, et * 128:(et + 1) * 128],
                                               src[p][:, mt, :],
                                               start=(mt == 0), stop=(mt == KT - 1))
                            mm.inc(ins, 1)
                            if si == len(specs) - 1:
                                xused[p] = mm.n
                            V.wait_ge(mm.h, mm.n)
                            ev.inc(V.tensor_copy(dst[:, et, cc * CH:(cc + 1) * CH],
                                                 mb[bk][:, :]), 1)
                            bank_ev2[bk] = ev.n
                        T.wait_ge(ev.h, ev.n)
                with ExitStack() as c4:
                    sb8 = [c4.enter_context(nc.psum_tensor(f"sb{b}_{i}", [128, CH], F32))
                           for i in range(8)]
                    T.wait_ge(ev.h, ev.n)
                    bank_sc = [0] * 8
                    rel_red_ev = 0
                    for qt in range(QT):
                        for kcg in range(2):
                            for et in range(2):
                                for k8 in range(8):
                                    kc = kcg * 8 + k8
                                    if et == 0 and bank_sc[k8]:
                                        kind_, val_ = bank_sc[k8]
                                        T.wait_ge(sc.h if kind_ == "sc" else ev.h, val_)
                                    ins = T.matmul(
                                        sb8[k8][:, :],
                                        qpT[:, et, qt * 128:(qt + 1) * 128],
                                        kpT[:, et, kc * CH:(kc + 1) * CH],
                                        start=(et == 0), stop=(et == 1))
                                    if et == 1:
                                        mm.inc(ins, 1)
                            for k8 in range(8):
                                col = relbuf[:, kcg * 8 + k8:kcg * 8 + k8 + 1]
                                if k8 % 2 == 0:
                                    SCA.wait_ge(mm.h, mm.n - (7 - k8))
                                    SCA.wait_ge(sc.h, sc.n)
                                    if rel_red_ev:
                                        SCA.wait_ge(ev.h, rel_red_ev)
                                    sc.inc(SCA.activation(
                                        out=dmp[:, :], in_=sb8[k8][:, :], func=AF.Relu,
                                        accum_out=col), 1)
                                    bank_sc[k8] = ("sc", sc.n)
                                else:
                                    V.wait_ge(mm.h, mm.n - (7 - k8))
                                    V.wait_ge(ev.h, ev.n)
                                    if rel_red_ev:
                                        V.wait_ge(ev.h, rel_red_ev)
                                    ev.inc(V.tensor_scalar(
                                        out=dmp2[:, :], in0=sb8[k8][:, :], scalar1=0.0,
                                        scalar2=None, op0=ALU.max, op1=ALU.add,
                                        accum_out=col), 1)
                                    bank_sc[k8] = ("ev", ev.n)
                        V.wait_ge(sc.h, sc.n)
                        V.wait_ge(ev.h, ev.n)
                        ev.inc(V.tensor_reduce(out=relsb[:, b * 64 + qt:b * 64 + qt + 1],
                                               in_=relbuf[:, :], axis=AX.X, op=ALU.add), 1)
                        rel_red_ev = ev.n
                    T.wait_ge(ev.h, ev.n)

            V.wait_ge(ev.h, ev.n)
            V.wait_ge(ev.h, ev_fcol)
            ev.inc(V.tensor_scalar(out=relsb[:, :], in0=relsb[:, :], scalar1=fcol[:, :],
                                   scalar2=None, op0=ALU.mult), 1)
            G.wait_ge(ev.h, ev.n)
            str_.inc(G.dma_start(out=relq[:, :], in_=relsb[:, :]), 16)
            G.wait_ge(str_.h, str_.n)

    return nc


def build_l1b():
    nc = bass.Bass("TRN2", target_bir_lowering=False)
    relqs = nc.dram_tensor("relqs", [NCORE, 128, 128], F32, kind="ExternalInput").ap()
    qTs = nc.dram_tensor("qTs", [256, 2 * S], BF16, kind="ExternalInput").ap()
    kTs = nc.dram_tensor("kTs", [256, 2 * S], BF16, kind="ExternalInput").ap()
    vs = nc.dram_tensor("vs", [2 * S, 256], BF16, kind="ExternalInput").ap()
    avnT = nc.dram_tensor("avnT", [256, 2 * S], BF16, kind="ExternalOutput").ap()

    SCL = 1.0 / np.sqrt(128.0)
    NEG = -1.0e9
    NITER = 48
    QCN = S // CH
    KTN = S // 128
    GRP = [(0, 6), (6, 6), (12, 4)]

    with ExitStack() as ctx:
        e = ctx.enter_context
        relsb = e(nc.sbuf_tensor("relsb", [128, 128], F32))
        rsum = e(nc.sbuf_tensor("rsum", [128, NCORE, 128], F32))
        ones_c = e(nc.sbuf_tensor("ones_c", [128, 1], F32))
        ones_r = e(nc.sbuf_tensor("ones_r", [1, 128], F32))
        identf = e(nc.sbuf_tensor("identf", [128, 128], F32))
        iot = e(nc.sbuf_tensor("iot", [128, 512], I32))
        dvals = e(nc.sbuf_tensor("dvals", [128, 512], F32))
        statm = e(nc.sbuf_tensor("statm", [128, 8, 512], F32))
        statc = e(nc.sbuf_tensor("statc", [128, 4, 512], F32))
        mtile = e(nc.sbuf_tensor("mtile", [128, 512], F32))
        msum = e(nc.sbuf_tensor("msum", [128, 512], F32))
        selb = e(nc.sbuf_tensor("selb", [128, 128], F32))
        gmin = e(nc.sbuf_tensor("gmin", [1, 1], F32))
        gmax = e(nc.sbuf_tensor("gmax", [1, 1], F32))
        lo = e(nc.sbuf_tensor("lo", [1, 1], F32))
        hi = e(nc.sbuf_tensor("hi", [1, 1], F32))
        mid = e(nc.sbuf_tensor("mid", [1, 1], F32))
        dlt = e(nc.sbuf_tensor("dlt", [1, 1], F32))
        pd = e(nc.sbuf_tensor("pd", [1, 1], F32))
        tot = e(nc.sbuf_tensor("tot", [1, 1], F32))
        pred = e(nc.sbuf_tensor("pred", [1, 1], F32))
        mcol = e(nc.sbuf_tensor("mcol", [128, 1], F32))
        locol = e(nc.sbuf_tensor("locol", [128, 1], F32))
        cnt = e(nc.sbuf_tensor("cnt", [128, 1], F32))
        cdmp = e(nc.sbuf_tensor("cdmp", [128, 64], F32))
        red1 = e(nc.sbuf_tensor("red1", [128, 1], F32))
        red2 = e(nc.sbuf_tensor("red2", [1, 128], F32))
        qb = e(nc.sbuf_tensor("qb", [128, 2, S], BF16))
        kb = e(nc.sbuf_tensor("kb", [128, 2, S], BF16))
        vb = e(nc.sbuf_tensor("vb", [128, KTN, 256], BF16))
        avn = e(nc.sbuf_tensor("avn", [128, 2, S], BF16))
        dacc = e(nc.sbuf_tensor("dacc", [128, 6 * CH], F32))
        pex = [e(nc.sbuf_tensor(f"pex{i}", [128, CH], BF16)) for i in range(2)]
        drow = e(nc.sbuf_tensor("drow", [1, CH], F32))
        bsb = e(nc.sbuf_tensor("bsb", [128, CH], F32))

        ld = SC(ctx, nc, "ld")
        bl = SC(ctx, nc, "bl")
        mm = SC(ctx, nc, "mm")
        ev = SC(ctx, nc, "ev")
        sc = SC(ctx, nc, "sc")
        gp = SC(ctx, nc, "gp")
        sta = SC(ctx, nc, "sta")
        block = e(nc.Block())

        @block.gpsimd
        def _(g):
            T, V, SCA, G, SY = nc.tensor, nc.vector, nc.scalar, nc.gpsimd, nc.sync

            gp.inc(G.iota(iot[:, :], pattern=[[1, 512]], base=0, channel_multiplier=-1), 1)
            ld.inc(SY.dma_start(out=rsum[:, :, :], in_=relqs.rearrange("r p c -> p r c")), 16)
            ev.inc(V.memset(ones_c[:, :], 1.0), 1)
            ev.inc(V.memset(ones_r[:, :], 1.0), 1)
            V.wait_ge(gp.h, gp.n)
            ev.inc(V.tensor_scalar(out=identf[:, :], in0=iot[:, 0:128], scalar1=0.0,
                                   scalar2=None, op0=ALU.is_equal), 1)
            ev.inc(V.tensor_copy(dvals[:, :], iot[:, :]), 1)
            V.wait_ge(ld.h, 16)
            ev.inc(V.tensor_reduce(out=relsb[:, :], in_=rsum.rearrange("p r c -> p c r"),
                                   axis=AX.X, op=ALU.add), 1)
            V.wait_ge(ev.h, ev.n)
            ev_rel = ev.n

            offs = [-512, -384, -256, -128, 0, 128, 256, 384]
            off_idx = {o: i for i, o in enumerate(offs)}
            for i, off in enumerate(offs):
                ev.inc(V.tensor_scalar(out=statm[:, i, :], in0=dvals[:, :],
                                       scalar1=float(off), scalar2=None, op0=ALU.is_ge), 1)
                ev.inc(V.tensor_scalar(out=mtile[:, :], in0=dvals[:, :],
                                       scalar1=float(off + 512), scalar2=None,
                                       op0=ALU.is_le), 1)
                V.wait_ge(ev.h, ev.n)
                ev.inc(V.tensor_tensor(out=statm[:, i, :], in0=statm[:, i, :],
                                       in1=mtile[:, :], op=ALU.mult), 1)
                V.wait_ge(ev.h, ev.n)
                ev.inc(V.tensor_scalar(out=statm[:, i, :], in0=statm[:, i, :],
                                       scalar1=-NEG, scalar2=NEG, op0=ALU.mult,
                                       op1=ALU.add), 1)
                V.wait_ge(ev.h, ev.n)
            for ci_, off in enumerate((0, 128, 256, 384)):
                ev.inc(V.tensor_scalar(out=statc[:, ci_, :], in0=dvals[:, :],
                                       scalar1=float(off), scalar2=None,
                                       op0=ALU.is_ge), 1)
                V.wait_ge(ev.h, ev.n)
                ev.inc(V.tensor_scalar(out=statc[:, ci_, :], in0=statc[:, ci_, :],
                                       scalar1=-NEG, scalar2=NEG, op0=ALU.mult,
                                       op1=ALU.add), 1)
                V.wait_ge(ev.h, ev.n)
            ev_stat = ev.n

            with nc.psum_tensor("tps0", [128, 128], F32) as tps0, \
                 nc.psum_tensor("tps1", [1, 1], F32) as tps1:
                for which, opx, dst in ((0, ALU.max, gmax), (1, ALU.min, gmin)):
                    V.wait_ge(ev.h, ev.n)
                    ev.inc(V.tensor_reduce(out=red1[:, :], in_=relsb[:, :],
                                           axis=AX.X, op=opx), 1)
                    T.wait_ge(ev.h, ev.n)
                    ins = T.transpose(tps0[0:1, 0:128], red1[:, 0:1], identf[:, :])
                    mm.inc(ins, 1)
                    V.wait_ge(mm.h, mm.n)
                    ev.inc(V.tensor_copy(red2[:, :], tps0[0:1, 0:128]), 1)
                    V.wait_ge(ev.h, ev.n)
                    ev.inc(V.tensor_reduce(out=dst[:, :], in_=red2[:, :],
                                           axis=AX.X, op=opx), 1)
                    V.wait_ge(ev.h, ev.n)
            ev_mm0 = mm.n

            with nc.psum_tensor("bps0", [128, 1], F32) as bps0, \
                 nc.psum_tensor("bps1", [1, 1], F32) as bps1:
                for b in range(2):
                    rel_b = relsb[:, b * 64:(b + 1) * 64]
                    V.wait_ge(ev.h, ev.n)
                    ev.inc(V.tensor_scalar(out=lo[:, :], in0=gmin[:, :], scalar1=-1.0,
                                           scalar2=None, op0=ALU.add), 1)
                    ev.inc(V.tensor_scalar(out=hi[:, :], in0=gmax[:, :], scalar1=1.0,
                                           scalar2=None, op0=ALU.add), 1)
                    for it in range(NITER):
                        V.wait_ge(ev.h, ev.n)
                        ev.inc(V.tensor_tensor(out=mid[:, :], in0=lo[:, :], in1=hi[:, :],
                                               op=ALU.add), 1)
                        V.wait_ge(ev.h, ev.n)
                        ev.inc(V.tensor_scalar(out=mid[:, :], in0=mid[:, :], scalar1=0.5,
                                               scalar2=None, op0=ALU.mult), 1)
                        T.wait_ge(ev.h, ev.n)
                        ins = T.matmul(bps0[:, :], ones_r[:, :], mid[:, :],
                                       start=True, stop=True)
                        mm.inc(ins, 1)
                        V.wait_ge(mm.h, mm.n)
                        ev.inc(V.tensor_copy(mcol[:, :], bps0[:, :]), 1)
                        V.wait_ge(ev.h, ev.n)
                        ev.inc(V.tensor_scalar(out=cdmp[:, :], in0=rel_b,
                                               scalar1=mcol[:, :], scalar2=None,
                                               op0=ALU.is_ge, op1=ALU.add,
                                               accum_out=cnt[:, :]), 1)
                        T.wait_ge(ev.h, ev.n)
                        ins = T.matmul(bps1[:, :], cnt[:, :], ones_c[:, :],
                                       start=True, stop=True)
                        mm.inc(ins, 1)
                        V.wait_ge(mm.h, mm.n)
                        ev.inc(V.tensor_scalar(out=pred[:, :], in0=bps1[:, :],
                                               scalar1=1024.0, scalar2=None,
                                               op0=ALU.is_ge), 1)
                        V.wait_ge(ev.h, ev.n)
                        ev.inc(V.tensor_tensor(out=dlt[:, :], in0=mid[:, :], in1=lo[:, :],
                                               op=ALU.subtract), 1)
                        V.wait_ge(ev.h, ev.n)
                        ev.inc(V.tensor_tensor(out=pd[:, :], in0=pred[:, :], in1=dlt[:, :],
                                               op=ALU.mult), 1)
                        V.wait_ge(ev.h, ev.n)
                        ev.inc(V.tensor_tensor(out=lo[:, :], in0=lo[:, :], in1=pd[:, :],
                                               op=ALU.add), 1)
                        ev.inc(V.tensor_tensor(out=dlt[:, :], in0=hi[:, :], in1=mid[:, :],
                                               op=ALU.subtract), 1)
                        V.wait_ge(ev.h, ev.n)
                        ev.inc(V.tensor_tensor(out=pd[:, :], in0=pred[:, :], in1=dlt[:, :],
                                               op=ALU.mult), 1)
                        V.wait_ge(ev.h, ev.n)
                        ev.inc(V.tensor_tensor(out=hi[:, :], in0=mid[:, :], in1=pd[:, :],
                                               op=ALU.add), 1)
                    V.wait_ge(ev.h, ev.n)
                    T.wait_ge(ev.h, ev.n)
                    ins = T.matmul(bps0[:, :], ones_r[:, :], lo[:, :],
                                   start=True, stop=True)
                    mm.inc(ins, 1)
                    V.wait_ge(mm.h, mm.n)
                    ev.inc(V.tensor_copy(locol[:, :], bps0[:, :]), 1)
                    V.wait_ge(ev.h, ev.n)
                    ev.inc(V.tensor_scalar(out=selb[:, b * 64:(b + 1) * 64], in0=rel_b,
                                           scalar1=locol[:, :], scalar2=None,
                                           op0=ALU.is_ge), 1)
                    V.wait_ge(ev.h, ev.n)
                    ev.inc(V.tensor_scalar(out=selb[:, b * 64:(b + 1) * 64],
                                           in0=selb[:, b * 64:(b + 1) * 64],
                                           scalar1=-NEG, scalar2=NEG, op0=ALU.mult,
                                           op1=ALU.add), 1)
                    V.wait_ge(ev.h, ev.n)
            ev_sel = ev.n

            pex_mm = [0, 0]
            pex_ev = [0, 0]
            for b in range(2):
                G.wait_ge(mm.h, mm.n)
                G.wait_ge(ev.h, ev.n)
                if b > 0:
                    V.wait_ge(sta.h, sta.n)
                bl.inc(G.dma_start(out=qb[:, :, :],
                                   in_=qTs.rearrange("(et p) t -> p et t", p=128)
                                   [:, :, b * S:(b + 1) * S]), 16)
                bl.inc(G.dma_start(out=kb[:, :, :],
                                   in_=kTs.rearrange("(et p) t -> p et t", p=128)
                                   [:, :, b * S:(b + 1) * S]), 16)
                bl.inc(G.dma_start(out=vb[:, :, :],
                                   in_=vs.rearrange("(c p) e -> p c e", p=128)
                                   [:, b * KTN:(b + 1) * KTN, :]), 16)
                bl_b = bl.n
                for h in range(2):
                    with ExitStack() as c5:
                        avp = [c5.enter_context(
                            nc.psum_tensor(f"avp{b}_{h}_{i}", [128, CH], F32))
                            for i in range(6)]
                        sps = [c5.enter_context(
                            nc.psum_tensor(f"sps{b}_{h}_{i}", [128, CH], F32))
                            for i in range(2)]
                        T.wait_ge(bl.h, bl_b)
                        if b == 0 and h == 0:
                            T.wait_ge(ev.h, ev_sel)
                        for (c0, ncq) in GRP:
                            V.wait_ge(ev.h, ev.n)
                            ev.inc(V.memset(dacc[:, 0:ncq * CH], 0.0), 1)
                            ev_dz = ev.n
                            T.wait_ge(ev.h, ev_dz)
                            ktmax = ((c0 + ncq) * CH - 1) // 128
                            av_open = [False] * ncq
                            sbk_ev = [0, 0]
                            sbk_i = 0
                            for kt in range(ktmax + 1):
                                k0 = kt * 128
                                for ci in range(ncq):
                                    qc = c0 + ci
                                    qlo, qhi = qc * CH, qc * CH + CH - 1
                                    if k0 > qhi:
                                        continue
                                    deep = (k0 + 127) < (qlo - 512)
                                    sbk = sbk_i % 2
                                    sbk_i += 1
                                    if sbk_ev[sbk]:
                                        T.wait_ge(sc.h, sbk_ev[sbk])
                                    ins = T.matmul(
                                        sps[sbk][:, :],
                                        kb[:, h, k0:k0 + 128],
                                        qb[:, h, qc * CH:(qc + 1) * CH],
                                        start=True, stop=True)
                                    mm.inc(ins, 1)
                                    mm_sc = mm.n
                                    pxb = sbk
                                    if deep:
                                        SCA.wait_ge(mm.h, mm_sc)
                                        SCA.wait_ge(sc.h, sc.n)
                                        if pex_mm[pxb]:
                                            SCA.wait_ge(mm.h, pex_mm[pxb])
                                            SCA.wait_ge(ev.h, pex_ev[pxb])
                                        sc.inc(SCA.activation(
                                            out=pex[pxb][:, :], in_=sps[sbk][:, :],
                                            func=AF.Exp, scale=SCL,
                                            bias=selb[:, b * 64 + kt:b * 64 + kt + 1]), 1)
                                    else:
                                        off = k0 - qlo
                                        V.wait_ge(ev.h, ev.n)
                                        ev.inc(V.tensor_scalar(
                                            out=mtile[:, :], in0=statm[:, off_idx[off], :],
                                            scalar1=selb[:, b * 64 + kt:b * 64 + kt + 1],
                                            scalar2=None, op0=ALU.max), 1)
                                        if off >= 0:
                                            V.wait_ge(ev.h, ev.n)
                                            ev.inc(V.tensor_tensor(
                                                out=mtile[:, :], in0=mtile[:, :],
                                                in1=statc[:, off // 128, :],
                                                op=ALU.add), 1)
                                        V.wait_ge(mm.h, mm_sc)
                                        V.wait_ge(ev.h, ev.n)
                                        ev.inc(V.tensor_tensor(
                                            out=msum[:, :], in0=sps[sbk][:, :],
                                            in1=mtile[:, :], op=ALU.add), 1)
                                        SCA.wait_ge(ev.h, ev.n)
                                        SCA.wait_ge(sc.h, sc.n)
                                        if pex_mm[pxb]:
                                            SCA.wait_ge(mm.h, pex_mm[pxb])
                                            SCA.wait_ge(ev.h, pex_ev[pxb])
                                        sc.inc(SCA.activation(
                                            out=pex[pxb][:, :], in_=msum[:, :],
                                            func=AF.Exp, scale=SCL), 1)
                                    sbk_ev[sbk] = sc.n
                                    V.wait_ge(sc.h, sc.n)
                                    V.wait_ge(ev.h, ev_dz)
                                    ev.inc(V.tensor_tensor(
                                        out=dacc[:, ci * CH:(ci + 1) * CH],
                                        in0=dacc[:, ci * CH:(ci + 1) * CH],
                                        in1=pex[pxb][:, :], op=ALU.add), 1)
                                    pex_ev[pxb] = ev.n
                                    T.wait_ge(sc.h, sc.n)
                                    islast = (kt == min(ktmax, (qhi) // 128))
                                    ins = T.matmul(
                                        avp[ci][:, :], vb[:, b * 0 + kt, h * 128:(h + 1) * 128],
                                        pex[pxb][:, :],
                                        start=(not av_open[ci]), stop=islast)
                                    mm.inc(ins, 1)
                                    pex_mm[pxb] = mm.n
                                    av_open[ci] = True
                            for ci in range(ncq):
                                qc = c0 + ci
                                V.wait_ge(ev.h, ev.n)
                                T.wait_ge(ev.h, ev.n)
                                T.wait_ge(sc.h, sc.n)
                                ins = T.matmul(sps[0][0:1, :], ones_c[:, :],
                                               dacc[:, ci * CH:(ci + 1) * CH],
                                               start=True, stop=True)
                                mm.inc(ins, 1)
                                V.wait_ge(mm.h, mm.n)
                                ev.inc(V.reciprocal(out=drow[:, :], in_=sps[0][0:1, :]), 1)
                                T.wait_ge(ev.h, ev.n)
                                ins = T.matmul(sps[1][:, :], ones_r[:, :], drow[:, :],
                                               start=True, stop=True)
                                mm.inc(ins, 1)
                                V.wait_ge(mm.h, mm.n)
                                ev.inc(V.tensor_copy(bsb[:, :], sps[1][:, :]), 1)
                                V.wait_ge(ev.h, ev.n)
                                ev.inc(V.tensor_tensor(
                                    out=avn[:, h, qc * CH:(qc + 1) * CH],
                                    in0=avp[ci][:, :], in1=bsb[:, :], op=ALU.mult), 1)
                            V.wait_ge(ev.h, ev.n)
                        T.wait_ge(ev.h, ev.n)
                SY.wait_ge(ev.h, ev.n)
                sta.inc(SY.dma_start(
                    out=avnT.rearrange("(et p) t -> p et t", p=128)[:, :, b * S:(b + 1) * S],
                    in_=avn[:, :, :]), 16)
            G.wait_ge(sta.h, sta.n)

    return nc


def build_quant():
    nc = bass.Bass("TRN2", target_bir_lowering=False)
    gur = nc.dram_tensor("gur", [D, D], F32, kind="ExternalInput").ap()
    wpr = nc.dram_tensor("wpr", [256, 4 * D], F32, kind="ExternalInput").ap()
    guq = nc.dram_tensor("guq", [D, D], BF16, kind="ExternalOutput").ap()
    wpq = nc.dram_tensor("wpq", [256, 4 * D], BF16, kind="ExternalOutput").ap()

    F = 2048
    G_ = F // 64

    with ExitStack() as ctx:
        e = ctx.enter_context
        wb = [e(nc.sbuf_tensor(f"wb{p}", [128, F], BF16)) for p in range(2)]
        gs = e(nc.sbuf_tensor("gs", [128, G_], F32))
        scl = e(nc.sbuf_tensor("scl", [128, G_], BF16))
        r32 = e(nc.sbuf_tensor("r32", [128, F], F32))
        sgn = e(nc.sbuf_tensor("sgn", [128, F], F32))
        ra = e(nc.sbuf_tensor("ra", [128, F], F32))
        ri = e(nc.sbuf_tensor("ri", [128, F], I32))
        hi_ = e(nc.sbuf_tensor("hi_", [128, F], I32))
        fr = e(nc.sbuf_tensor("fr", [128, F], F32))
        aa = e(nc.sbuf_tensor("aa", [128, F], F32))
        ee = e(nc.sbuf_tensor("ee", [128, F], F32))
        oo = e(nc.sbuf_tensor("oo", [128, F], F32))
        qa = e(nc.sbuf_tensor("qa", [128, F], F32))
        ob = [e(nc.sbuf_tensor(f"ob{p}", [128, F], BF16)) for p in range(2)]

        ldw = [SC(ctx, nc, f"ldw{p}") for p in range(2)]
        ev = SC(ctx, nc, "ev")
        sc = SC(ctx, nc, "sc")
        sto = [SC(ctx, nc, f"sto{p}") for p in range(2)]
        block = e(nc.Block())

        @block.gpsimd
        def _(g):
            T, V, SCA, G, SY = nc.tensor, nc.vector, nc.scalar, nc.gpsimd, nc.sync

            units = [("gu", t, 0) for t in range(KT)]
            units += [("wp", t, c) for t in range(2) for c in range(4)]
            used_ev = [0, 0]
            for i, (kind, t, c) in enumerate(units):
                p = i % 2
                if i >= 2:
                    G.wait_ge(ev.h, used_ev[p])
                    G.wait_ge(sto[p].h, sto[p].n)
                if kind == "gu":
                    src_ap = gur[t * 128:(t + 1) * 128, :]
                    dst_ap = guq[t * 128:(t + 1) * 128, :]
                else:
                    src_ap = wpr[t * 128:(t + 1) * 128, c * F:(c + 1) * F]
                    dst_ap = wpq[t * 128:(t + 1) * 128, c * F:(c + 1) * F]
                ldw[p].inc(G.dma_start(out=wb[p][:, :], in_=src_ap), 16)
                V.wait_ge(ldw[p].h, ldw[p].n)
                V.wait_ge(ev.h, ev.n)
                gv = lambda x: x.rearrange("q (g n) -> q g n", n=64)
                bscl = scl[:, :].to_broadcast((128, G_, 64))
                ev.inc(V.tensor_reduce(out=gs[:, :], in_=gv(wb[p][:, :]),
                                       axis=AX.X, op=ALU.add,
                                       apply_absolute_value=True), 1)
                V.wait_ge(ev.h, ev.n)
                ev.inc(V.tensor_scalar(out=scl[:, :], in0=gs[:, :],
                                       scalar1=1.0 / 64, scalar2=1e-8,
                                       op0=ALU.mult, op1=ALU.max), 1)
                V.wait_ge(ev.h, ev.n)
                ev.inc(V.reciprocal(out=gs[:, :], in_=scl[:, :]), 1)
                V.wait_ge(ev.h, ev.n)
                ev.inc(V.tensor_tensor(out=gv(r32[:, :]), in0=gv(wb[p][:, :]),
                                       in1=gs[:, :].to_broadcast((128, G_, 64)),
                                       op=ALU.mult), 1)
                SCA.wait_ge(ev.h, ev.n)
                SCA.wait_ge(sc.h, sc.n)
                sc.inc(SCA.activation(out=sgn[:, :], in_=r32[:, :], func=AF.Sign), 1)
                V.wait_ge(sc.h, sc.n)
                ev.inc(V.tensor_tensor(out=ra[:, :], in0=r32[:, :], in1=sgn[:, :],
                                       op=ALU.mult), 1)
                V.wait_ge(ev.h, ev.n)
                ev.inc(V.tensor_copy(ri[:, :], ra[:, :]), 1)
                V.wait_ge(ev.h, ev.n)
                ev.inc(V.tensor_tensor(out=fr[:, :], in0=ra[:, :], in1=ri[:, :],
                                       op=ALU.subtract), 1)
                ev.inc(V.tensor_scalar(out=hi_[:, :], in0=ri[:, :], scalar1=0.5,
                                       scalar2=None, op0=ALU.mult), 1)
                V.wait_ge(ev.h, ev.n)
                ev.inc(V.tensor_scalar(out=aa[:, :], in0=fr[:, :], scalar1=0.5,
                                       scalar2=None, op0=ALU.is_gt), 1)
                ev.inc(V.tensor_scalar(out=ee[:, :], in0=fr[:, :], scalar1=0.5,
                                       scalar2=None, op0=ALU.is_equal), 1)
                ev.inc(V.scalar_tensor_tensor(out=oo[:, :], in0=hi_[:, :], scalar=-2.0,
                                              in1=ri[:, :], op0=ALU.mult,
                                              op1=ALU.add), 1)
                V.wait_ge(ev.h, ev.n)
                ev.inc(V.tensor_tensor(out=ee[:, :], in0=ee[:, :], in1=oo[:, :],
                                       op=ALU.mult), 1)
                V.wait_ge(ev.h, ev.n)
                ev.inc(V.tensor_tensor(out=qa[:, :], in0=ri[:, :], in1=aa[:, :],
                                       op=ALU.add), 1)
                V.wait_ge(ev.h, ev.n)
                ev.inc(V.tensor_tensor(out=qa[:, :], in0=qa[:, :], in1=ee[:, :],
                                       op=ALU.add), 1)
                V.wait_ge(ev.h, ev.n)
                ev.inc(V.tensor_tensor(out=qa[:, :], in0=qa[:, :], in1=sgn[:, :],
                                       op=ALU.mult), 1)
                V.wait_ge(ev.h, ev.n)
                ev.inc(V.tensor_scalar(out=qa[:, :], in0=qa[:, :],
                                       scalar1=-1.0, scalar2=1.0,
                                       op0=ALU.max, op1=ALU.min), 1)
                V.wait_ge(ev.h, ev.n)
                ev.inc(V.tensor_tensor(out=gv(ob[p][:, :]), in0=gv(qa[:, :]),
                                       in1=bscl, op=ALU.mult), 1)
                used_ev[p] = ev.n
                SY.wait_ge(ev.h, ev.n)
                sto[p].inc(SY.dma_start(out=dst_ap, in_=ob[p][:, :]), 16)
            for p in range(2):
                G.wait_ge(sto[p].h, sto[p].n)

    return nc


def build_l1tail():
    nc = bass.Bass("TRN2", target_bir_lowering=False)
    avi = nc.dram_tensor("avi", [D, TPC], BF16, kind="ExternalInput").ap()
    x2i = nc.dram_tensor("x2i", [D, TPC], F32, kind="ExternalInput").ap()
    wo = nc.dram_tensor("wo", [D, D], F32, kind="ExternalInput").ap()
    guqT = nc.dram_tensor("guqT", [D, 8 * D], BF16, kind="ExternalInput").ap()
    wpqT = nc.dram_tensor("wpqT", [4 * D, D], BF16, kind="ExternalInput").ap()
    x4T = nc.dram_tensor("x4T", [D, TPC], F32, kind="ExternalOutput").ap()
    z_scr = nc.dram_tensor("z_scr", [D, TPC], F32).ap()
    x3_scr = nc.dram_tensor("x3_scr", [D, TPC], F32).ap()
    h_scr = nc.dram_tensor("h_scr", [4 * D, TPC], BF16).ap()
    z2_scr = nc.dram_tensor("z2_scr", [D, TPC], F32).ap()

    EPS1 = 1e-5
    EPSR = float(np.finfo(np.float32).eps)
    H4 = 4 * D // 128

    with ExitStack() as ctx:
        e = ctx.enter_context
        ones_c = e(nc.sbuf_tensor("ones_c", [128, 1], F32))
        onesb = e(nc.sbuf_tensor("onesb", [128, 1], BF16))
        ones_r = e(nc.sbuf_tensor("ones_r", [1, 128], F32))
        murow = e(nc.sbuf_tensor("murow", [1, TPC], F32))
        q2row = e(nc.sbuf_tensor("q2row", [1, TPC], F32))
        srow = e(nc.sbuf_tensor("srow", [1, TPC], F32))
        Asb = e(nc.sbuf_tensor("Asb", [128, TPC], F32))
        Bsb = e(nc.sbuf_tensor("Bsb", [128, TPC], F32))
        atile = [e(nc.sbuf_tensor(f"atile{p}", [128, TPC], F32)) for p in range(2)]
        ctile = [e(nc.sbuf_tensor(f"ctile{p}", [128, TPC], F32)) for p in range(2)]
        sqt = [e(nc.sbuf_tensor(f"sqt{p}", [128, TPC], BF16)) for p in range(2)]
        wbuf = [e(nc.sbuf_tensor(f"wbuf{p}", [128, KT, 128], BF16)) for p in range(2)]

        ld = SC(ctx, nc, "ld")
        ald = [SC(ctx, nc, f"ald{p}") for p in range(2)]
        bld = [SC(ctx, nc, f"bld{p}") for p in range(2)]
        wc = [SC(ctx, nc, f"wc{p}") for p in range(2)]
        mm = SC(ctx, nc, "mm")
        ev = SC(ctx, nc, "ev")
        sc = SC(ctx, nc, "sc")
        stz = [SC(ctx, nc, f"stz{p}") for p in range(2)]
        stx = [SC(ctx, nc, f"stx{p}") for p in range(2)]
        sth = [SC(ctx, nc, f"sth{p}") for p in range(2)]
        block = e(nc.Block())

        @block.gpsimd
        def _(g):
            T, V, SCA, G, SY = nc.tensor, nc.vector, nc.scalar, nc.gpsimd, nc.sync

            ev.inc(V.memset(ones_c[:, :], 1.0), 1)
            ev.inc(V.memset(onesb[:, :], 1.0), 1)
            ev.inc(V.memset(ones_r[:, :], 1.0), 1)
            ev_ones = ev.n

            def pbar():
                for eng in (SY, V, T, SCA, G):
                    eng.wait_ge(ev.h, ev.n)
                    eng.wait_ge(mm.h, mm.n)
                    eng.wait_ge(sc.h, sc.n)
                for eng in (SY, V, T, G):
                    for s_ in (stz, stx, sth, ald, bld, wc):
                        for p in range(2):
                            eng.wait_ge(s_[p].h, s_[p].n)

            def stats_to_AB(stats, statq, eps, nmean):
                V.wait_ge(mm.h, mm.n)
                for c in range(NCH):
                    ev.inc(V.tensor_scalar(out=murow[:, c * CH:(c + 1) * CH],
                                           in0=stats[c][:, :], scalar1=1.0 / nmean,
                                           scalar2=None, op0=ALU.mult), 1)
                    ev.inc(V.tensor_scalar(out=q2row[:, c * CH:(c + 1) * CH],
                                           in0=statq[c][:, :], scalar1=1.0 / nmean,
                                           scalar2=None, op0=ALU.mult), 1)
                V.wait_ge(ev.h, ev.n)
                ev.inc(V.tensor_tensor(out=Bsb[0:1, 0:TPC], in0=murow[:, :],
                                       in1=murow[:, :], op=ALU.mult), 1)
                V.wait_ge(ev.h, ev.n)
                ev.inc(V.scalar_tensor_tensor(out=q2row[:, :], in0=q2row[:, :],
                                              scalar=eps, in1=Bsb[0:1, 0:TPC],
                                              op0=ALU.add, op1=ALU.subtract), 1)
                SCA.wait_ge(ev.h, ev.n)
                SCA.wait_ge(sc.h, sc.n)
                sc.inc(SCA.activation(out=q2row[:, :], in_=q2row[:, :], func=AF.Sqrt), 1)
                V.wait_ge(sc.h, sc.n)
                ev.inc(V.reciprocal(out=q2row[:, :], in_=q2row[:, :]), 1)
                V.wait_ge(ev.h, ev.n)
                ev.inc(V.scalar_tensor_tensor(out=murow[:, :], in0=murow[:, :],
                                              scalar=-1.0, in1=q2row[:, :],
                                              op0=ALU.mult, op1=ALU.mult), 1)
                return ev.n

            def bcast_AB(bps):
                for c in range(NCH):
                    T.wait_ge(ev.h, ev.n)
                    T.matmul(bps[0][:, :], ones_r[:, :], q2row[:, c * CH:(c + 1) * CH],
                             start=True, stop=True)
                    ins = T.matmul(bps[1][:, :], ones_r[:, :],
                                   murow[:, c * CH:(c + 1) * CH], start=True, stop=True)
                    mm.inc(ins, 1)
                    V.wait_ge(mm.h, mm.n)
                    ev.inc(V.tensor_copy(Asb[:, c * CH:(c + 1) * CH], bps[0][:, :]), 1)
                    ev.inc(V.tensor_copy(Bsb[:, c * CH:(c + 1) * CH], bps[1][:, :]), 1)
                return ev.n

            with ExitStack() as cA:
                avsb = cA.enter_context(nc.sbuf_tensor("avsb", [128, KT, TPC], BF16))
                mps = [cA.enter_context(nc.psum_tensor(f"mps{i}", [128, CH], F32))
                       for i in range(2)]
                ld.inc(SY.dma_start(out=avsb[:, :, :], in_=_r16(avi)), 16)
                wcast = []
                for i in range(2):
                    wc[i % 2].inc(G.dma_start(out=wbuf[i % 2][:, :, :],
                                              in_=_r16(wo)[:, :, i * 128:(i + 1) * 128]), 16)
                    wcast.append((i % 2, wc[i % 2].n))
                T.wait_ge(ld.h, ld.n)
                T.wait_ge(ev.h, ev_ones)
                bank_ev = [0, 0]
                x2_used = [0, 0]
                for oc in range(KT):
                    p = oc % 2
                    wp = oc % 2
                    T.wait_ge(wc[wcast[oc][0]].h, wcast[oc][1])
                    if oc >= 2:
                        SY.wait_ge(ev.h, x2_used[p])
                        SY.wait_ge(stz[p].h, stz[p].n)
                        SY.wait_ge(ald[p].h, ald[p].n)
                    ald[p].inc(SY.dma_start(out=atile[p][:, :],
                                            in_=x2i[oc * 128:(oc + 1) * 128, :]), 16)
                    for c in range(NCH):
                        bk = (oc * NCH + c) % 2
                        if bank_ev[bk]:
                            T.wait_ge(ev.h, bank_ev[bk])
                        for kt in range(KT):
                            ins = T.matmul(mps[bk][:, :], wbuf[wp][:, kt, :],
                                           avsb[:, kt, c * CH:(c + 1) * CH],
                                           start=(kt == 0), stop=(kt == KT - 1))
                        mm.inc(ins, 1)
                        V.wait_ge(mm.h, mm.n)
                        V.wait_ge(ald[p].h, ald[p].n)
                        if oc >= 2 and c == 0:
                            V.wait_ge(stz[p].h, stz[p].n)
                        ev.inc(V.tensor_tensor(out=ctile[p][:, c * CH:(c + 1) * CH],
                                               in0=mps[bk][:, :],
                                               in1=atile[p][:, c * CH:(c + 1) * CH],
                                               op=ALU.add), 1)
                        bank_ev[bk] = ev.n
                    x2_used[p] = ev.n
                    ev_z = ev.n
                    SY.wait_ge(ev.h, ev_z)
                    stz[p].inc(SY.dma_start(out=z_scr[oc * 128:(oc + 1) * 128, :],
                                            in_=ctile[p][:, :]), 16)
                    if oc + 2 < KT:
                        G.wait_ge(mm.h, mm.n)
                        wc[oc % 2].inc(G.dma_start(
                            out=wbuf[oc % 2][:, :, :],
                            in_=_r16(wo)[:, :, (oc + 2) * 128:(oc + 3) * 128]), 16)
                        wcast.append((oc % 2, wc[oc % 2].n))

            pbar()
            with ExitStack() as cS1:
                s1 = [cS1.enter_context(nc.psum_tensor(f"s1_{c}", [1, CH], F32))
                      for c in range(NCH)]
                q1 = [cS1.enter_context(nc.psum_tensor(f"q1_{c}", [1, CH], F32))
                      for c in range(NCH)]
                zu1 = [[0, 0], [0, 0]]
                for j in range(KT):
                    p = j % 2
                    if j >= 2:
                        SY.wait_ge(mm.h, zu1[p][0])
                        SY.wait_ge(sc.h, zu1[p][1])
                        SY.wait_ge(ald[p].h, ald[p].n)
                    ald[p].inc(SY.dma_start(out=atile[p][:, :],
                                            in_=z_scr[j * 128:(j + 1) * 128, :]), 16)
                    SCA.wait_ge(ald[p].h, ald[p].n)
                    SCA.wait_ge(sc.h, sc.n)
                    if j >= 2:
                        SCA.wait_ge(mm.h, zu1[p][0])
                    sc.inc(SCA.activation(out=sqt[p][:, :], in_=atile[p][:, :],
                                          func=AF.Square), 1)
                    T.wait_ge(ald[p].h, ald[p].n)
                    T.wait_ge(sc.h, sc.n)
                    for c in range(NCH):
                        T.matmul(s1[c][:, :], ones_c[:, :],
                                 atile[p][:, c * CH:(c + 1) * CH],
                                 start=(j == 0), stop=(j == KT - 1))
                        ins = T.matmul(q1[c][:, :], onesb[:, :],
                                       sqt[p][:, c * CH:(c + 1) * CH],
                                       start=(j == 0), stop=(j == KT - 1))
                    mm.inc(ins, 1)
                    zu1[p] = [mm.n, sc.n]
                stats_to_AB(s1, q1, EPS1, D)
            with nc.psum_tensor("bpA", [128, CH], F32) as b0, \
                 nc.psum_tensor("bpB", [128, CH], F32) as b1:
                ev_ab1 = bcast_AB([b0, b1])

            pbar()
            with ExitStack() as cB:
                x3bf = cB.enter_context(nc.sbuf_tensor("x3bf", [128, KT, TPC], BF16))
                x3_used = [0, 0]
                for j in range(KT):
                    p = j % 2
                    if j >= 2:
                        SY.wait_ge(ev.h, x3_used[p])
                        SY.wait_ge(stx[p].h, stx[p].n)
                        SY.wait_ge(ald[p].h, ald[p].n)
                    ald[p].inc(SY.dma_start(out=atile[p][:, :],
                                            in_=z_scr[j * 128:(j + 1) * 128, :]), 16)
                    V.wait_ge(ald[p].h, ald[p].n)
                    V.wait_ge(ev.h, ev_ab1)
                    if j >= 2:
                        V.wait_ge(stx[p].h, stx[p].n)
                    ev.inc(V.tensor_tensor(out=ctile[p][:, :], in0=atile[p][:, :],
                                           in1=Asb[:, :], op=ALU.mult), 1)
                    V.wait_ge(ev.h, ev.n)
                    ev.inc(V.tensor_tensor(out=ctile[p][:, :], in0=ctile[p][:, :],
                                           in1=Bsb[:, :], op=ALU.add), 1)
                    V.wait_ge(ev.h, ev.n)
                    ev.inc(V.tensor_copy(x3bf[:, j, :], ctile[p][:, :]), 1)
                    x3_used[p] = ev.n
                    SY.wait_ge(ev.h, ev.n)
                    stx[p].inc(SY.dma_start(out=x3_scr[j * 128:(j + 1) * 128, :],
                                            in_=ctile[p][:, :]), 16)

                pbar()
                with ExitStack() as cC:
                    s2 = [cC.enter_context(nc.psum_tensor(f"s2_{c}", [1, CH], F32))
                          for c in range(NCH)]
                    gp_ = [cC.enter_context(nc.psum_tensor(f"gp{i}", [128, CH], F32))
                           for i in range(4)]
                    hst = [cC.enter_context(nc.sbuf_tensor(f"hst{i}", [128, TPC], BF16))
                           for i in range(2)]
                    gf = cC.enter_context(nc.sbuf_tensor("gf", [128, CH], F32))
                    hsq2 = cC.enter_context(nc.sbuf_tensor("hsq2", [128, CH], BF16))
                    wg = [cC.enter_context(nc.sbuf_tensor(f"wg{i}", [128, KT, 128], BF16))
                          for i in range(4)]
                    wgc = [SC(ctx, nc, f"wgc{i}") for i in range(4)]
                    wgcast = []
                    for i in range(2):
                        G.wait_ge(mm.h, mm.n)
                        wgc[2 * (i % 2)].inc(G.dma_start(
                            out=wg[2 * (i % 2)][:, :, :],
                            in_=_r16(guqT)[:, :, i * 128:(i + 1) * 128]), 16)
                        wgc[2 * (i % 2) + 1].inc(G.dma_start(
                            out=wg[2 * (i % 2) + 1][:, :, :],
                            in_=_r16(guqT)[:, :, (i + 64) * 128:(i + 65) * 128]), 16)
                        wgcast.append((2 * (i % 2), wgc[2 * (i % 2)].n,
                                       wgc[2 * (i % 2) + 1].n))
                    bank_sc2 = [0, 0]
                    h_used_mm = [0, 0]
                    for ht in range(H4):
                        wpi, wv0, wv1 = wgcast[ht]
                        T.wait_ge(wgc[wpi].h, wv0)
                        T.wait_ge(wgc[wpi + 1].h, wv1)
                        p = ht % 2
                        if ht >= 2:
                            V.wait_ge(sth[p].h, sth[p].n)
                        for c in range(NCH):
                            bk = 2 * (c % 2)
                            if bank_sc2[c % 2]:
                                T.wait_ge(ev.h, bank_sc2[c % 2])
                            for kt in range(KT):
                                T.matmul(gp_[bk][:, :], wg[wpi][:, kt, :],
                                         x3bf[:, kt, c * CH:(c + 1) * CH],
                                         start=(kt == 0), stop=(kt == KT - 1))
                                ins = T.matmul(gp_[bk + 1][:, :], wg[wpi + 1][:, kt, :],
                                               x3bf[:, kt, c * CH:(c + 1) * CH],
                                               start=(kt == 0), stop=(kt == KT - 1))
                            mm.inc(ins, 2)
                            SCA.wait_ge(mm.h, mm.n)
                            SCA.wait_ge(sc.h, sc.n)
                            sc.inc(SCA.activation(out=gf[:, :], in_=gp_[bk][:, :],
                                                  func=AF.Silu), 1)
                            V.wait_ge(sc.h, sc.n)
                            if ht >= 2 and c == 0:
                                V.wait_ge(mm.h, h_used_mm[p])
                            ev.inc(V.tensor_tensor(out=hst[p][:, c * CH:(c + 1) * CH],
                                                   in0=gf[:, :], in1=gp_[bk + 1][:, :],
                                                   op=ALU.mult), 1)
                            ev_h = ev.n
                            SCA.wait_ge(ev.h, ev_h)
                            SCA.wait_ge(sc.h, sc.n)
                            sc.inc(SCA.activation(out=hsq2[:, :],
                                                  in_=hst[p][:, c * CH:(c + 1) * CH],
                                                  func=AF.Square), 1)
                            T.wait_ge(sc.h, sc.n)
                            ins = T.matmul(s2[c][:, :], onesb[:, :], hsq2[:, :],
                                           start=(ht == 0), stop=(ht == H4 - 1))
                            mm.inc(ins, 1)
                            bank_sc2[c % 2] = ev_h
                        h_used_mm[p] = mm.n
                        SY.wait_ge(ev.h, ev.n)
                        sth[p].inc(SY.dma_start(out=h_scr[ht * 128:(ht + 1) * 128, :],
                                                in_=hst[p][:, :]), 16)
                        if ht + 2 < H4:
                            G.wait_ge(mm.h, mm.n)
                            i2 = ht + 2
                            wgc[2 * (i2 % 2)].inc(G.dma_start(
                                out=wg[2 * (i2 % 2)][:, :, :],
                                in_=_r16(guqT)[:, :, i2 * 128:(i2 + 1) * 128]), 16)
                            wgc[2 * (i2 % 2) + 1].inc(G.dma_start(
                                out=wg[2 * (i2 % 2) + 1][:, :, :],
                                in_=_r16(guqT)[:, :, (i2 + 64) * 128:(i2 + 65) * 128]), 16)
                            wgcast.append((2 * (i2 % 2), wgc[2 * (i2 % 2)].n,
                                           wgc[2 * (i2 % 2) + 1].n))
                    V.wait_ge(mm.h, mm.n)
                    for c in range(NCH):
                        ev.inc(V.tensor_scalar(out=srow[:, c * CH:(c + 1) * CH],
                                               in0=s2[c][:, :], scalar1=1.0 / (4 * D),
                                               scalar2=EPSR, op0=ALU.mult,
                                               op1=ALU.add), 1)
                    SCA.wait_ge(ev.h, ev.n)
                    SCA.wait_ge(sc.h, sc.n)
                    sc.inc(SCA.activation(out=srow[:, :], in_=srow[:, :], func=AF.Sqrt), 1)
                    V.wait_ge(sc.h, sc.n)
                    ev.inc(V.reciprocal(out=srow[:, :], in_=srow[:, :]), 1)

            pbar()
            PG = TPC // 4
            NSUB = PG // CH
            with ExitStack() as cD:
                hch = cD.enter_context(nc.sbuf_tensor("hch", [128, H4, PG], BF16))
                wp_ = [cD.enter_context(nc.sbuf_tensor(f"wp{i}", [128, H4, 128], BF16))
                       for i in range(2)]
                sbb = cD.enter_context(nc.sbuf_tensor("sbb", [128, PG], F32))
                z2sb = [cD.enter_context(nc.sbuf_tensor(f"z2sb{i}", [128, PG], F32))
                        for i in range(2)]
                pps = [cD.enter_context(nc.psum_tensor(f"pps{i}", [128, CH], F32))
                       for i in range(2)]
                bps2 = [cD.enter_context(nc.psum_tensor(f"bps2_{i}", [128, CH], F32))
                        for i in range(2)]
                hl = SC(ctx, nc, "hl")
                wpc = [SC(ctx, nc, f"wpc{i}") for i in range(2)]
                z2_used = [0, 0]
                for grp_i in range(4):
                    tk0 = grp_i * PG
                    pbar()
                    hl.inc(SY.dma_start(
                        out=hch[:, :, :],
                        in_=_r16(h_scr)[:, :, tk0:tk0 + PG]), 16)
                    T.wait_ge(ev.h, ev.n)
                    for sub in range(NSUB):
                        ins = T.matmul(bps2[sub][:, :], ones_r[:, :],
                                       srow[:, tk0 + sub * CH:tk0 + (sub + 1) * CH],
                                       start=True, stop=True)
                    mm.inc(ins, 1)
                    V.wait_ge(mm.h, mm.n)
                    for sub in range(NSUB):
                        ev.inc(V.tensor_copy(sbb[:, sub * CH:(sub + 1) * CH],
                                             bps2[sub][:, :]), 1)
                    ev_sbb = ev.n
                    wpcast = []
                    for i in range(2):
                        G.wait_ge(mm.h, mm.n)
                        wpc[i].inc(G.dma_start(
                            out=wp_[i][:, :, :],
                            in_=_r16(wpqT)[:, :, i * 128:(i + 1) * 128]), 16)
                        wpcast.append(wpc[i].n)
                    T.wait_ge(hl.h, hl.n)
                    bank_ev3 = [0, 0]
                    for oc in range(KT):
                        p = oc % 2
                        T.wait_ge(wpc[p].h, wpcast[oc])
                        if oc >= 2:
                            SY.wait_ge(ev.h, z2_used[p])
                            SY.wait_ge(stz[p].h, stz[p].n)
                            SY.wait_ge(bld[p].h, bld[p].n)
                        bld[p].inc(SY.dma_start(
                            out=atile[p][:, 0:PG],
                            in_=x3_scr[oc * 128:(oc + 1) * 128, tk0:tk0 + PG]), 16)
                        for sub in range(NSUB):
                            bk = sub
                            if bank_ev3[bk]:
                                T.wait_ge(ev.h, bank_ev3[bk])
                            for kt in range(H4):
                                ins = T.matmul(pps[bk][:, :], wp_[p][:, kt, :],
                                               hch[:, kt, sub * CH:(sub + 1) * CH],
                                               start=(kt == 0), stop=(kt == H4 - 1))
                            mm.inc(ins, 1)
                            V.wait_ge(mm.h, mm.n)
                            V.wait_ge(ev.h, ev_sbb)
                            V.wait_ge(bld[p].h, bld[p].n)
                            if oc >= 2 and sub == 0:
                                V.wait_ge(stz[p].h, stz[p].n)
                            ev.inc(V.tensor_tensor(out=z2sb[p][:, sub * CH:(sub + 1) * CH],
                                                   in0=pps[bk][:, :],
                                                   in1=sbb[:, sub * CH:(sub + 1) * CH],
                                                   op=ALU.mult), 1)
                            V.wait_ge(ev.h, ev.n)
                            ev.inc(V.tensor_tensor(out=z2sb[p][:, sub * CH:(sub + 1) * CH],
                                                   in0=z2sb[p][:, sub * CH:(sub + 1) * CH],
                                                   in1=atile[p][:, sub * CH:(sub + 1) * CH],
                                                   op=ALU.add), 1)
                            bank_ev3[bk] = ev.n
                        z2_used[p] = ev.n
                        SY.wait_ge(ev.h, ev.n)
                        stz[p].inc(SY.dma_start(
                            out=z2_scr[oc * 128:(oc + 1) * 128, tk0:tk0 + PG],
                            in_=z2sb[p][:, :]), 16)
                        if oc + 2 < KT:
                            G.wait_ge(mm.h, mm.n)
                            wpc[p].inc(G.dma_start(
                                out=wp_[p][:, :, :],
                                in_=_r16(wpqT)[:, :, (oc + 2) * 128:(oc + 3) * 128]), 16)
                            wpcast.append(wpc[p].n)

            pbar()
            with ExitStack() as cE:
                s3 = [cE.enter_context(nc.psum_tensor(f"s3_{c}", [1, CH], F32))
                      for c in range(NCH)]
                q3 = [cE.enter_context(nc.psum_tensor(f"q3_{c}", [1, CH], F32))
                      for c in range(NCH)]
                zu = [[0, 0], [0, 0]]
                for j in range(KT):
                    p = j % 2
                    if j >= 2:
                        SY.wait_ge(mm.h, zu[p][0])
                        SY.wait_ge(sc.h, zu[p][1])
                        SY.wait_ge(ald[p].h, ald[p].n)
                    ald[p].inc(SY.dma_start(out=atile[p][:, :],
                                            in_=z2_scr[j * 128:(j + 1) * 128, :]), 16)
                    SCA.wait_ge(ald[p].h, ald[p].n)
                    SCA.wait_ge(sc.h, sc.n)
                    if j >= 2:
                        SCA.wait_ge(mm.h, zu[p][0])
                    sc.inc(SCA.activation(out=sqt[p][:, :], in_=atile[p][:, :],
                                          func=AF.Square), 1)
                    T.wait_ge(ald[p].h, ald[p].n)
                    T.wait_ge(sc.h, sc.n)
                    for c in range(NCH):
                        T.matmul(s3[c][:, :], ones_c[:, :],
                                 atile[p][:, c * CH:(c + 1) * CH],
                                 start=(j == 0), stop=(j == KT - 1))
                        ins = T.matmul(q3[c][:, :], onesb[:, :],
                                       sqt[p][:, c * CH:(c + 1) * CH],
                                       start=(j == 0), stop=(j == KT - 1))
                    mm.inc(ins, 1)
                    zu[p] = [mm.n, sc.n]
                stats_to_AB(s3, q3, EPS1, D)
            with nc.psum_tensor("bpC", [128, CH], F32) as b0, \
                 nc.psum_tensor("bpD", [128, CH], F32) as b1:
                ev_ab2 = bcast_AB([b0, b1])
            pbar()
            x4u = [0, 0]
            for j in range(KT):
                p = j % 2
                if j >= 2:
                    SY.wait_ge(ev.h, x4u[p])
                    SY.wait_ge(stx[p].h, stx[p].n)
                    SY.wait_ge(ald[p].h, ald[p].n)
                ald[p].inc(SY.dma_start(out=atile[p][:, :],
                                        in_=z2_scr[j * 128:(j + 1) * 128, :]), 16)
                V.wait_ge(ald[p].h, ald[p].n)
                V.wait_ge(ev.h, ev_ab2)
                if j >= 2:
                    V.wait_ge(stx[p].h, stx[p].n)
                ev.inc(V.tensor_tensor(out=ctile[p][:, :], in0=atile[p][:, :],
                                       in1=Asb[:, :], op=ALU.mult), 1)
                V.wait_ge(ev.h, ev.n)
                ev.inc(V.tensor_tensor(out=ctile[p][:, :], in0=ctile[p][:, :],
                                       in1=Bsb[:, :], op=ALU.add), 1)
                x4u[p] = ev.n
                SY.wait_ge(ev.h, ev.n)
                stx[p].inc(SY.dma_start(out=x4T[j * 128:(j + 1) * 128, :],
                                        in_=ctile[p][:, :]), 16)
            for p in range(2):
                G.wait_ge(stx[p].h, stx[p].n)
                G.wait_ge(stz[p].h, stz[p].n)
                G.wait_ge(sth[p].h, sth[p].n)

    return nc


def _run(nc, in_maps):
    res = run_bass_kernel_spmd(nc, in_maps, core_ids=list(range(NCORE)))
    return res.results


def kernel(**inputs):
    ii = {k: np.asarray(v) for k, v in inputs.items()}

    x = ii["L0_attn_x"].astype(np.float32, copy=False)
    xT = np.ascontiguousarray(x.transpose(2, 0, 1).reshape(D, 2 * S))

    def tok_shard(full):
        out = []
        for c in range(NCORE):
            out.append(np.ascontiguousarray(np.concatenate(
                [full[:, c * HB:(c + 1) * HB],
                 full[:, S + c * HB:S + (c + 1) * HB]], axis=1)))
        return out

    def tok_assemble(shards, dtype):
        full = np.empty((shards[0].shape[0], 2 * S), dtype=dtype)
        for c in range(NCORE):
            full[:, c * HB:(c + 1) * HB] = shards[c][:, :HB]
            full[:, S + c * HB:S + (c + 1) * HB] = shards[c][:, HB:]
        return full

    xs = tok_shard(xT)
    row = lambda v: np.ascontiguousarray(v.astype(np.float32, copy=False).reshape(1, -1))

    wgu = ii["L1_ffn_w_gate_up"].astype(np.float32, copy=False)
    wpj = ii["L1_ffn_w_proj"].astype(np.float32, copy=False)
    r0 = _run(build_quant(), [
        {"gur": np.ascontiguousarray(wgu[c * 2048:(c + 1) * 2048, :]),
         "wpr": np.ascontiguousarray(wpj[c * 256:(c + 1) * 256, :])}
        for c in range(NCORE)])
    guq = np.concatenate([np.asarray(r0[c]["guq"]) for c in range(NCORE)], axis=0)
    wpq = np.concatenate([np.asarray(r0[c]["wpq"]) for c in range(NCORE)], axis=0)
    guqT = np.ascontiguousarray(guq.T)
    wpqT = np.ascontiguousarray(wpq.T)

    w = np.ascontiguousarray(ii["L0_attn_W"].astype(np.float32, copy=False))
    bv = row(ii["L0_attn_b"])
    r1 = _run(build_l0qkv(), [{"xs": xs[c], "w": w, "bvec": bv} for c in range(NCORE)])
    parts = np.ascontiguousarray(np.stack([np.asarray(r1[c]["part"]) for c in range(NCORE)]))

    gw = np.ascontiguousarray(ii["L0_ffn_gate_w"].astype(np.float32, copy=False))
    r2 = _run(build_l0x1(), [
        {"xs": xs[c], "qTi": np.asarray(r1[c]["qT"]), "parts": parts, "gw": gw,
         "gb": row(ii["L0_ffn_gate_b"]), "ing": row(ii["L0_ffn_in_gamma"]),
         "inb": row(ii["L0_ffn_in_beta"])} for c in range(NCORE)])

    xnl = np.ascontiguousarray(np.asarray(r2[NCORE - 1]["xnl"]))
    d1w = ii["L0_ffn_d1_w"].astype(np.float32, copy=False)
    d2w = ii["L0_ffn_d2_w"].astype(np.float32, copy=False)
    png_, pnb_ = row(ii["L0_ffn_pn_gamma"]), row(ii["L0_ffn_pn_beta"])
    d1b_ = row(ii["L0_ffn_d1_b"])
    d2b_ = row(ii["L0_ffn_d2_b"])
    tmaps = [{"png": png_, "pnb": pnb_,
              "d1ws": np.ascontiguousarray(d1w[:, c * 1024:(c + 1) * 1024]),
              "d1bs": np.ascontiguousarray(d1b_[:, c * 1024:(c + 1) * 1024]),
              "d2ws": np.ascontiguousarray(d2w[c * 1024:(c + 1) * 1024, :])}
             for c in range(NCORE)]

    def pack_upds(rs):
        a = np.stack([np.asarray(rs[c]["updp"]) for c in range(NCORE)])
        return np.ascontiguousarray(
            a.reshape(NCORE, KT, 128, 2).transpose(2, 0, 1, 3).reshape(128, NCORE * KT * 2))

    nc_first = build_tstep(True)
    nc_next = build_tstep(False)
    rt = _run(nc_first, [
        {**tmaps[c], "xnli": xnl,
         "aggw": np.ascontiguousarray(ii["L0_ffn_agg_w"].astype(np.float32, copy=False)),
         "aggb": row(ii["L0_ffn_agg_b"])} for c in range(NCORE)])
    for _step in range(3):
        thv = np.ascontiguousarray(np.asarray(rt[0]["thv"]))
        rt = _run(nc_next, [
            {**tmaps[c], "thpi": thv, "updsi": pack_upds(rt), "d2br": d2b_}
            for c in range(NCORE)])

    thv = np.ascontiguousarray(np.asarray(rt[0]["thv"]))
    upds = pack_upds(rt)
    r4 = _run(build_l0tail(), [
        {"x1i": np.asarray(r2[c]["x1T"]), "gti": np.asarray(r2[c]["gateT"]),
         "thpi": thv, "updsi": upds, "d2br": d2b_,
         "bcw": np.ascontiguousarray(ii["L0_ffn_bc_w"].astype(np.float32, copy=False)),
         "bcbr": row(ii["L0_ffn_bc_b"]),
         "outg": row(ii["L0_ffn_out_gamma"]),
         "outb": row(ii["L0_ffn_out_beta"])} for c in range(NCORE)])
    x2bf_full = tok_assemble([np.asarray(r4[c]["x2bf"]) for c in range(NCORE)], BF)

    wq = ii["L1_attn_Wq"].astype(np.float32, copy=False)
    wk = ii["L1_attn_Wk"].astype(np.float32, copy=False)
    wv = ii["L1_attn_Wv"].astype(np.float32, copy=False)
    wqi = ii["L1_attn_Wq_ind"].astype(np.float32, copy=False)
    wki = ii["L1_attn_Wk_ind"].astype(np.float32, copy=False)
    hw = ii["L1_attn_head_weights"].astype(np.float32, copy=False).reshape(-1)
    temp = ii["L1_attn_temperature_param"].astype(np.float32, copy=False).reshape(1, 1)
    r5 = _run(build_l1a(), [
        {"x2f": x2bf_full,
         "wvs": np.ascontiguousarray(wv[:, c * 256:(c + 1) * 256]),
         "wqss": np.ascontiguousarray(wq[:, c * 256:(c + 1) * 256]),
         "wkss": np.ascontiguousarray(wk[:, c * 256:(c + 1) * 256])}
        for c in range(NCORE)])

    qlf = np.concatenate([np.asarray(r5[c]["qTs"]) for c in range(NCORE)], axis=0)
    klf = np.concatenate([np.asarray(r5[c]["kTs"]) for c in range(NCORE)], axis=0)
    r5b = _run(build_l1ind(), [
        {"qlf": qlf, "klf": klf,
         "wqis": np.ascontiguousarray(wqi[:, c * 256:(c + 1) * 256]),
         "wkis": np.ascontiguousarray(wki[:, c * 256:(c + 1) * 256]),
         "hwme": np.ascontiguousarray(hw[c].reshape(1, 1)),
         "temp": np.ascontiguousarray(temp)} for c in range(NCORE)])
    relqs = np.ascontiguousarray(np.stack([np.asarray(r5b[c]["relq"]) for c in range(NCORE)]))

    r6 = _run(build_l1b(), [
        {"relqs": relqs, "qTs": np.asarray(r5[c]["qTs"]),
         "kTs": np.asarray(r5[c]["kTs"]), "vs": np.asarray(r5[c]["vs"])}
        for c in range(NCORE)])
    avn_full = np.concatenate([np.asarray(r6[c]["avnT"]) for c in range(NCORE)], axis=0)
    avi = tok_shard(avn_full)

    wo = np.ascontiguousarray(ii["L1_attn_Wo"].astype(np.float32, copy=False))
    r7 = _run(build_l1tail(), [
        {"avi": avi[c], "x2i": np.asarray(r4[c]["x2T"]), "wo": wo,
         "guqT": guqT, "wpqT": wpqT} for c in range(NCORE)])

    outT = tok_assemble([np.asarray(r7[c]["x4T"]) for c in range(NCORE)], np.float32)
    return np.ascontiguousarray(outT.reshape(D, 2, S).transpose(1, 2, 0))


# version 05
# speedup vs baseline: 1.1750x; 1.1750x over previous
import numpy as np
import ml_dtypes
from contextlib import ExitStack

import concourse.bass as bass
import concourse.mybir as mybir
from concourse.bass_utils import run_bass_kernel_spmd

F32 = mybir.dt.float32
BF16 = mybir.dt.bfloat16
I32 = mybir.dt.int32
AF = mybir.ActivationFunctionType
ALU = mybir.AluOpType
AX = mybir.AxisListType

D = 2048
S = 8192
B = 2
NCORE = 8
TPC = 2 * S // NCORE
HB = TPC // 2
KT = D // 128
CH = 512
NCH = TPC // CH

BF = ml_dtypes.bfloat16


class SC:

    def __init__(self, ctx, nc, name):
        self.h = ctx.enter_context(nc.semaphore(name))
        self.n = 0

    def inc(self, inst, k=1):
        inst.then_inc(self.h, k)
        self.n += k
        return self.n


def _r16(ap):
    return ap.rearrange("(kt p) t -> p kt t", p=128)


def build_l0qkv():
    nc = bass.Bass("TRN2", target_bir_lowering=False)
    xs = nc.dram_tensor("xs", [D, TPC], F32, kind="ExternalInput").ap()
    w = nc.dram_tensor("w", [D, 3 * D], F32, kind="ExternalInput").ap()
    bvec = nc.dram_tensor("bvec", [1, 3 * D], F32, kind="ExternalInput").ap()
    qT = nc.dram_tensor("qT", [D, TPC], F32, kind="ExternalOutput").ap()
    part = nc.dram_tensor("part", [128, 96], F32, kind="ExternalOutput").ap()

    with ExitStack() as ctx:
        e = ctx.enter_context
        xbf = e(nc.sbuf_tensor("xbf", [128, KT, TPC], BF16))
        wbuf = [e(nc.sbuf_tensor(f"wbuf{p}", [128, KT, 128], BF16)) for p in range(2)]
        brow = e(nc.sbuf_tensor("brow", [1, 3 * D], F32))
        bcol = e(nc.sbuf_tensor("bcol", [128, 48], F32))
        one_sb = e(nc.sbuf_tensor("one_sb", [1, 1], F32))
        qsb = [e(nc.sbuf_tensor(f"qsb{p}", [128, TPC], F32)) for p in range(2)]
        ksb = e(nc.sbuf_tensor("ksb", [128, TPC], F32))
        dumpS = e(nc.sbuf_tensor("dumpS", [128, CH], F32))
        dumpV = e(nc.sbuf_tensor("dumpV", [128, CH], F32))
        sq_acc = e(nc.sbuf_tensor("sq_acc", [128, 64], F32))
        sk_acc = e(nc.sbuf_tensor("sk_acc", [128, 64], F32))
        skv_acc = e(nc.sbuf_tensor("skv_acc", [128, 64], F32))
        part_sb = e(nc.sbuf_tensor("part_sb", [128, 96], F32))
        mmps = [e(nc.psum_tensor(f"mmps{p}", [128, CH], F32)) for p in range(2)]
        bps = e(nc.psum_tensor("bps", [128, 48], F32))

        ld = SC(ctx, nc, "ld")
        xld = SC(ctx, nc, "xld")
        wc = [SC(ctx, nc, "wc0"), SC(ctx, nc, "wc1")]
        mm = SC(ctx, nc, "mm")
        ev = SC(ctx, nc, "ev")
        sc = SC(ctx, nc, "sc")
        st = [SC(ctx, nc, "st0"), SC(ctx, nc, "st1")]
        stp = SC(ctx, nc, "stp")
        block = e(nc.Block())

        ocs = list(range(16))
        for j in range(16):
            ocs += [16 + j, 32 + j]

        @block.gpsimd
        def _(g):
            T, V, SCA, G, SY = nc.tensor, nc.vector, nc.scalar, nc.gpsimd, nc.sync

            ld.inc(SY.dma_start(out=brow[:, :], in_=bvec[:, :]), 16)
            xld.inc(G.dma_start(out=xbf[:, :, :], in_=_r16(xs)), 16)
            ev.inc(V.memset(one_sb[:, :], 1.0), 1)
            ev_ones = ev.n

            wcast_at = []
            for i in range(2):
                oc = ocs[i]
                wc[i % 2].inc(G.dma_start(out=wbuf[i % 2][:, :, :],
                                          in_=_r16(w)[:, :, oc * 128:(oc + 1) * 128]), 16)
                wcast_at.append((i % 2, wc[i % 2].n))

            T.wait_ge(ld.h, 16)
            T.wait_ge(ev.h, ev_ones)
            for j in range(48):
                ins = T.matmul(bps[:, j:j + 1], brow[:, j * 128:(j + 1) * 128],
                               one_sb[:, :], start=True, stop=True)
            mm.inc(ins, 1)
            mm_bias = mm.n
            V.wait_ge(mm.h, mm_bias)
            ev.inc(V.tensor_copy(bcol[:, :], bps[:, :]), 1)
            ev_bias = ev.n
            V.wait_ge(ev.h, ev_bias)

            bank_free_ev = [0, 0]
            qsb_free = [(0, 0)] * 2
            ksb_free_sc = 0
            n_q_sq = 0
            n_k_sq = 0

            T.wait_ge(xld.h, 16)

            for i, oc in enumerate(ocs):
                wp = i % 2
                T.wait_ge(wc[wcast_at[i][0]].h, wcast_at[i][1])
                is_q = oc < 16
                is_k = 16 <= oc < 32
                is_v = oc >= 32
                j = oc - 16 if is_k else (oc - 32 if is_v else oc)

                if is_q:
                    need_st, need_sc = qsb_free[oc % 2]
                    if need_st or need_sc:
                        V.wait_ge(st[oc % 2].h, need_st)
                        V.wait_ge(sc.h, need_sc)
                if is_k and j > 0:
                    V.wait_ge(sc.h, ksb_free_sc)
                if is_k:
                    ksb_w_ev = [0] * NCH

                for c in range(NCH):
                    bk = (i * NCH + c) % 2
                    if bank_free_ev[bk]:
                        T.wait_ge(ev.h, bank_free_ev[bk])
                    for kt in range(KT):
                        ins = T.matmul(mmps[bk][:, :], wbuf[wp][:, kt, :],
                                       xbf[:, kt, c * CH:(c + 1) * CH],
                                       start=(kt == 0), stop=(kt == KT - 1))
                    mm.inc(ins, 1)
                    V.wait_ge(mm.h, mm.n)
                    b01 = c // 2
                    if is_q:
                        ev.inc(V.tensor_scalar(
                            out=qsb[oc % 2][:, c * CH:(c + 1) * CH], in0=mmps[bk][:, :],
                            scalar1=bcol[:, oc:oc + 1], scalar2=None, op0=ALU.add), 1)
                        bank_free_ev[bk] = ev.n
                        SCA.wait_ge(ev.h, ev.n)
                        SCA.wait_ge(sc.h, sc.n)
                        sc.inc(SCA.activation(
                            out=dumpS[:, :], in_=qsb[oc % 2][:, c * CH:(c + 1) * CH],
                            func=AF.Square, accum_out=sq_acc[:, oc * 4 + c:oc * 4 + c + 1]), 1)
                        n_q_sq += 1
                    elif is_k:
                        ev.inc(V.tensor_scalar(
                            out=ksb[:, c * CH:(c + 1) * CH], in0=mmps[bk][:, :],
                            scalar1=bcol[:, oc:oc + 1], scalar2=None, op0=ALU.add), 1)
                        bank_free_ev[bk] = ev.n
                        ksb_w_ev[c] = ev.n
                        SCA.wait_ge(ev.h, ev.n)
                        SCA.wait_ge(sc.h, sc.n)
                        sc.inc(SCA.activation(
                            out=dumpS[:, :], in_=ksb[:, c * CH:(c + 1) * CH],
                            func=AF.Square, accum_out=sk_acc[:, j * 4 + c:j * 4 + c + 1]), 1)
                        n_k_sq += 1
                    else:
                        V.wait_ge(ev.h, ksb_w_ev[c])
                        V.wait_ge(ev.h, ev.n)
                        ev.inc(V.scalar_tensor_tensor(
                            out=dumpV[:, :], in0=mmps[bk][:, :],
                            scalar=bcol[:, oc:oc + 1], in1=ksb[:, c * CH:(c + 1) * CH],
                            op0=ALU.add, op1=ALU.mult,
                            accum_out=skv_acc[:, j * 4 + c:j * 4 + c + 1]), 1)
                        bank_free_ev[bk] = ev.n

                if is_q:
                    SY.wait_ge(ev.h, ev.n)
                    st[oc % 2].inc(SY.dma_start(out=qT[oc * 128:(oc + 1) * 128, :],
                                                in_=qsb[oc % 2][:, :]), 16)
                    qsb_free[oc % 2] = (st[oc % 2].n, sc.n)
                if is_v:
                    ksb_free_sc = 64 + 4 * (j + 1)

                if i + 2 < len(ocs):
                    ocn = ocs[i + 2]
                    G.wait_ge(mm.h, mm.n)
                    wc[i % 2].inc(G.dma_start(out=wbuf[(i + 2) % 2][:, :, :],
                                              in_=_r16(w)[:, :, ocn * 128:(ocn + 1) * 128]), 16)
                    wcast_at.append((i % 2, wc[i % 2].n))

            V.wait_ge(sc.h, sc.n)
            V.wait_ge(ev.h, ev.n)
            grp = lambda t: t.rearrange("p (j b two) -> p (j b) two", two=2, b=2)
            ev.inc(V.tensor_reduce(out=part_sb[:, 0:32], in_=grp(sq_acc),
                                   axis=AX.X, op=ALU.add), 1)
            ev.inc(V.tensor_reduce(out=part_sb[:, 32:64], in_=grp(sk_acc),
                                   axis=AX.X, op=ALU.add), 1)
            ev.inc(V.tensor_reduce(out=part_sb[:, 64:96], in_=grp(skv_acc),
                                   axis=AX.X, op=ALU.add), 1)
            G.wait_ge(ev.h, ev.n)
            stp.inc(G.dma_start(out=part[:, :], in_=part_sb[:, :]), 16)
            G.wait_ge(st[0].h, st[0].n)
            G.wait_ge(st[1].h, st[1].n)
            G.wait_ge(stp.h, stp.n)

    return nc


def build_l0x1():
    nc = bass.Bass("TRN2", target_bir_lowering=False)
    xs = nc.dram_tensor("xs", [D, TPC], F32, kind="ExternalInput").ap()
    qTi = nc.dram_tensor("qTi", [D, TPC], F32, kind="ExternalInput").ap()
    parts = nc.dram_tensor("parts", [NCORE, 128, 96], F32, kind="ExternalInput").ap()
    gw = nc.dram_tensor("gw", [D, D], F32, kind="ExternalInput").ap()
    gb = nc.dram_tensor("gb", [1, D], F32, kind="ExternalInput").ap()
    ing = nc.dram_tensor("ing", [1, D], F32, kind="ExternalInput").ap()
    inb = nc.dram_tensor("inb", [1, D], F32, kind="ExternalInput").ap()
    x1T = nc.dram_tensor("x1T", [D, TPC], F32, kind="ExternalOutput").ap()
    gateT = nc.dram_tensor("gateT", [D, TPC], F32, kind="ExternalOutput").ap()
    xnl = nc.dram_tensor("xnl", [D, 2], F32, kind="ExternalOutput").ap()
    r_scr = nc.dram_tensor("r_scr", [D, TPC], F32).ap()

    EPS1 = 1e-5
    EPS2 = 1e-6

    with ExitStack() as ctx:
        e = ctx.enter_context
        psb = e(nc.sbuf_tensor("psb", [128, NCORE, 96], F32))
        sums = e(nc.sbuf_tensor("sums", [128, 96], F32))
        scr32 = e(nc.sbuf_tensor("scr32", [128, 32], F32))
        s_sb = e(nc.sbuf_tensor("s_sb", [128, 32], F32))
        qtile = [e(nc.sbuf_tensor(f"qtile{p}", [128, TPC], F32)) for p in range(2)]
        xtile = [e(nc.sbuf_tensor(f"xtile{p}", [128, TPC], F32)) for p in range(2)]
        sqt = [e(nc.sbuf_tensor(f"sqt{p}", [128, TPC], BF16)) for p in range(2)]
        onesb = e(nc.sbuf_tensor("onesb", [128, 1], BF16))
        x1bf = e(nc.sbuf_tensor("x1bf", [128, KT, TPC], BF16))
        x1t = [e(nc.sbuf_tensor(f"x1t{p}", [128, TPC], F32)) for p in range(2)]
        wbuf = [e(nc.sbuf_tensor(f"wbuf{p}", [128, KT, 128], BF16)) for p in range(2)]
        gsb = [e(nc.sbuf_tensor(f"gsb{p}", [128, CH], F32)) for p in range(2)]
        ones_c = e(nc.sbuf_tensor("ones_c", [128, 1], F32))
        ones_r = e(nc.sbuf_tensor("ones_r", [1, 128], F32))
        murow = e(nc.sbuf_tensor("murow", [1, TPC], F32))
        q2row = e(nc.sbuf_tensor("q2row", [1, TPC], F32))
        Asb = e(nc.sbuf_tensor("Asb", [128, TPC], F32))
        Bsb = e(nc.sbuf_tensor("Bsb", [128, TPC], F32))
        xl = e(nc.sbuf_tensor("xl", [128, KT, 2], F32))
        xlsq = e(nc.sbuf_tensor("xlsq", [128, KT, 2], F32))
        xn = e(nc.sbuf_tensor("xn", [128, KT, 2], F32))
        prow = e(nc.sbuf_tensor("prow", [1, D], F32))
        gcol = e(nc.sbuf_tensor("gcol", [128, KT], F32))
        bcol = e(nc.sbuf_tensor("bcol", [128, KT], F32))
        gbcol = e(nc.sbuf_tensor("gbcol", [128, KT], F32))
        mrow = e(nc.sbuf_tensor("mrow", [1, 2], F32))
        vrow = e(nc.sbuf_tensor("vrow", [1, 2], F32))
        trow = e(nc.sbuf_tensor("trow", [1, 2], F32))
        mu2b = e(nc.sbuf_tensor("mu2b", [128, 2], F32))
        inv2b = e(nc.sbuf_tensor("inv2b", [128, 2], F32))
        t2 = e(nc.sbuf_tensor("t2", [128, 2], F32))

        stats_ctx = ExitStack()
        stats = [stats_ctx.enter_context(nc.psum_tensor(f"stats{c}", [1, CH], F32))
                 for c in range(NCH)]
        statq = [stats_ctx.enter_context(nc.psum_tensor(f"statq{c}", [1, CH], F32))
                 for c in range(NCH)]
        bps_holder = []

        ld = SC(ctx, nc, "ld")
        rowld = SC(ctx, nc, "rowld")
        qld = [SC(ctx, nc, f"qld{p}") for p in range(2)]
        xld = [SC(ctx, nc, f"xld{p}") for p in range(2)]
        rld = [SC(ctx, nc, f"rld{p}") for p in range(2)]
        wc = [SC(ctx, nc, f"wc{p}") for p in range(2)]
        mm = SC(ctx, nc, "mm")
        ev = SC(ctx, nc, "ev")
        sc = SC(ctx, nc, "sc")
        str_ = [SC(ctx, nc, f"str{p}") for p in range(2)]
        stx = [SC(ctx, nc, f"stx{p}") for p in range(2)]
        stg = [SC(ctx, nc, f"stg{p}") for p in range(2)]
        sto = SC(ctx, nc, "sto")
        block = e(nc.Block())

        @block.gpsimd
        def _(g):
            T, V, SCA, G, SY = nc.tensor, nc.vector, nc.scalar, nc.gpsimd, nc.sync

            ld.inc(SY.dma_start(out=psb[:, :, :], in_=parts.rearrange("r p c -> p r c")), 16)
            ev.inc(V.memset(ones_c[:, :], 1.0), 1)
            ev.inc(V.memset(ones_r[:, :], 1.0), 1)
            ev.inc(V.memset(onesb[:, :], 1.0), 1)
            ev_ones = ev.n

            V.wait_ge(ld.h, 16)
            ev.inc(V.tensor_reduce(out=sums[:, :], in_=psb.rearrange("p r c -> p c r"),
                                   axis=AX.X, op=ALU.add), 1)
            V.wait_ge(ev.h, ev.n)
            ev.inc(V.tensor_tensor(out=scr32[:, :], in0=sums[:, 0:32],
                                   in1=sums[:, 32:64], op=ALU.mult), 1)
            SCA.wait_ge(ev.h, ev.n)
            sc.inc(SCA.activation(out=scr32[:, :], in_=scr32[:, :], func=AF.Sqrt), 1)
            V.wait_ge(sc.h, sc.n)
            ev.inc(V.reciprocal(out=scr32[:, :], in_=scr32[:, :]), 1)
            V.wait_ge(ev.h, ev.n)
            ev.inc(V.tensor_tensor(out=s_sb[:, :], in0=sums[:, 64:96],
                                   in1=scr32[:, :], op=ALU.mult), 1)
            ev_s = ev.n

            q_used_ev = [0, 0]
            r_used_mm = [0, 0]
            r_used_sc = [0, 0]
            V.wait_ge(ev.h, ev_s)
            for j in range(KT):
                p = j % 2
                if j >= 2:
                    SY.wait_ge(ev.h, q_used_ev[p])
                    SY.wait_ge(str_[p].h, str_[p].n)
                    SY.wait_ge(mm.h, r_used_mm[p])
                    SY.wait_ge(sc.h, r_used_sc[p])
                qld[p].inc(SY.dma_start(out=qtile[p][:, :],
                                        in_=qTi[j * 128:(j + 1) * 128, :]), 16)
                xld[p].inc(SY.dma_start(out=xtile[p][:, :],
                                        in_=xs[j * 128:(j + 1) * 128, :]), 16)
                V.wait_ge(qld[p].h, qld[p].n)
                V.wait_ge(xld[p].h, xld[p].n)
                for b in range(2):
                    ev.inc(V.scalar_tensor_tensor(
                        out=xtile[p][:, b * HB:(b + 1) * HB],
                        in0=qtile[p][:, b * HB:(b + 1) * HB],
                        scalar=s_sb[:, j * 2 + b:j * 2 + b + 1],
                        in1=xtile[p][:, b * HB:(b + 1) * HB],
                        op0=ALU.mult, op1=ALU.add), 1)
                ev_r = ev.n
                q_used_ev[p] = ev.n
                SCA.wait_ge(ev.h, ev_r)
                SCA.wait_ge(sc.h, sc.n)
                if j >= 2:
                    SCA.wait_ge(mm.h, r_used_mm[p])
                sc.inc(SCA.activation(out=sqt[p][:, :], in_=xtile[p][:, :],
                                      func=AF.Square), 1)
                T.wait_ge(ev.h, ev_r)
                T.wait_ge(sc.h, sc.n)
                for c in range(NCH):
                    T.matmul(stats[c][:, :], ones_c[:, :], xtile[p][:, c * CH:(c + 1) * CH],
                             start=(j == 0), stop=(j == KT - 1))
                    ins = T.matmul(statq[c][:, :], onesb[:, :], sqt[p][:, c * CH:(c + 1) * CH],
                                   start=(j == 0), stop=(j == KT - 1))
                mm.inc(ins, 1)
                r_used_mm[p] = mm.n
                r_used_sc[p] = sc.n
                SY.wait_ge(ev.h, ev_r)
                str_[p].inc(SY.dma_start(out=r_scr[j * 128:(j + 1) * 128, :],
                                         in_=xtile[p][:, :]), 16)
            mm_stats = mm.n

            V.wait_ge(mm.h, mm_stats)
            for c in range(NCH):
                ev.inc(V.tensor_scalar(out=murow[:, c * CH:(c + 1) * CH], in0=stats[c][:, :],
                                       scalar1=1.0 / D, scalar2=None, op0=ALU.mult), 1)
                ev.inc(V.tensor_scalar(out=q2row[:, c * CH:(c + 1) * CH], in0=statq[c][:, :],
                                       scalar1=1.0 / D, scalar2=None, op0=ALU.mult), 1)
            V.wait_ge(ev.h, ev.n)
            ev.inc(V.tensor_tensor(out=Bsb[0:1, 0:TPC], in0=murow[:, :], in1=murow[:, :],
                                   op=ALU.mult), 1)
            V.wait_ge(ev.h, ev.n)
            ev.inc(V.scalar_tensor_tensor(out=q2row[:, :], in0=q2row[:, :], scalar=EPS1,
                                          in1=Bsb[0:1, 0:TPC], op0=ALU.add,
                                          op1=ALU.subtract), 1)
            SCA.wait_ge(ev.h, ev.n)
            SCA.wait_ge(sc.h, sc.n)
            sc.inc(SCA.activation(out=q2row[:, :], in_=q2row[:, :], func=AF.Sqrt), 1)
            V.wait_ge(sc.h, sc.n)
            ev.inc(V.reciprocal(out=q2row[:, :], in_=q2row[:, :]), 1)
            V.wait_ge(ev.h, ev.n)
            ev.inc(V.scalar_tensor_tensor(out=murow[:, :], in0=murow[:, :], scalar=-1.0,
                                          in1=q2row[:, :], op0=ALU.mult, op1=ALU.mult), 1)
            ev_rows = ev.n
            stats_ctx.close()
            bps = [e(nc.psum_tensor(f"bps{p}", [128, CH], F32)) for p in range(2)]
            bps_holder.append(bps)

            T.wait_ge(ev.h, ev_rows)
            for name, src_ap, dst in ((0, ing, gcol), (1, inb, bcol), (2, gb, gbcol)):
                if name > 0:
                    SY.wait_ge(mm.h, mm.n)
                rowld.inc(SY.dma_start(out=prow[:, :], in_=src_ap[:, :]), 16)
                T.wait_ge(rowld.h, rowld.n)
                if name > 0:
                    T.wait_ge(ev.h, ev.n)
                for j in range(KT):
                    ins = T.matmul(bps[0][:, j:j + 1], prow[:, j * 128:(j + 1) * 128],
                                   ones_c[0:1, :], start=True, stop=True)
                mm.inc(ins, 1)
                V.wait_ge(mm.h, mm.n)
                ev.inc(V.tensor_copy(dst[:, :], bps[0][:, 0:KT]), 1)
            ev_cols = ev.n

            T.wait_ge(ev.h, ev_cols)
            for c in range(NCH):
                if c > 0:
                    T.wait_ge(ev.h, ev.n)
                T.matmul(bps[0][:, :], ones_r[:, :], q2row[:, c * CH:(c + 1) * CH],
                         start=True, stop=True)
                ins = T.matmul(bps[1][:, :], ones_r[:, :], murow[:, c * CH:(c + 1) * CH],
                               start=True, stop=True)
                mm.inc(ins, 1)
                V.wait_ge(mm.h, mm.n)
                ev.inc(V.tensor_copy(Asb[:, c * CH:(c + 1) * CH], bps[0][:, :]), 1)
                ev.inc(V.tensor_copy(Bsb[:, c * CH:(c + 1) * CH], bps[1][:, :]), 1)
            ev_AB = ev.n

            r_bufev = [0, 0]
            for j in range(KT):
                p = j % 2
                if j >= 2:
                    SY.wait_ge(ev.h, r_bufev[p])
                else:
                    SY.wait_ge(str_[p].h, str_[p].n)
                    SY.wait_ge(mm.h, mm_stats)
                    SY.wait_ge(sc.h, r_used_sc[p])
                rld[p].inc(SY.dma_start(out=xtile[p][:, :],
                                        in_=r_scr[j * 128:(j + 1) * 128, :]), 16)
                V.wait_ge(rld[p].h, rld[p].n)
                V.wait_ge(ev.h, ev_AB)
                if j >= 2:
                    V.wait_ge(stx[p].h, stx[p].n)
                ev.inc(V.tensor_tensor(out=x1t[p][:, :], in0=xtile[p][:, :],
                                       in1=Asb[:, :], op=ALU.mult), 1)
                r_bufev[p] = ev.n
                V.wait_ge(ev.h, ev.n)
                ev.inc(V.tensor_tensor(out=x1t[p][:, :], in0=x1t[p][:, :],
                                       in1=Bsb[:, :], op=ALU.add), 1)
                ev_x1 = ev.n
                V.wait_ge(ev.h, ev_x1)
                ev.inc(V.tensor_copy(x1bf[:, j, :], x1t[p][:, :]), 1)
                ev.inc(V.tensor_copy(xl[:, j, 0:1], x1t[p][:, HB - 1:HB]), 1)
                ev.inc(V.tensor_copy(xl[:, j, 1:2], x1t[p][:, TPC - 1:TPC]), 1)
                SY.wait_ge(ev.h, ev.n)
                stx[p].inc(SY.dma_start(out=x1T[j * 128:(j + 1) * 128, :],
                                        in_=x1t[p][:, :]), 16)
            ev_passB = ev.n
            sc_passB = sc.n

            SCA.wait_ge(ev.h, ev_passB)
            SCA.wait_ge(sc.h, sc_passB)
            sc.inc(SCA.activation(out=xlsq[:, :, :], in_=xl[:, :, :], func=AF.Square), 1)
            T.wait_ge(ev.h, ev_passB)
            T.wait_ge(sc.h, sc.n)
            T.wait_ge(ev.h, ev_AB)
            for j in range(KT):
                T.matmul(bps[0][0:1, 0:2], ones_c[:, :], xl[:, j, :],
                         start=(j == 0), stop=(j == KT - 1))
                ins = T.matmul(bps[1][0:1, 0:2], ones_c[:, :], xlsq[:, j, :],
                               start=(j == 0), stop=(j == KT - 1))
            mm.inc(ins, 1)
            V.wait_ge(mm.h, mm.n)
            ev.inc(V.tensor_scalar(out=mrow[:, :], in0=bps[0][0:1, 0:2],
                                   scalar1=1.0 / D, scalar2=None, op0=ALU.mult), 1)
            ev.inc(V.tensor_scalar(out=vrow[:, :], in0=bps[1][0:1, 0:2],
                                   scalar1=1.0 / D, scalar2=None, op0=ALU.mult), 1)
            V.wait_ge(ev.h, ev.n)
            ev.inc(V.tensor_tensor(out=trow[:, :], in0=mrow[:, :], in1=mrow[:, :],
                                   op=ALU.mult), 1)
            V.wait_ge(ev.h, ev.n)
            ev.inc(V.scalar_tensor_tensor(out=vrow[:, :], in0=vrow[:, :], scalar=EPS2,
                                          in1=trow[:, :], op0=ALU.add, op1=ALU.subtract), 1)
            SCA.wait_ge(ev.h, ev.n)
            SCA.wait_ge(sc.h, sc.n)
            sc.inc(SCA.activation(out=vrow[:, :], in_=vrow[:, :], func=AF.Sqrt), 1)
            V.wait_ge(sc.h, sc.n)
            ev.inc(V.reciprocal(out=vrow[:, :], in_=vrow[:, :]), 1)
            ev_nrm = ev.n
            T.wait_ge(ev.h, ev_nrm)
            T.matmul(bps[0][:, 0:2], ones_r[:, :], mrow[:, :], start=True, stop=True)
            ins = T.matmul(bps[1][:, 0:2], ones_r[:, :], vrow[:, :], start=True, stop=True)
            mm.inc(ins, 1)
            V.wait_ge(mm.h, mm.n)
            ev.inc(V.tensor_copy(mu2b[:, :], bps[0][:, 0:2]), 1)
            ev.inc(V.tensor_copy(inv2b[:, :], bps[1][:, 0:2]), 1)
            V.wait_ge(ev.h, ev.n)
            for j in range(KT):
                ev.inc(V.tensor_tensor(out=t2[:, :], in0=xl[:, j, :], in1=mu2b[:, :],
                                       op=ALU.subtract), 1)
                V.wait_ge(ev.h, ev.n)
                ev.inc(V.tensor_tensor(out=t2[:, :], in0=t2[:, :], in1=inv2b[:, :],
                                       op=ALU.mult), 1)
                V.wait_ge(ev.h, ev.n)
                ev.inc(V.tensor_scalar(out=xn[:, j, :], in0=t2[:, :],
                                       scalar1=gcol[:, j:j + 1], scalar2=bcol[:, j:j + 1],
                                       op0=ALU.mult, op1=ALU.add), 1)
                V.wait_ge(ev.h, ev.n)
            G.wait_ge(ev.h, ev.n)
            sto.inc(G.dma_start(out=_r16(xnl), in_=xn[:, :, :]), 16)

            wcast_at = []
            for i in range(2):
                wc[i % 2].inc(G.dma_start(out=wbuf[i % 2][:, :, :],
                                          in_=_r16(gw)[:, :, i * 128:(i + 1) * 128]), 16)
                wcast_at.append((i % 2, wc[i % 2].n))
            bank_free = [0, 0]
            gsb_free = [0, 0]
            for oc in range(KT):
                wp = oc % 2
                T.wait_ge(wc[wcast_at[oc][0]].h, wcast_at[oc][1])
                T.wait_ge(ev.h, ev_passB)
                for c in range(NCH):
                    bk = (oc * NCH + c) % 2
                    if bank_free[bk]:
                        T.wait_ge(sc.h, bank_free[bk])
                    for kt in range(KT):
                        ins = T.matmul(bps[bk][:, :], wbuf[wp][:, kt, :],
                                       x1bf[:, kt, c * CH:(c + 1) * CH],
                                       start=(kt == 0), stop=(kt == KT - 1))
                    mm.inc(ins, 1)
                    SCA.wait_ge(mm.h, mm.n)
                    SCA.wait_ge(stg[bk].h, gsb_free[bk])
                    sc.inc(SCA.activation(out=gsb[bk][:, :],
                                          in_=bps[bk][:, :], func=AF.Sigmoid,
                                          bias=gbcol[:, oc:oc + 1], scale=1.0), 1)
                    bank_free[bk] = sc.n
                    SY.wait_ge(sc.h, sc.n)
                    stg[bk].inc(SY.dma_start(
                        out=gateT[oc * 128:(oc + 1) * 128, c * CH:(c + 1) * CH],
                        in_=gsb[bk][:, :]), 16)
                    gsb_free[bk] = stg[bk].n
                if oc + 2 < KT:
                    G.wait_ge(mm.h, mm.n)
                    wc[oc % 2].inc(G.dma_start(
                        out=wbuf[oc % 2][:, :, :],
                        in_=_r16(gw)[:, :, (oc + 2) * 128:(oc + 3) * 128]), 16)
                    wcast_at.append((oc % 2, wc[oc % 2].n))

            for p in range(2):
                G.wait_ge(stx[p].h, stx[p].n)
                G.wait_ge(stg[p].h, stg[p].n)
            G.wait_ge(sto.h, sto.n)

    return nc


def build_tstep(first):
    nc = bass.Bass("TRN2", target_bir_lowering=False)
    if first:
        xnli = nc.dram_tensor("xnli", [D, 2], F32, kind="ExternalInput").ap()
        aggw = nc.dram_tensor("aggw", [D, D], F32, kind="ExternalInput").ap()
        aggb = nc.dram_tensor("aggb", [1, D], F32, kind="ExternalInput").ap()
    else:
        thpi = nc.dram_tensor("thpi", [D, 2], F32, kind="ExternalInput").ap()
        updsi = nc.dram_tensor("updsi", [128, NCORE * KT * 2], F32, kind="ExternalInput").ap()
        d2br = nc.dram_tensor("d2br", [1, D], F32, kind="ExternalInput").ap()
    png = nc.dram_tensor("png", [1, D], F32, kind="ExternalInput").ap()
    pnb = nc.dram_tensor("pnb", [1, D], F32, kind="ExternalInput").ap()
    d1ws = nc.dram_tensor("d1ws", [D, 1024], F32, kind="ExternalInput").ap()
    d1bs = nc.dram_tensor("d1bs", [1, 1024], F32, kind="ExternalInput").ap()
    d2ws = nc.dram_tensor("d2ws", [1024, D], F32, kind="ExternalInput").ap()
    thv = nc.dram_tensor("thv", [D, 2], F32, kind="ExternalOutput").ap()
    updp = nc.dram_tensor("updp", [D, 2], F32, kind="ExternalOutput").ap()

    EPS2 = 1e-6
    H8 = 1024 // 128

    with ExitStack() as ctx:
        e = ctx.enter_context
        th = e(nc.sbuf_tensor("th", [128, KT, 2], F32))
        thsq = e(nc.sbuf_tensor("thsq", [128, KT, 2], F32))
        tlnb = e(nc.sbuf_tensor("tlnb", [128, KT, 2], BF16))
        tln = e(nc.sbuf_tensor("tln", [128, 2], F32))
        hb = e(nc.sbuf_tensor("hb", [128, H8, 2], BF16))
        w1 = e(nc.sbuf_tensor("w1", [128, KT, 1024], BF16))
        w2 = e(nc.sbuf_tensor("w2", [128, H8, D], BF16))
        wag = e(nc.sbuf_tensor("wag", [128, KT, D], BF16)) if first else None
        upds_sb = e(nc.sbuf_tensor("upds_sb", [128, NCORE, KT * 2], F32)) if not first else None
        thp_sb = e(nc.sbuf_tensor("thp_sb", [128, KT, 2], F32)) if not first else None
        usum = e(nc.sbuf_tensor("usum", [128, KT * 2], F32)) if not first else None
        prow = e(nc.sbuf_tensor("prow", [1, D], F32))
        pgcol = e(nc.sbuf_tensor("pgcol", [128, KT], F32))
        pbcol = e(nc.sbuf_tensor("pbcol", [128, KT], F32))
        acol = e(nc.sbuf_tensor("acol", [128, KT], F32)) if first else None
        d2col = e(nc.sbuf_tensor("d2col", [128, KT], F32)) if not first else None
        d1col = e(nc.sbuf_tensor("d1col", [128, H8], F32))
        xnb = e(nc.sbuf_tensor("xnb", [128, KT, 2], BF16)) if first else None
        ones_c = e(nc.sbuf_tensor("ones_c", [128, 1], F32))
        ones_r = e(nc.sbuf_tensor("ones_r", [1, 128], F32))
        mrow = e(nc.sbuf_tensor("mrow", [1, 2], F32))
        vrow = e(nc.sbuf_tensor("vrow", [1, 2], F32))
        trow = e(nc.sbuf_tensor("trow", [1, 2], F32))
        mu2b = e(nc.sbuf_tensor("mu2b", [128, 2], F32))
        inv2b = e(nc.sbuf_tensor("inv2b", [128, 2], F32))
        outv = e(nc.sbuf_tensor("outv", [128, KT, 2], F32))

        mps = [e(nc.psum_tensor(f"mps{p}", [128, 2], F32)) for p in range(2)]
        sps = [e(nc.psum_tensor(f"sps{p}", [1, 2], F32)) for p in range(2)]
        bcp = [e(nc.psum_tensor(f"bcp{p}", [128, 2], F32)) for p in range(2)]
        tps = e(nc.psum_tensor("tps", [128, KT], F32))

        ld = SC(ctx, nc, "ld")
        rowld = SC(ctx, nc, "rowld")
        wl = SC(ctx, nc, "wl")
        mm = SC(ctx, nc, "mm")
        ev = SC(ctx, nc, "ev")
        sc = SC(ctx, nc, "sc")
        sto = SC(ctx, nc, "sto")
        block = e(nc.Block())

        @block.gpsimd
        def _(g):
            T, V, SCA, G, SY = nc.tensor, nc.vector, nc.scalar, nc.gpsimd, nc.sync

            ev.inc(V.memset(ones_c[:, :], 1.0), 1)
            ev.inc(V.memset(ones_r[:, :], 1.0), 1)
            ev_ones = ev.n
            wl.inc(G.dma_start(out=w1[:, :, :], in_=_r16(d1ws)), 16)
            wl.inc(G.dma_start(out=w2[:, :, :], in_=d2ws.rearrange("(kt p) t -> p kt t", p=128)), 16)
            if first:
                wl.inc(G.dma_start(out=wag[:, :, :], in_=_r16(aggw)), 16)
                ld.inc(G.dma_start(out=xnb[:, :, :], in_=_r16(xnli)), 16)
            else:
                ld.inc(SY.dma_start(out=upds_sb[:, :, :],
                                    in_=updsi.rearrange("p (r m) -> p r m", r=NCORE)), 16)
                ld.inc(SY.dma_start(out=thp_sb[:, :, :], in_=_r16(thpi)), 16)
            wl_n = wl.n
            ld_n = ld.n

            T.wait_ge(ev.h, ev_ones)
            cols = [(png, pgcol, KT), (pnb, pbcol, KT), (d1bs, d1col, H8)]
            cols.append((aggb, acol, KT) if first else (d2br, d2col, KT))
            for src_ap, dst, nt in cols:
                SY.wait_ge(mm.h, mm.n)
                rowld.inc(SY.dma_start(out=prow[:, 0:nt * 128], in_=src_ap[:, :]), 16)
                T.wait_ge(rowld.h, rowld.n)
                T.wait_ge(ev.h, ev.n)
                for j in range(nt):
                    ins = T.matmul(tps[:, j:j + 1], prow[:, j * 128:(j + 1) * 128],
                                   ones_c[0:1, :], start=True, stop=True)
                mm.inc(ins, 1)
                V.wait_ge(mm.h, mm.n)
                ev.inc(V.tensor_copy(dst[:, 0:nt], tps[:, 0:nt]), 1)
            ev_cols = ev.n

            if first:
                T.wait_ge(wl.h, wl_n)
                T.wait_ge(ld.h, ld_n)
                bank_ev = [0, 0]
                for oc in range(KT):
                    bk = oc % 2
                    if bank_ev[bk]:
                        T.wait_ge(ev.h, bank_ev[bk])
                    for kt in range(KT):
                        ins = T.matmul(mps[bk][:, :], wag[:, kt, oc * 128:(oc + 1) * 128],
                                       xnb[:, kt, :], start=(kt == 0), stop=(kt == KT - 1))
                    mm.inc(ins, 1)
                    V.wait_ge(mm.h, mm.n)
                    V.wait_ge(ev.h, ev_cols)
                    ev.inc(V.tensor_scalar(out=th[:, oc, :], in0=mps[bk][:, :],
                                           scalar1=acol[:, oc:oc + 1], scalar2=None,
                                           op0=ALU.add), 1)
                    bank_ev[bk] = ev.n
            else:
                V.wait_ge(ld.h, ld_n)
                V.wait_ge(ev.h, ev_cols)
                ev.inc(V.tensor_reduce(out=usum[:, :],
                                       in_=upds_sb.rearrange("p r m -> p m r"),
                                       axis=AX.X, op=ALU.add), 1)
                V.wait_ge(ev.h, ev.n)
                ev.inc(V.tensor_tensor(out=usum[:, :], in0=usum[:, :],
                                       in1=thp_sb.rearrange("p j c -> p (j c)"),
                                       op=ALU.add), 1)
                V.wait_ge(ev.h, ev.n)
                ev.inc(V.tensor_tensor(
                    out=th[:, :, :],
                    in0=usum[:, :].rearrange("p (j c) -> p j c", c=2),
                    in1=d2col[:, :, None].to_broadcast((128, KT, 2)),
                    op=ALU.add), 1)
            ev_th = ev.n
            SY.wait_ge(ev.h, ev_th)
            sto.inc(SY.dma_start(out=_r16(thv), in_=th[:, :, :]), 16)

            SCA.wait_ge(ev.h, ev_th)
            SCA.wait_ge(sc.h, sc.n)
            sc.inc(SCA.activation(out=thsq[:, :, :], in_=th[:, :, :], func=AF.Square), 1)
            T.wait_ge(ev.h, ev_th)
            T.wait_ge(sc.h, sc.n)
            for j in range(KT):
                T.matmul(sps[0][:, :], ones_c[:, :], th[:, j, :],
                         start=(j == 0), stop=(j == KT - 1))
                ins = T.matmul(sps[1][:, :], ones_c[:, :], thsq[:, j, :],
                               start=(j == 0), stop=(j == KT - 1))
            mm.inc(ins, 1)
            V.wait_ge(mm.h, mm.n)
            ev.inc(V.tensor_scalar(out=mrow[:, :], in0=sps[0][:, :], scalar1=1.0 / D,
                                   scalar2=None, op0=ALU.mult), 1)
            ev.inc(V.tensor_scalar(out=vrow[:, :], in0=sps[1][:, :], scalar1=1.0 / D,
                                   scalar2=None, op0=ALU.mult), 1)
            V.wait_ge(ev.h, ev.n)
            ev.inc(V.tensor_tensor(out=trow[:, :], in0=mrow[:, :], in1=mrow[:, :],
                                   op=ALU.mult), 1)
            V.wait_ge(ev.h, ev.n)
            ev.inc(V.scalar_tensor_tensor(out=vrow[:, :], in0=vrow[:, :], scalar=EPS2,
                                          in1=trow[:, :], op0=ALU.add, op1=ALU.subtract), 1)
            SCA.wait_ge(ev.h, ev.n)
            SCA.wait_ge(sc.h, sc.n)
            sc.inc(SCA.activation(out=vrow[:, :], in_=vrow[:, :], func=AF.Sqrt), 1)
            V.wait_ge(sc.h, sc.n)
            ev.inc(V.reciprocal(out=vrow[:, :], in_=vrow[:, :]), 1)
            ev_nrm = ev.n
            T.wait_ge(ev.h, ev_nrm)
            T.matmul(bcp[0][:, :], ones_r[:, :], mrow[:, :], start=True, stop=True)
            ins = T.matmul(bcp[1][:, :], ones_r[:, :], vrow[:, :], start=True, stop=True)
            mm.inc(ins, 1)
            V.wait_ge(mm.h, mm.n)
            ev.inc(V.tensor_copy(mu2b[:, :], bcp[0][:, :]), 1)
            ev.inc(V.tensor_copy(inv2b[:, :], bcp[1][:, :]), 1)
            V.wait_ge(ev.h, ev.n)
            for j in range(KT):
                ev.inc(V.tensor_tensor(out=tln[:, :], in0=th[:, j, :], in1=mu2b[:, :],
                                       op=ALU.subtract), 1)
                V.wait_ge(ev.h, ev.n)
                ev.inc(V.tensor_tensor(out=tln[:, :], in0=tln[:, :], in1=inv2b[:, :],
                                       op=ALU.mult), 1)
                V.wait_ge(ev.h, ev.n)
                ev.inc(V.tensor_scalar(out=tlnb[:, j, :], in0=tln[:, :],
                                       scalar1=pgcol[:, j:j + 1], scalar2=pbcol[:, j:j + 1],
                                       op0=ALU.mult, op1=ALU.add), 1)
                V.wait_ge(ev.h, ev.n)
            ev_tln = ev.n

            if first:
                pass
            T.wait_ge(wl.h, wl_n)
            bank_sc = [0, 0]
            for oc in range(H8):
                T.wait_ge(ev.h, ev_tln)
                bk = oc % 2
                if bank_sc[bk]:
                    T.wait_ge(sc.h, bank_sc[bk])
                for kt in range(KT):
                    ins = T.matmul(mps[bk][:, :], w1[:, kt, oc * 128:(oc + 1) * 128],
                                   tlnb[:, kt, :], start=(kt == 0), stop=(kt == KT - 1))
                mm.inc(ins, 1)
                SCA.wait_ge(mm.h, mm.n)
                SCA.wait_ge(sc.h, sc.n)
                sc.inc(SCA.activation(out=hb[:, oc, :], in_=mps[bk][:, :],
                                      func=AF.Gelu, bias=d1col[:, oc:oc + 1], scale=1.0), 1)
                bank_sc[bk] = sc.n
            sc_h = sc.n

            bank_ev2 = [0, 0]
            for oc in range(KT):
                T.wait_ge(sc.h, sc_h)
                bk = oc % 2
                if bank_ev2[bk]:
                    T.wait_ge(ev.h, bank_ev2[bk])
                for kt in range(H8):
                    ins = T.matmul(mps[bk][:, :], w2[:, kt, oc * 128:(oc + 1) * 128],
                                   hb[:, kt, :], start=(kt == 0), stop=(kt == H8 - 1))
                mm.inc(ins, 1)
                V.wait_ge(mm.h, mm.n)
                ev.inc(V.tensor_copy(outv[:, oc, :], mps[bk][:, :]), 1)
                bank_ev2[bk] = ev.n
            SY.wait_ge(ev.h, ev.n)
            sto.inc(SY.dma_start(out=_r16(updp), in_=outv[:, :, :]), 16)
            G.wait_ge(sto.h, sto.n)

    return nc


def build_l0tail():
    nc = bass.Bass("TRN2", target_bir_lowering=False)
    x1i = nc.dram_tensor("x1i", [D, TPC], F32, kind="ExternalInput").ap()
    gti = nc.dram_tensor("gti", [D, TPC], F32, kind="ExternalInput").ap()
    thpi = nc.dram_tensor("thpi", [D, 2], F32, kind="ExternalInput").ap()
    updsi = nc.dram_tensor("updsi", [128, NCORE * KT * 2], F32, kind="ExternalInput").ap()
    d2br = nc.dram_tensor("d2br", [1, D], F32, kind="ExternalInput").ap()
    bcw = nc.dram_tensor("bcw", [D, D], F32, kind="ExternalInput").ap()
    bcbr = nc.dram_tensor("bcbr", [1, D], F32, kind="ExternalInput").ap()
    outg = nc.dram_tensor("outg", [1, D], F32, kind="ExternalInput").ap()
    outb = nc.dram_tensor("outb", [1, D], F32, kind="ExternalInput").ap()
    x2T = nc.dram_tensor("x2T", [D, TPC], F32, kind="ExternalOutput").ap()
    x2bf = nc.dram_tensor("x2bf", [D, TPC], BF16, kind="ExternalOutput").ap()
    u_scr = nc.dram_tensor("u_scr", [D, TPC], F32).ap()
    z_scr = nc.dram_tensor("z_scr", [D, TPC], F32).ap()

    EPS1 = 1e-5
    EPS2 = 1e-6

    with ExitStack() as ctx:
        e = ctx.enter_context
        atile = [e(nc.sbuf_tensor(f"atile{p}", [128, TPC], F32)) for p in range(2)]
        btile = [e(nc.sbuf_tensor(f"btile{p}", [128, TPC], F32)) for p in range(2)]
        ctile = [e(nc.sbuf_tensor(f"ctile{p}", [128, TPC], F32)) for p in range(2)]
        sqt = [e(nc.sbuf_tensor(f"sqt{p}", [128, TPC], BF16)) for p in range(2)]
        obf = [e(nc.sbuf_tensor(f"obf{p}", [128, TPC], BF16)) for p in range(2)]
        bcc = e(nc.sbuf_tensor("bcc", [128, KT, 2], F32))
        upds_sb = e(nc.sbuf_tensor("upds_sb", [128, NCORE, KT * 2], F32))
        thp_sb = e(nc.sbuf_tensor("thp_sb", [128, KT, 2], F32))
        usum = e(nc.sbuf_tensor("usum", [128, KT * 2], F32))
        thbf = e(nc.sbuf_tensor("thbf", [128, KT, 2], BF16))
        d2col = e(nc.sbuf_tensor("d2col", [128, KT], F32))
        bccol = e(nc.sbuf_tensor("bccol", [128, KT], F32))
        wbc = e(nc.sbuf_tensor("wbc", [128, KT, D], BF16))
        ones_c = e(nc.sbuf_tensor("ones_c", [128, 1], F32))
        onesb = e(nc.sbuf_tensor("onesb", [128, 1], BF16))
        ones_r = e(nc.sbuf_tensor("ones_r", [1, 128], F32))
        murow = e(nc.sbuf_tensor("murow", [1, TPC], F32))
        q2row = e(nc.sbuf_tensor("q2row", [1, TPC], F32))
        Asb = e(nc.sbuf_tensor("Asb", [128, TPC], F32))
        Bsb = e(nc.sbuf_tensor("Bsb", [128, TPC], F32))
        prow = e(nc.sbuf_tensor("prow", [1, D], F32))
        gcol = e(nc.sbuf_tensor("gcol", [128, KT], F32))
        bcol = e(nc.sbuf_tensor("bcol", [128, KT], F32))

        ld = SC(ctx, nc, "ld")
        rowld = SC(ctx, nc, "rowld")
        wcb = SC(ctx, nc, "wcb")
        ald = [SC(ctx, nc, f"ald{p}") for p in range(2)]
        bld = [SC(ctx, nc, f"bld{p}") for p in range(2)]
        mm = SC(ctx, nc, "mm")
        ev = SC(ctx, nc, "ev")
        sc = SC(ctx, nc, "sc")
        stu = [SC(ctx, nc, f"stu{p}") for p in range(2)]
        stz = [SC(ctx, nc, f"stz{p}") for p in range(2)]
        stx = [SC(ctx, nc, f"stx{p}") for p in range(2)]
        stb = [SC(ctx, nc, f"stb{p}") for p in range(2)]
        block = e(nc.Block())

        @block.gpsimd
        def _(g):
            T, V, SCA, G, SY = nc.tensor, nc.vector, nc.scalar, nc.gpsimd, nc.sync

            ld.inc(SY.dma_start(out=upds_sb[:, :, :],
                                in_=updsi.rearrange("p (r m) -> p r m", r=NCORE)), 16)
            ld.inc(SY.dma_start(out=thp_sb[:, :, :], in_=_r16(thpi)), 16)
            ld_th = ld.n
            ev.inc(V.memset(ones_c[:, :], 1.0), 1)
            ev.inc(V.memset(onesb[:, :], 1.0), 1)
            ev.inc(V.memset(ones_r[:, :], 1.0), 1)
            ev_ones = ev.n

            with nc.psum_tensor("tps", [128, KT], F32) as tps:
                T.wait_ge(ev.h, ev_ones)
                for src_ap, dst in ((outg, gcol), (outb, bcol), (d2br, d2col),
                                    (bcbr, bccol)):
                    SY.wait_ge(mm.h, mm.n)
                    rowld.inc(SY.dma_start(out=prow[:, :], in_=src_ap[:, :]), 16)
                    T.wait_ge(rowld.h, rowld.n)
                    T.wait_ge(ev.h, ev.n)
                    for j in range(KT):
                        ins = T.matmul(tps[:, j:j + 1], prow[:, j * 128:(j + 1) * 128],
                                       ones_c[0:1, :], start=True, stop=True)
                    mm.inc(ins, 1)
                    V.wait_ge(mm.h, mm.n)
                    ev.inc(V.tensor_copy(dst[:, :], tps[:, 0:KT]), 1)
            ev_cols = ev.n

            V.wait_ge(ld.h, ld_th)
            V.wait_ge(ev.h, ev_cols)
            ev.inc(V.tensor_reduce(out=usum[:, :],
                                   in_=upds_sb.rearrange("p r m -> p m r"),
                                   axis=AX.X, op=ALU.add), 1)
            V.wait_ge(ev.h, ev.n)
            ev.inc(V.tensor_tensor(out=usum[:, :], in0=usum[:, :],
                                   in1=thp_sb.rearrange("p j c -> p (j c)"),
                                   op=ALU.add), 1)
            V.wait_ge(ev.h, ev.n)
            ev.inc(V.tensor_tensor(out=thbf[:, :, :],
                                   in0=usum[:, :].rearrange("p (j c) -> p j c", c=2),
                                   in1=d2col[:, :, None].to_broadcast((128, KT, 2)),
                                   op=ALU.add), 1)
            ev_thb = ev.n
            wcb.inc(G.dma_start(out=wbc[:, :, :], in_=_r16(bcw)), 16)
            with nc.psum_tensor("bps_bc", [128, 2], F32) as bps_bc:
                T.wait_ge(wcb.h, wcb.n)
                T.wait_ge(ev.h, ev_thb)
                for oc in range(KT):
                    if oc > 0:
                        T.wait_ge(ev.h, ev.n)
                    for kt in range(KT):
                        ins = T.matmul(bps_bc[:, :], wbc[:, kt, oc * 128:(oc + 1) * 128],
                                       thbf[:, kt, :], start=(kt == 0), stop=(kt == KT - 1))
                    mm.inc(ins, 1)
                    V.wait_ge(mm.h, mm.n)
                    ev.inc(V.tensor_scalar(out=bcc[:, oc, :], in0=bps_bc[:, :],
                                           scalar1=bccol[:, oc:oc + 1], scalar2=None,
                                           op0=ALU.add), 1)
                T.wait_ge(ev.h, ev.n)
            ev_bcc = ev.n

            def pbar():
                for eng in (SY, V):
                    eng.wait_ge(ev.h, ev.n)
                    eng.wait_ge(mm.h, mm.n)
                    eng.wait_ge(sc.h, sc.n)
                    for s in (stu, stz, stx, stb, ald, bld):
                        for p in range(2):
                            eng.wait_ge(s[p].h, s[p].n)

            def stream_ln_pass(in_ld_fns, combine_fn, out_scr, store_sems, with_stats,
                               stats, statq, first_wait_ev):
                used = {"mm": [0, 0], "sc": [0, 0], "ev": [0, 0]}
                for j in range(KT):
                    p = j % 2
                    if j >= 2:
                        SY.wait_ge(ev.h, used["ev"][p])
                        SY.wait_ge(store_sems[p].h, store_sems[p].n)
                        if with_stats:
                            SY.wait_ge(mm.h, used["mm"][p])
                            SY.wait_ge(sc.h, used["sc"][p])
                    for fn in in_ld_fns:
                        fn(j, p)
                    combine_fn(j, p)
                    used["ev"][p] = ev.n
                    if with_stats:
                        SCA.wait_ge(ev.h, ev.n)
                        SCA.wait_ge(sc.h, sc.n)
                        if j >= 2:
                            SCA.wait_ge(mm.h, used["mm"][p])
                        sc.inc(SCA.activation(out=sqt[p][:, :], in_=ctile[p][:, :],
                                              func=AF.Square), 1)
                        T.wait_ge(ev.h, ev.n)
                        T.wait_ge(sc.h, sc.n)
                        if j == 0:
                            T.wait_ge(ev.h, first_wait_ev)
                        for c in range(NCH):
                            T.matmul(stats[c][:, :], ones_c[:, :],
                                     ctile[p][:, c * CH:(c + 1) * CH],
                                     start=(j == 0), stop=(j == KT - 1))
                            ins = T.matmul(statq[c][:, :], onesb[:, :],
                                           sqt[p][:, c * CH:(c + 1) * CH],
                                           start=(j == 0), stop=(j == KT - 1))
                        mm.inc(ins, 1)
                        used["mm"][p] = mm.n
                        used["sc"][p] = sc.n
                    if out_scr is not None:
                        SY.wait_ge(ev.h, ev.n)
                        store_sems[p].inc(SY.dma_start(
                            out=out_scr[j * 128:(j + 1) * 128, :],
                            in_=ctile[p][:, :]), 16)
                return used

            def stats_to_AB(stats, statq, eps):
                V.wait_ge(mm.h, mm.n)
                for c in range(NCH):
                    ev.inc(V.tensor_scalar(out=murow[:, c * CH:(c + 1) * CH],
                                           in0=stats[c][:, :], scalar1=1.0 / D,
                                           scalar2=None, op0=ALU.mult), 1)
                    ev.inc(V.tensor_scalar(out=q2row[:, c * CH:(c + 1) * CH],
                                           in0=statq[c][:, :], scalar1=1.0 / D,
                                           scalar2=None, op0=ALU.mult), 1)
                V.wait_ge(ev.h, ev.n)
                ev.inc(V.tensor_tensor(out=Bsb[0:1, 0:TPC], in0=murow[:, :],
                                       in1=murow[:, :], op=ALU.mult), 1)
                V.wait_ge(ev.h, ev.n)
                ev.inc(V.scalar_tensor_tensor(out=q2row[:, :], in0=q2row[:, :],
                                              scalar=eps, in1=Bsb[0:1, 0:TPC],
                                              op0=ALU.add, op1=ALU.subtract), 1)
                SCA.wait_ge(ev.h, ev.n)
                SCA.wait_ge(sc.h, sc.n)
                sc.inc(SCA.activation(out=q2row[:, :], in_=q2row[:, :], func=AF.Sqrt), 1)
                V.wait_ge(sc.h, sc.n)
                ev.inc(V.reciprocal(out=q2row[:, :], in_=q2row[:, :]), 1)
                V.wait_ge(ev.h, ev.n)
                ev.inc(V.scalar_tensor_tensor(out=murow[:, :], in0=murow[:, :],
                                              scalar=-1.0, in1=q2row[:, :],
                                              op0=ALU.mult, op1=ALU.mult), 1)
                return ev.n

            def bcast_AB(bps):
                for c in range(NCH):
                    T.wait_ge(ev.h, ev.n)
                    T.matmul(bps[0][:, :], ones_r[:, :], q2row[:, c * CH:(c + 1) * CH],
                             start=True, stop=True)
                    ins = T.matmul(bps[1][:, :], ones_r[:, :],
                                   murow[:, c * CH:(c + 1) * CH], start=True, stop=True)
                    mm.inc(ins, 1)
                    V.wait_ge(mm.h, mm.n)
                    ev.inc(V.tensor_copy(Asb[:, c * CH:(c + 1) * CH], bps[0][:, :]), 1)
                    ev.inc(V.tensor_copy(Bsb[:, c * CH:(c + 1) * CH], bps[1][:, :]), 1)
                return ev.n

            def ld_x1(j, p):
                ald[p].inc(SY.dma_start(out=atile[p][:, :],
                                        in_=x1i[j * 128:(j + 1) * 128, :]), 16)
            def ld_gt(j, p):
                bld[p].inc(SY.dma_start(out=btile[p][:, :],
                                        in_=gti[j * 128:(j + 1) * 128, :]), 16)

            def comb_u(j, p):
                V.wait_ge(ald[p].h, ald[p].n)
                V.wait_ge(bld[p].h, bld[p].n)
                V.wait_ge(ev.h, ev_bcc)
                for b in range(2):
                    ev.inc(V.scalar_tensor_tensor(
                        out=ctile[p][:, b * HB:(b + 1) * HB],
                        in0=btile[p][:, b * HB:(b + 1) * HB],
                        scalar=bcc[:, j, b:b + 1],
                        in1=atile[p][:, b * HB:(b + 1) * HB],
                        op0=ALU.mult, op1=ALU.add), 1)

            stats_ctx = ExitStack()
            stats = [stats_ctx.enter_context(nc.psum_tensor(f"s1_{c}", [1, CH], F32))
                     for c in range(NCH)]
            statq = [stats_ctx.enter_context(nc.psum_tensor(f"q1_{c}", [1, CH], F32))
                     for c in range(NCH)]
            stream_ln_pass([ld_x1, ld_gt], comb_u, u_scr, stu, True, stats, statq,
                           ev_cols)
            aba = stats_to_AB(stats, statq, EPS2)
            stats_ctx.close()
            with nc.psum_tensor("bp0", [128, CH], F32) as b0, \
                 nc.psum_tensor("bp1", [128, CH], F32) as b1:
                ev_ab1 = bcast_AB([b0, b1])

            def ld_u(j, p):
                ald[p].inc(SY.dma_start(out=atile[p][:, :],
                                        in_=u_scr[j * 128:(j + 1) * 128, :]), 16)

            def comb_z(j, p):
                V.wait_ge(ald[p].h, ald[p].n)
                V.wait_ge(bld[p].h, bld[p].n)
                V.wait_ge(ev.h, ev_ab1)
                ev.inc(V.tensor_tensor(out=ctile[p][:, :], in0=atile[p][:, :],
                                       in1=Asb[:, :], op=ALU.mult), 1)
                V.wait_ge(ev.h, ev.n)
                ev.inc(V.tensor_tensor(out=ctile[p][:, :], in0=ctile[p][:, :],
                                       in1=Bsb[:, :], op=ALU.add), 1)
                V.wait_ge(ev.h, ev.n)
                ev.inc(V.tensor_scalar(out=ctile[p][:, :], in0=ctile[p][:, :],
                                       scalar1=gcol[:, j:j + 1], scalar2=bcol[:, j:j + 1],
                                       op0=ALU.mult, op1=ALU.add), 1)
                V.wait_ge(ev.h, ev.n)
                ev.inc(V.tensor_tensor(out=ctile[p][:, :], in0=ctile[p][:, :],
                                       in1=btile[p][:, :], op=ALU.add), 1)

            def ld_x1b(j, p):
                bld[p].inc(SY.dma_start(out=btile[p][:, :],
                                        in_=x1i[j * 128:(j + 1) * 128, :]), 16)

            pbar()
            stats_ctx2 = ExitStack()
            stats2 = [stats_ctx2.enter_context(nc.psum_tensor(f"s2_{c}", [1, CH], F32))
                      for c in range(NCH)]
            statq2 = [stats_ctx2.enter_context(nc.psum_tensor(f"q2_{c}", [1, CH], F32))
                      for c in range(NCH)]
            stream_ln_pass([ld_u, ld_x1b], comb_z, z_scr, stz, True, stats2, statq2,
                           ev_ab1)
            stats_to_AB(stats2, statq2, EPS1)
            stats_ctx2.close()
            with nc.psum_tensor("bp2", [128, CH], F32) as b0, \
                 nc.psum_tensor("bp3", [128, CH], F32) as b1:
                ev_ab2 = bcast_AB([b0, b1])

            def ld_z(j, p):
                ald[p].inc(SY.dma_start(out=atile[p][:, :],
                                        in_=z_scr[j * 128:(j + 1) * 128, :]), 16)

            def comb_x2(j, p):
                V.wait_ge(ald[p].h, ald[p].n)
                V.wait_ge(ev.h, ev_ab2)
                if j >= 2:
                    V.wait_ge(stb[p].h, stb[p].n)
                ev.inc(V.tensor_tensor(out=ctile[p][:, :], in0=atile[p][:, :],
                                       in1=Asb[:, :], op=ALU.mult), 1)
                V.wait_ge(ev.h, ev.n)
                ev.inc(V.tensor_tensor(out=ctile[p][:, :], in0=ctile[p][:, :],
                                       in1=Bsb[:, :], op=ALU.add), 1)
                V.wait_ge(ev.h, ev.n)
                ev.inc(V.tensor_copy(obf[p][:, :], ctile[p][:, :]), 1)

            pbar()
            passc_used = [0, 0]
            for j in range(KT):
                p = j % 2
                if j >= 2:
                    SY.wait_ge(ev.h, passc_used[p])
                    SY.wait_ge(stx[p].h, stx[p].n)
                    SY.wait_ge(stb[p].h, stb[p].n)
                ld_z(j, p)
                comb_x2(j, p)
                passc_used[p] = ev.n
                SY.wait_ge(ev.h, ev.n)
                stx[p].inc(SY.dma_start(out=x2T[j * 128:(j + 1) * 128, :],
                                        in_=ctile[p][:, :]), 16)
                stb[p].inc(SY.dma_start(out=x2bf[j * 128:(j + 1) * 128, :],
                                        in_=obf[p][:, :]), 16)

            for p in range(2):
                G.wait_ge(stu[p].h, stu[p].n)
                G.wait_ge(stz[p].h, stz[p].n)
                G.wait_ge(stx[p].h, stx[p].n)
                G.wait_ge(stb[p].h, stb[p].n)

    return nc


def build_l1a():
    nc = bass.Bass("TRN2", target_bir_lowering=False)
    x2f = nc.dram_tensor("x2f", [D, 2 * S], BF16, kind="ExternalInput").ap()
    wvs = nc.dram_tensor("wvs", [D, 256], F32, kind="ExternalInput").ap()
    wqss = nc.dram_tensor("wqss", [D, 256], F32, kind="ExternalInput").ap()
    wkss = nc.dram_tensor("wkss", [D, 256], F32, kind="ExternalInput").ap()
    qTs = nc.dram_tensor("qTs", [256, 2 * S], BF16, kind="ExternalOutput").ap()
    kTs = nc.dram_tensor("kTs", [256, 2 * S], BF16, kind="ExternalOutput").ap()
    vs = nc.dram_tensor("vs", [2 * S, 256], BF16, kind="ExternalOutput").ap()

    NCHB = S // CH

    with ExitStack() as ctx:
        e = ctx.enter_context
        ident = e(nc.sbuf_tensor("ident", [128, 128], BF16))
        iot = e(nc.sbuf_tensor("iot", [128, 128], I32))
        wslc = e(nc.sbuf_tensor("wslc", [128, KT, 256], BF16))
        wkslc = e(nc.sbuf_tensor("wkslc", [128, KT, 256], BF16))
        wvslc = e(nc.sbuf_tensor("wvslc", [128, KT, 256], BF16))
        xch = [e(nc.sbuf_tensor(f"xch{p}", [128, KT, CH], BF16)) for p in range(2)]
        qout = [e(nc.sbuf_tensor(f"qout{p}", [128, 2, CH], BF16)) for p in range(2)]
        kout = [e(nc.sbuf_tensor(f"kout{p}", [128, 2, CH], BF16)) for p in range(2)]
        vT_sb = [e(nc.sbuf_tensor(f"vT{p}", [128, 2, CH], BF16)) for p in range(2)]
        vnat = [e(nc.sbuf_tensor(f"vnat{p}", [128, 4, 256], BF16)) for p in range(2)]

        sld = SC(ctx, nc, "sld")
        xl = [SC(ctx, nc, f"xl{p}") for p in range(2)]
        mm = SC(ctx, nc, "mm")
        ev = SC(ctx, nc, "ev")
        gp = SC(ctx, nc, "gp")
        stq = [SC(ctx, nc, f"stq{p}") for p in range(2)]
        stk = [SC(ctx, nc, f"stk{p}") for p in range(2)]
        stv = [SC(ctx, nc, f"stv{p}") for p in range(2)]
        block = e(nc.Block())

        @block.gpsimd
        def _(g):
            T, V, SCA, G, SY = nc.tensor, nc.vector, nc.scalar, nc.gpsimd, nc.sync

            gp.inc(G.iota(iot[:, :], pattern=[[1, 128]], base=0, channel_multiplier=-1), 1)
            V.wait_ge(gp.h, gp.n)
            ev.inc(V.tensor_scalar(out=ident[:, :], in0=iot[:, :], scalar1=0.0,
                                   scalar2=None, op0=ALU.is_equal), 1)
            ev_id = ev.n
            sld.inc(G.dma_start(out=wvslc[:, :, :], in_=_r16(wvs)), 16)
            sld.inc(G.dma_start(out=wslc[:, :, :], in_=_r16(wqss)), 16)
            sld.inc(G.dma_start(out=wkslc[:, :, :], in_=_r16(wkss)), 16)
            sld_n = sld.n

            xused = [0, 0]
            vused = [0, 0]
            for b in range(2):
                for cc in range(NCHB):
                    p = cc % 2
                    tok0 = b * S + cc * CH
                    if xused[p]:
                        SY.wait_ge(mm.h, xused[p])
                    xl[p].inc(SY.dma_start(
                        out=xch[p][:, :, :],
                        in_=x2f.rearrange("(kt q) t -> q kt t", q=128)[:, :, tok0:tok0 + CH]), 16)
                    T.wait_ge(xl[p].h, xl[p].n)
                    if cc == 0 and b == 0:
                        T.wait_ge(sld.h, sld_n)
                        T.wait_ge(ev.h, ev_id)
                    with ExitStack() as c3:
                        mb = [c3.enter_context(nc.psum_tensor(f"mb{b}_{cc}_{i}", [128, CH], F32))
                              for i in range(6)]
                        tp2 = [c3.enter_context(nc.psum_tensor(f"t2_{b}_{cc}_{i}", [128, 128], BF16))
                               for i in range(2)]
                        specs = [(wslc, 0, "q"), (wslc, 1, "q"), (wkslc, 0, "k"),
                                 (wkslc, 1, "k"), (wvslc, 0, "v"), (wvslc, 1, "v")]
                        bank_ev2 = [0] * 6
                        for si, (wsb, et, kind) in enumerate(specs):
                            bk = si % 6
                            if bank_ev2[bk]:
                                T.wait_ge(ev.h, bank_ev2[bk])
                            for mt in range(KT):
                                ins = T.matmul(mb[bk][:, :],
                                               wsb[:, mt, et * 128:(et + 1) * 128],
                                               xch[p][:, mt, :],
                                               start=(mt == 0), stop=(mt == KT - 1))
                            mm.inc(ins, 1)
                            if si == len(specs) - 1:
                                xused[p] = mm.n
                            V.wait_ge(mm.h, mm.n)
                            if kind == "q":
                                if et == 0:
                                    V.wait_ge(stq[p].h, stq[p].n)
                                ev.inc(V.tensor_copy(qout[p][:, et, :], mb[bk][:, :]), 1)
                            elif kind == "k":
                                if et == 0:
                                    V.wait_ge(stk[p].h, stk[p].n)
                                ev.inc(V.tensor_copy(kout[p][:, et, :], mb[bk][:, :]), 1)
                            else:
                                ev.inc(V.tensor_copy(vT_sb[p][:, et, :], mb[bk][:, :]), 1)
                            bank_ev2[bk] = ev.n
                        ev_vT = ev.n
                        if vused[p]:
                            V.wait_ge(stv[p].h, stv[p].n)
                        tb_ev = [0, 0]
                        for et in range(2):
                            for tt in range(4):
                                tb = (et * 4 + tt) % 2
                                T.wait_ge(ev.h, ev_vT if tb_ev[tb] == 0 else tb_ev[tb])
                                ins = T.transpose(tp2[tb][:, :],
                                                  vT_sb[p][:, et, tt * 128:(tt + 1) * 128],
                                                  ident[:, :])
                                mm.inc(ins, 1)
                                V.wait_ge(mm.h, mm.n)
                                ev.inc(V.tensor_copy(
                                    vnat[p][:, tt, et * 128:(et + 1) * 128],
                                    tp2[tb][:, :]), 1)
                                tb_ev[tb] = ev.n
                        vused[p] = ev.n
                        T.wait_ge(ev.h, ev.n)
                    SY.wait_ge(ev.h, ev.n)
                    stq[p].inc(SY.dma_start(
                        out=qTs.rearrange("(et q) t -> q et t", q=128)[:, :, tok0:tok0 + CH],
                        in_=qout[p][:, :, :]), 16)
                    stk[p].inc(SY.dma_start(
                        out=kTs.rearrange("(et q) t -> q et t", q=128)[:, :, tok0:tok0 + CH],
                        in_=kout[p][:, :, :]), 16)
                    stv[p].inc(SY.dma_start(
                        out=vs.rearrange("(c tt q) e -> q c tt e", q=128, tt=4)[:, b * NCHB + cc, :, :],
                        in_=vnat[p][:, :, :]), 16)
            for p in range(2):
                G.wait_ge(stq[p].h, stq[p].n)
                G.wait_ge(stk[p].h, stk[p].n)
                G.wait_ge(stv[p].h, stv[p].n)

    return nc


def build_l1ind():
    nc = bass.Bass("TRN2", target_bir_lowering=False)
    qlf = nc.dram_tensor("qlf", [D, 2 * S], BF16, kind="ExternalInput").ap()
    klf = nc.dram_tensor("klf", [D, 2 * S], BF16, kind="ExternalInput").ap()
    wqis = nc.dram_tensor("wqis", [D, 256], F32, kind="ExternalInput").ap()
    wkis = nc.dram_tensor("wkis", [D, 256], F32, kind="ExternalInput").ap()
    hwme = nc.dram_tensor("hwme", [1, 1], F32, kind="ExternalInput").ap()
    temp = nc.dram_tensor("temp", [1, 1], F32, kind="ExternalInput").ap()
    relq = nc.dram_tensor("relq", [128, 128], F32, kind="ExternalOutput").ap()

    NCHB = S // CH
    QT = S // 128

    with ExitStack() as ctx:
        e = ctx.enter_context
        ones_r = e(nc.sbuf_tensor("ones_r", [1, 128], F32))
        wqi_sb = e(nc.sbuf_tensor("wqi_sb", [128, KT, 256], BF16))
        wki_sb = e(nc.sbuf_tensor("wki_sb", [128, KT, 256], BF16))
        xch = [e(nc.sbuf_tensor(f"xch{p}", [128, KT, CH], BF16)) for p in range(2)]
        kch = [e(nc.sbuf_tensor(f"kch{p}", [128, KT, CH], BF16)) for p in range(2)]
        qpT = e(nc.sbuf_tensor("qpT", [128, 2, S], BF16))
        kpT = e(nc.sbuf_tensor("kpT", [128, 2, S], BF16))
        relbuf = e(nc.sbuf_tensor("relbuf", [128, 16], F32))
        relsb = e(nc.sbuf_tensor("relsb", [128, 128], F32))
        fac = e(nc.sbuf_tensor("fac", [1, 1], F32))
        hwsb = e(nc.sbuf_tensor("hwsb", [1, 1], F32))
        tsb = e(nc.sbuf_tensor("tsb", [1, 1], F32))
        fcol = e(nc.sbuf_tensor("fcol", [128, 1], F32))
        dmp = e(nc.sbuf_tensor("dmp", [128, CH], BF16))
        dmp2 = e(nc.sbuf_tensor("dmp2", [128, CH], BF16))

        ld = SC(ctx, nc, "ld")
        sld = SC(ctx, nc, "sld")
        xl = [SC(ctx, nc, f"xl{p}") for p in range(2)]
        kl = [SC(ctx, nc, f"kl{p}") for p in range(2)]
        mm = SC(ctx, nc, "mm")
        ev = SC(ctx, nc, "ev")
        sc = SC(ctx, nc, "sc")
        str_ = SC(ctx, nc, "str")
        block = e(nc.Block())

        @block.gpsimd
        def _(g):
            T, V, SCA, G, SY = nc.tensor, nc.vector, nc.scalar, nc.gpsimd, nc.sync

            ev.inc(V.memset(ones_r[:, :], 1.0), 1)
            ev_ones = ev.n
            ld.inc(SY.dma_start(out=hwsb[:, :], in_=hwme[:, :]), 16)
            ld.inc(SY.dma_start(out=tsb[:, :], in_=temp[:, :]), 16)
            SCA.wait_ge(ld.h, 32)
            sc.inc(SCA.activation(out=fac[:, :], in_=tsb[:, :], func=AF.Exp,
                                  scale=-1.0), 1)
            V.wait_ge(sc.h, sc.n)
            ev.inc(V.tensor_tensor(out=fac[:, :], in0=fac[:, :], in1=hwsb[:, :],
                                   op=ALU.mult), 1)
            ev_fac = ev.n
            T.wait_ge(ev.h, ev_fac)
            with nc.psum_tensor("fps", [128, 1], F32) as fps:
                ins = T.matmul(fps[:, :], ones_r[:, :], fac[:, :], start=True, stop=True)
                mm.inc(ins, 1)
                V.wait_ge(mm.h, mm.n)
                ev.inc(V.tensor_copy(fcol[:, :], fps[:, :]), 1)
            ev_fcol = ev.n
            sld.inc(G.dma_start(out=wqi_sb[:, :, :], in_=_r16(wqis)), 16)
            sld.inc(G.dma_start(out=wki_sb[:, :, :], in_=_r16(wkis)), 16)
            sld_n = sld.n

            for b in range(2):
                xused = [0, 0]
                for cc in range(NCHB):
                    p = cc % 2
                    tok0 = b * S + cc * CH
                    if b > 0 or cc >= 2:
                        SY.wait_ge(mm.h, xused[p])
                    xl[p].inc(SY.dma_start(
                        out=xch[p][:, :, :],
                        in_=qlf.rearrange("(kt q) t -> q kt t", q=128)[:, :, tok0:tok0 + CH]), 16)
                    kl[p].inc(SY.dma_start(
                        out=kch[p][:, :, :],
                        in_=klf.rearrange("(kt q) t -> q kt t", q=128)[:, :, tok0:tok0 + CH]), 16)
                    T.wait_ge(xl[p].h, xl[p].n)
                    T.wait_ge(kl[p].h, kl[p].n)
                    if cc == 0 and b == 0:
                        T.wait_ge(sld.h, sld_n)
                    if b > 0 and cc < 2:
                        T.wait_ge(mm.h, mm.n)
                    with ExitStack() as c3:
                        mb = [c3.enter_context(
                            nc.psum_tensor(f"mb{b}_{cc}_{i}", [128, CH], F32))
                            for i in range(4)]
                        specs = [(wqi_sb, xch, qpT, 0), (wqi_sb, xch, qpT, 1),
                                 (wki_sb, kch, kpT, 0), (wki_sb if False else wki_sb, kch, kpT, 1)]
                        bank_ev2 = [0] * 4
                        for si, (wsb, src, dst, et) in enumerate(specs):
                            bk = si % 4
                            if bank_ev2[bk]:
                                T.wait_ge(ev.h, bank_ev2[bk])
                            for mt in range(KT):
                                ins = T.matmul(mb[bk][:, :],
                                               wsb[:, mt, et * 128:(et + 1) * 128],
                                               src[p][:, mt, :],
                                               start=(mt == 0), stop=(mt == KT - 1))
                            mm.inc(ins, 1)
                            if si == len(specs) - 1:
                                xused[p] = mm.n
                            V.wait_ge(mm.h, mm.n)
                            ev.inc(V.tensor_copy(dst[:, et, cc * CH:(cc + 1) * CH],
                                                 mb[bk][:, :]), 1)
                            bank_ev2[bk] = ev.n
                        T.wait_ge(ev.h, ev.n)
                with ExitStack() as c4:
                    sb8 = [c4.enter_context(nc.psum_tensor(f"sb{b}_{i}", [128, CH], F32))
                           for i in range(8)]
                    T.wait_ge(ev.h, ev.n)
                    bank_sc = [0] * 8
                    rel_red_ev = 0
                    for qt in range(QT):
                        for kcg in range(2):
                            for et in range(2):
                                for k8 in range(8):
                                    kc = kcg * 8 + k8
                                    if et == 0 and bank_sc[k8]:
                                        kind_, val_ = bank_sc[k8]
                                        T.wait_ge(sc.h if kind_ == "sc" else ev.h, val_)
                                    ins = T.matmul(
                                        sb8[k8][:, :],
                                        qpT[:, et, qt * 128:(qt + 1) * 128],
                                        kpT[:, et, kc * CH:(kc + 1) * CH],
                                        start=(et == 0), stop=(et == 1))
                                    if et == 1:
                                        mm.inc(ins, 1)
                            for k8 in range(8):
                                col = relbuf[:, kcg * 8 + k8:kcg * 8 + k8 + 1]
                                if k8 % 2 == 0:
                                    SCA.wait_ge(mm.h, mm.n - (7 - k8))
                                    SCA.wait_ge(sc.h, sc.n)
                                    if rel_red_ev:
                                        SCA.wait_ge(ev.h, rel_red_ev)
                                    sc.inc(SCA.activation(
                                        out=dmp[:, :], in_=sb8[k8][:, :], func=AF.Relu,
                                        accum_out=col), 1)
                                    bank_sc[k8] = ("sc", sc.n)
                                else:
                                    V.wait_ge(mm.h, mm.n - (7 - k8))
                                    V.wait_ge(ev.h, ev.n)
                                    if rel_red_ev:
                                        V.wait_ge(ev.h, rel_red_ev)
                                    ev.inc(V.tensor_scalar(
                                        out=dmp2[:, :], in0=sb8[k8][:, :], scalar1=0.0,
                                        scalar2=None, op0=ALU.max, op1=ALU.add,
                                        accum_out=col), 1)
                                    bank_sc[k8] = ("ev", ev.n)
                        V.wait_ge(sc.h, sc.n)
                        V.wait_ge(ev.h, ev.n)
                        ev.inc(V.tensor_reduce(out=relsb[:, b * 64 + qt:b * 64 + qt + 1],
                                               in_=relbuf[:, :], axis=AX.X, op=ALU.add), 1)
                        rel_red_ev = ev.n
                    T.wait_ge(ev.h, ev.n)

            V.wait_ge(ev.h, ev.n)
            V.wait_ge(ev.h, ev_fcol)
            ev.inc(V.tensor_scalar(out=relsb[:, :], in0=relsb[:, :], scalar1=fcol[:, :],
                                   scalar2=None, op0=ALU.mult), 1)
            G.wait_ge(ev.h, ev.n)
            str_.inc(G.dma_start(out=relq[:, :], in_=relsb[:, :]), 16)
            G.wait_ge(str_.h, str_.n)

    return nc


def build_l1b():
    nc = bass.Bass("TRN2", target_bir_lowering=False)
    relqs = nc.dram_tensor("relqs", [NCORE, 128, 128], F32, kind="ExternalInput").ap()
    qTs = nc.dram_tensor("qTs", [256, 2 * S], BF16, kind="ExternalInput").ap()
    kTs = nc.dram_tensor("kTs", [256, 2 * S], BF16, kind="ExternalInput").ap()
    vs = nc.dram_tensor("vs", [2 * S, 256], BF16, kind="ExternalInput").ap()
    avnT = nc.dram_tensor("avnT", [256, 2 * S], BF16, kind="ExternalOutput").ap()

    SCL = 1.0 / np.sqrt(128.0)
    NEG = -1.0e9
    NITER = 48
    QCN = S // CH
    KTN = S // 128
    GRP = [(0, 4), (4, 4), (8, 4), (12, 4)]

    with ExitStack() as ctx:
        e = ctx.enter_context
        relsb = e(nc.sbuf_tensor("relsb", [128, 128], F32))
        rsum = e(nc.sbuf_tensor("rsum", [128, NCORE, 128], F32))
        ones_c = e(nc.sbuf_tensor("ones_c", [128, 1], F32))
        ones_r = e(nc.sbuf_tensor("ones_r", [1, 128], F32))
        identf = e(nc.sbuf_tensor("identf", [128, 128], F32))
        iot = e(nc.sbuf_tensor("iot", [128, 512], I32))
        dvals = e(nc.sbuf_tensor("dvals", [128, 512], F32))
        statm = e(nc.sbuf_tensor("statm", [128, 8, 512], F32))
        statc = e(nc.sbuf_tensor("statc", [128, 4, 512], F32))
        mtile = e(nc.sbuf_tensor("mtile", [128, 512], F32))
        msum = e(nc.sbuf_tensor("msum", [128, 512], F32))
        selb = e(nc.sbuf_tensor("selb", [128, 128], F32))
        gmin = e(nc.sbuf_tensor("gmin", [1, 1], F32))
        gmax = e(nc.sbuf_tensor("gmax", [1, 1], F32))
        lo = e(nc.sbuf_tensor("lo", [1, 1], F32))
        hi = e(nc.sbuf_tensor("hi", [1, 1], F32))
        mid = e(nc.sbuf_tensor("mid", [1, 1], F32))
        dlt = e(nc.sbuf_tensor("dlt", [1, 1], F32))
        pd = e(nc.sbuf_tensor("pd", [1, 1], F32))
        tot = e(nc.sbuf_tensor("tot", [1, 1], F32))
        pred = e(nc.sbuf_tensor("pred", [1, 1], F32))
        mcol = e(nc.sbuf_tensor("mcol", [128, 1], F32))
        locol = e(nc.sbuf_tensor("locol", [128, 1], F32))
        cnt = e(nc.sbuf_tensor("cnt", [128, 1], F32))
        cdmp = e(nc.sbuf_tensor("cdmp", [128, 64], F32))
        red1 = e(nc.sbuf_tensor("red1", [128, 1], F32))
        red2 = e(nc.sbuf_tensor("red2", [1, 128], F32))
        qb = e(nc.sbuf_tensor("qb", [128, 2, S], BF16))
        kb = e(nc.sbuf_tensor("kb", [128, 2, S], BF16))
        vb = e(nc.sbuf_tensor("vb", [128, KTN, 256], BF16))
        avn = e(nc.sbuf_tensor("avn", [128, 2, S], BF16))
        dacc = e(nc.sbuf_tensor("dacc", [128, 4 * CH], F32))
        pex = [e(nc.sbuf_tensor(f"pex{i}", [128, CH], BF16)) for i in range(4)]
        drow = e(nc.sbuf_tensor("drow", [1, CH], F32))
        bsb = e(nc.sbuf_tensor("bsb", [128, CH], F32))

        ld = SC(ctx, nc, "ld")
        bl = SC(ctx, nc, "bl")
        mm = SC(ctx, nc, "mm")
        ev = SC(ctx, nc, "ev")
        sc = SC(ctx, nc, "sc")
        gp = SC(ctx, nc, "gp")
        sta = SC(ctx, nc, "sta")
        block = e(nc.Block())

        @block.gpsimd
        def _(g):
            T, V, SCA, G, SY = nc.tensor, nc.vector, nc.scalar, nc.gpsimd, nc.sync

            gp.inc(G.iota(iot[:, :], pattern=[[1, 512]], base=0, channel_multiplier=-1), 1)
            ld.inc(SY.dma_start(out=rsum[:, :, :], in_=relqs.rearrange("r p c -> p r c")), 16)
            ev.inc(V.memset(ones_c[:, :], 1.0), 1)
            ev.inc(V.memset(ones_r[:, :], 1.0), 1)
            V.wait_ge(gp.h, gp.n)
            ev.inc(V.tensor_scalar(out=identf[:, :], in0=iot[:, 0:128], scalar1=0.0,
                                   scalar2=None, op0=ALU.is_equal), 1)
            ev.inc(V.tensor_copy(dvals[:, :], iot[:, :]), 1)
            V.wait_ge(ld.h, 16)
            ev.inc(V.tensor_reduce(out=relsb[:, :], in_=rsum.rearrange("p r c -> p c r"),
                                   axis=AX.X, op=ALU.add), 1)
            V.wait_ge(ev.h, ev.n)
            ev_rel = ev.n

            offs = [-512, -384, -256, -128, 0, 128, 256, 384]
            off_idx = {o: i for i, o in enumerate(offs)}
            for i, off in enumerate(offs):
                ev.inc(V.tensor_scalar(out=statm[:, i, :], in0=dvals[:, :],
                                       scalar1=float(off), scalar2=None, op0=ALU.is_ge), 1)
                ev.inc(V.tensor_scalar(out=mtile[:, :], in0=dvals[:, :],
                                       scalar1=float(off + 512), scalar2=None,
                                       op0=ALU.is_le), 1)
                V.wait_ge(ev.h, ev.n)
                ev.inc(V.tensor_tensor(out=statm[:, i, :], in0=statm[:, i, :],
                                       in1=mtile[:, :], op=ALU.mult), 1)
                V.wait_ge(ev.h, ev.n)
                ev.inc(V.tensor_scalar(out=statm[:, i, :], in0=statm[:, i, :],
                                       scalar1=-NEG, scalar2=NEG, op0=ALU.mult,
                                       op1=ALU.add), 1)
                V.wait_ge(ev.h, ev.n)
            for ci_, off in enumerate((0, 128, 256, 384)):
                ev.inc(V.tensor_scalar(out=statc[:, ci_, :], in0=dvals[:, :],
                                       scalar1=float(off), scalar2=None,
                                       op0=ALU.is_ge), 1)
                V.wait_ge(ev.h, ev.n)
                ev.inc(V.tensor_scalar(out=statc[:, ci_, :], in0=statc[:, ci_, :],
                                       scalar1=-NEG, scalar2=NEG, op0=ALU.mult,
                                       op1=ALU.add), 1)
                V.wait_ge(ev.h, ev.n)
            ev_stat = ev.n

            with nc.psum_tensor("tps0", [128, 128], F32) as tps0, \
                 nc.psum_tensor("tps1", [1, 1], F32) as tps1:
                for which, opx, dst in ((0, ALU.max, gmax), (1, ALU.min, gmin)):
                    V.wait_ge(ev.h, ev.n)
                    ev.inc(V.tensor_reduce(out=red1[:, :], in_=relsb[:, :],
                                           axis=AX.X, op=opx), 1)
                    T.wait_ge(ev.h, ev.n)
                    ins = T.transpose(tps0[0:1, 0:128], red1[:, 0:1], identf[:, :])
                    mm.inc(ins, 1)
                    V.wait_ge(mm.h, mm.n)
                    ev.inc(V.tensor_copy(red2[:, :], tps0[0:1, 0:128]), 1)
                    V.wait_ge(ev.h, ev.n)
                    ev.inc(V.tensor_reduce(out=dst[:, :], in_=red2[:, :],
                                           axis=AX.X, op=opx), 1)
                    V.wait_ge(ev.h, ev.n)
            ev_mm0 = mm.n

            with nc.psum_tensor("bps0", [128, 1], F32) as bps0, \
                 nc.psum_tensor("bps1", [1, 1], F32) as bps1:
                for b in range(2):
                    rel_b = relsb[:, b * 64:(b + 1) * 64]
                    V.wait_ge(ev.h, ev.n)
                    ev.inc(V.tensor_scalar(out=lo[:, :], in0=gmin[:, :], scalar1=-1.0,
                                           scalar2=None, op0=ALU.add), 1)
                    ev.inc(V.tensor_scalar(out=hi[:, :], in0=gmax[:, :], scalar1=1.0,
                                           scalar2=None, op0=ALU.add), 1)
                    for it in range(NITER):
                        V.wait_ge(ev.h, ev.n)
                        ev.inc(V.tensor_tensor(out=mid[:, :], in0=lo[:, :], in1=hi[:, :],
                                               op=ALU.add), 1)
                        V.wait_ge(ev.h, ev.n)
                        ev.inc(V.tensor_scalar(out=mid[:, :], in0=mid[:, :], scalar1=0.5,
                                               scalar2=None, op0=ALU.mult), 1)
                        T.wait_ge(ev.h, ev.n)
                        ins = T.matmul(bps0[:, :], ones_r[:, :], mid[:, :],
                                       start=True, stop=True)
                        mm.inc(ins, 1)
                        V.wait_ge(mm.h, mm.n)
                        ev.inc(V.tensor_copy(mcol[:, :], bps0[:, :]), 1)
                        V.wait_ge(ev.h, ev.n)
                        ev.inc(V.tensor_scalar(out=cdmp[:, :], in0=rel_b,
                                               scalar1=mcol[:, :], scalar2=None,
                                               op0=ALU.is_ge, op1=ALU.add,
                                               accum_out=cnt[:, :]), 1)
                        T.wait_ge(ev.h, ev.n)
                        ins = T.matmul(bps1[:, :], cnt[:, :], ones_c[:, :],
                                       start=True, stop=True)
                        mm.inc(ins, 1)
                        V.wait_ge(mm.h, mm.n)
                        ev.inc(V.tensor_scalar(out=pred[:, :], in0=bps1[:, :],
                                               scalar1=1024.0, scalar2=None,
                                               op0=ALU.is_ge), 1)
                        V.wait_ge(ev.h, ev.n)
                        ev.inc(V.tensor_tensor(out=dlt[:, :], in0=mid[:, :], in1=lo[:, :],
                                               op=ALU.subtract), 1)
                        V.wait_ge(ev.h, ev.n)
                        ev.inc(V.tensor_tensor(out=pd[:, :], in0=pred[:, :], in1=dlt[:, :],
                                               op=ALU.mult), 1)
                        V.wait_ge(ev.h, ev.n)
                        ev.inc(V.tensor_tensor(out=lo[:, :], in0=lo[:, :], in1=pd[:, :],
                                               op=ALU.add), 1)
                        ev.inc(V.tensor_tensor(out=dlt[:, :], in0=hi[:, :], in1=mid[:, :],
                                               op=ALU.subtract), 1)
                        V.wait_ge(ev.h, ev.n)
                        ev.inc(V.tensor_tensor(out=pd[:, :], in0=pred[:, :], in1=dlt[:, :],
                                               op=ALU.mult), 1)
                        V.wait_ge(ev.h, ev.n)
                        ev.inc(V.tensor_tensor(out=hi[:, :], in0=mid[:, :], in1=pd[:, :],
                                               op=ALU.add), 1)
                    V.wait_ge(ev.h, ev.n)
                    T.wait_ge(ev.h, ev.n)
                    ins = T.matmul(bps0[:, :], ones_r[:, :], lo[:, :],
                                   start=True, stop=True)
                    mm.inc(ins, 1)
                    V.wait_ge(mm.h, mm.n)
                    ev.inc(V.tensor_copy(locol[:, :], bps0[:, :]), 1)
                    V.wait_ge(ev.h, ev.n)
                    ev.inc(V.tensor_scalar(out=selb[:, b * 64:(b + 1) * 64], in0=rel_b,
                                           scalar1=locol[:, :], scalar2=None,
                                           op0=ALU.is_ge), 1)
                    V.wait_ge(ev.h, ev.n)
                    ev.inc(V.tensor_scalar(out=selb[:, b * 64:(b + 1) * 64],
                                           in0=selb[:, b * 64:(b + 1) * 64],
                                           scalar1=-NEG, scalar2=NEG, op0=ALU.mult,
                                           op1=ALU.add), 1)
                    V.wait_ge(ev.h, ev.n)
            ev_sel = ev.n

            pex_mm = [0, 0, 0, 0]
            pex_ev = [0, 0, 0, 0]
            for b in range(2):
                G.wait_ge(mm.h, mm.n)
                G.wait_ge(ev.h, ev.n)
                if b > 0:
                    V.wait_ge(sta.h, sta.n)
                bl.inc(G.dma_start(out=qb[:, :, :],
                                   in_=qTs.rearrange("(et p) t -> p et t", p=128)
                                   [:, :, b * S:(b + 1) * S]), 16)
                bl.inc(G.dma_start(out=kb[:, :, :],
                                   in_=kTs.rearrange("(et p) t -> p et t", p=128)
                                   [:, :, b * S:(b + 1) * S]), 16)
                bl.inc(G.dma_start(out=vb[:, :, :],
                                   in_=vs.rearrange("(c p) e -> p c e", p=128)
                                   [:, b * KTN:(b + 1) * KTN, :]), 16)
                bl_b = bl.n
                for h in range(2):
                    with ExitStack() as c5:
                        avp = [c5.enter_context(
                            nc.psum_tensor(f"avp{b}_{h}_{i}", [128, CH], F32))
                            for i in range(4)]
                        sps = [c5.enter_context(
                            nc.psum_tensor(f"sps{b}_{h}_{i}", [128, CH], F32))
                            for i in range(4)]
                        T.wait_ge(bl.h, bl_b)
                        if b == 0 and h == 0:
                            T.wait_ge(ev.h, ev_sel)
                        for (c0, ncq) in GRP:
                            V.wait_ge(ev.h, ev.n)
                            ev.inc(V.memset(dacc[:, 0:ncq * CH], 0.0), 1)
                            ev_dz = ev.n
                            T.wait_ge(ev.h, ev_dz)
                            ktmax = ((c0 + ncq) * CH - 1) // 128
                            av_open = [False] * ncq
                            sbk_ev = [None] * ncq
                            for kt in range(ktmax + 1):
                                k0 = kt * 128
                                valid = []
                                for ci in range(ncq):
                                    qc = c0 + ci
                                    qlo, qhi = qc * CH, qc * CH + CH - 1
                                    if k0 > qhi:
                                        continue
                                    valid.append((ci, qc, qlo, qhi,
                                                  (k0 + 127) < (qlo - 512)))
                                for ci, qc, qlo, qhi, deep in valid:
                                    if sbk_ev[ci]:
                                        kind_, val_ = sbk_ev[ci]
                                        T.wait_ge(sc.h if kind_ == "sc" else ev.h, val_)
                                    ins = T.matmul(
                                        sps[ci][:, :],
                                        kb[:, h, k0:k0 + 128],
                                        qb[:, h, qc * CH:(qc + 1) * CH],
                                        start=True, stop=True)
                                    mm.inc(ins, 1)
                                mm_lastsc = mm.n
                                for vi, (ci, qc, qlo, qhi, deep) in enumerate(valid):
                                    mm_sc = mm_lastsc - (len(valid) - 1 - vi)
                                    pxb = ci
                                    if deep:
                                        SCA.wait_ge(mm.h, mm_sc)
                                        SCA.wait_ge(sc.h, sc.n)
                                        if pex_mm[pxb]:
                                            SCA.wait_ge(mm.h, pex_mm[pxb])
                                            SCA.wait_ge(ev.h, pex_ev[pxb])
                                        sc.inc(SCA.activation(
                                            out=pex[pxb][:, :], in_=sps[ci][:, :],
                                            func=AF.Exp, scale=SCL,
                                            bias=selb[:, b * 64 + kt:b * 64 + kt + 1]), 1)
                                        sbk_ev[ci] = ("sc", sc.n)
                                    else:
                                        off = k0 - qlo
                                        V.wait_ge(ev.h, ev.n)
                                        ev.inc(V.tensor_scalar(
                                            out=mtile[:, :], in0=statm[:, off_idx[off], :],
                                            scalar1=selb[:, b * 64 + kt:b * 64 + kt + 1],
                                            scalar2=None, op0=ALU.max), 1)
                                        if off >= 0:
                                            V.wait_ge(ev.h, ev.n)
                                            ev.inc(V.tensor_tensor(
                                                out=mtile[:, :], in0=mtile[:, :],
                                                in1=statc[:, off // 128, :],
                                                op=ALU.add), 1)
                                        V.wait_ge(mm.h, mm_sc)
                                        V.wait_ge(ev.h, ev.n)
                                        ev.inc(V.tensor_tensor(
                                            out=msum[:, :], in0=sps[ci][:, :],
                                            in1=mtile[:, :], op=ALU.add), 1)
                                        sbk_ev[ci] = ("ev", ev.n)
                                        SCA.wait_ge(ev.h, ev.n)
                                        SCA.wait_ge(sc.h, sc.n)
                                        if pex_mm[pxb]:
                                            SCA.wait_ge(mm.h, pex_mm[pxb])
                                            SCA.wait_ge(ev.h, pex_ev[pxb])
                                        sc.inc(SCA.activation(
                                            out=pex[pxb][:, :], in_=msum[:, :],
                                            func=AF.Exp, scale=SCL), 1)
                                    V.wait_ge(sc.h, sc.n)
                                    V.wait_ge(ev.h, ev_dz)
                                    ev.inc(V.tensor_tensor(
                                        out=dacc[:, ci * CH:(ci + 1) * CH],
                                        in0=dacc[:, ci * CH:(ci + 1) * CH],
                                        in1=pex[pxb][:, :], op=ALU.add), 1)
                                    pex_ev[pxb] = ev.n
                                    T.wait_ge(sc.h, sc.n)
                                    islast = (kt == min(ktmax, (qhi) // 128))
                                    ins = T.matmul(
                                        avp[ci][:, :], vb[:, kt, h * 128:(h + 1) * 128],
                                        pex[pxb][:, :],
                                        start=(not av_open[ci]), stop=islast)
                                    mm.inc(ins, 1)
                                    pex_mm[pxb] = mm.n
                                    av_open[ci] = True
                            for ci in range(ncq):
                                qc = c0 + ci
                                V.wait_ge(ev.h, ev.n)
                                T.wait_ge(ev.h, ev.n)
                                T.wait_ge(sc.h, sc.n)
                                ins = T.matmul(sps[0][0:1, :], ones_c[:, :],
                                               dacc[:, ci * CH:(ci + 1) * CH],
                                               start=True, stop=True)
                                mm.inc(ins, 1)
                                V.wait_ge(mm.h, mm.n)
                                ev.inc(V.reciprocal(out=drow[:, :], in_=sps[0][0:1, :]), 1)
                                T.wait_ge(ev.h, ev.n)
                                ins = T.matmul(sps[1][:, :], ones_r[:, :], drow[:, :],
                                               start=True, stop=True)
                                mm.inc(ins, 1)
                                V.wait_ge(mm.h, mm.n)
                                ev.inc(V.tensor_copy(bsb[:, :], sps[1][:, :]), 1)
                                V.wait_ge(ev.h, ev.n)
                                ev.inc(V.tensor_tensor(
                                    out=avn[:, h, qc * CH:(qc + 1) * CH],
                                    in0=avp[ci][:, :], in1=bsb[:, :], op=ALU.mult), 1)
                            V.wait_ge(ev.h, ev.n)
                        T.wait_ge(ev.h, ev.n)
                SY.wait_ge(ev.h, ev.n)
                sta.inc(SY.dma_start(
                    out=avnT.rearrange("(et p) t -> p et t", p=128)[:, :, b * S:(b + 1) * S],
                    in_=avn[:, :, :]), 16)
            G.wait_ge(sta.h, sta.n)

    return nc


def build_quant():
    nc = bass.Bass("TRN2", target_bir_lowering=False)
    gur = nc.dram_tensor("gur", [D, D], F32, kind="ExternalInput").ap()
    wpr = nc.dram_tensor("wpr", [256, 4 * D], F32, kind="ExternalInput").ap()
    guq = nc.dram_tensor("guq", [D, D], BF16, kind="ExternalOutput").ap()
    wpq = nc.dram_tensor("wpq", [256, 4 * D], BF16, kind="ExternalOutput").ap()

    F = 2048
    G_ = F // 64

    with ExitStack() as ctx:
        e = ctx.enter_context
        wb = [e(nc.sbuf_tensor(f"wb{p}", [128, F], BF16)) for p in range(2)]
        gs = e(nc.sbuf_tensor("gs", [128, G_], F32))
        scl = e(nc.sbuf_tensor("scl", [128, G_], BF16))
        r32 = e(nc.sbuf_tensor("r32", [128, F], F32))
        sgn = e(nc.sbuf_tensor("sgn", [128, F], F32))
        ra = e(nc.sbuf_tensor("ra", [128, F], F32))
        ri = e(nc.sbuf_tensor("ri", [128, F], I32))
        hi_ = e(nc.sbuf_tensor("hi_", [128, F], I32))
        fr = e(nc.sbuf_tensor("fr", [128, F], F32))
        aa = e(nc.sbuf_tensor("aa", [128, F], F32))
        ee = e(nc.sbuf_tensor("ee", [128, F], F32))
        oo = e(nc.sbuf_tensor("oo", [128, F], F32))
        qa = e(nc.sbuf_tensor("qa", [128, F], F32))
        ob = [e(nc.sbuf_tensor(f"ob{p}", [128, F], BF16)) for p in range(2)]

        ldw = [SC(ctx, nc, f"ldw{p}") for p in range(2)]
        ev = SC(ctx, nc, "ev")
        sc = SC(ctx, nc, "sc")
        sto = [SC(ctx, nc, f"sto{p}") for p in range(2)]
        block = e(nc.Block())

        @block.gpsimd
        def _(g):
            T, V, SCA, G, SY = nc.tensor, nc.vector, nc.scalar, nc.gpsimd, nc.sync

            units = [("gu", t, 0) for t in range(KT)]
            units += [("wp", t, c) for t in range(2) for c in range(4)]
            used_ev = [0, 0]
            for i, (kind, t, c) in enumerate(units):
                p = i % 2
                if i >= 2:
                    G.wait_ge(ev.h, used_ev[p])
                    G.wait_ge(sto[p].h, sto[p].n)
                if kind == "gu":
                    src_ap = gur[t * 128:(t + 1) * 128, :]
                    dst_ap = guq[t * 128:(t + 1) * 128, :]
                else:
                    src_ap = wpr[t * 128:(t + 1) * 128, c * F:(c + 1) * F]
                    dst_ap = wpq[t * 128:(t + 1) * 128, c * F:(c + 1) * F]
                ldw[p].inc(G.dma_start(out=wb[p][:, :], in_=src_ap), 16)
                V.wait_ge(ldw[p].h, ldw[p].n)
                V.wait_ge(ev.h, ev.n)
                gv = lambda x: x.rearrange("q (g n) -> q g n", n=64)
                bscl = scl[:, :].to_broadcast((128, G_, 64))
                ev.inc(V.tensor_reduce(out=gs[:, :], in_=gv(wb[p][:, :]),
                                       axis=AX.X, op=ALU.add,
                                       apply_absolute_value=True), 1)
                V.wait_ge(ev.h, ev.n)
                ev.inc(V.tensor_scalar(out=scl[:, :], in0=gs[:, :],
                                       scalar1=1.0 / 64, scalar2=1e-8,
                                       op0=ALU.mult, op1=ALU.max), 1)
                V.wait_ge(ev.h, ev.n)
                ev.inc(V.reciprocal(out=gs[:, :], in_=scl[:, :]), 1)
                V.wait_ge(ev.h, ev.n)
                ev.inc(V.tensor_tensor(out=gv(r32[:, :]), in0=gv(wb[p][:, :]),
                                       in1=gs[:, :].to_broadcast((128, G_, 64)),
                                       op=ALU.mult), 1)
                SCA.wait_ge(ev.h, ev.n)
                SCA.wait_ge(sc.h, sc.n)
                sc.inc(SCA.activation(out=sgn[:, :], in_=r32[:, :], func=AF.Sign), 1)
                V.wait_ge(sc.h, sc.n)
                ev.inc(V.tensor_tensor(out=ra[:, :], in0=r32[:, :], in1=sgn[:, :],
                                       op=ALU.mult), 1)
                V.wait_ge(ev.h, ev.n)
                ev.inc(V.tensor_copy(ri[:, :], ra[:, :]), 1)
                V.wait_ge(ev.h, ev.n)
                ev.inc(V.tensor_tensor(out=fr[:, :], in0=ra[:, :], in1=ri[:, :],
                                       op=ALU.subtract), 1)
                ev.inc(V.tensor_scalar(out=hi_[:, :], in0=ri[:, :], scalar1=0.5,
                                       scalar2=None, op0=ALU.mult), 1)
                V.wait_ge(ev.h, ev.n)
                ev.inc(V.tensor_scalar(out=aa[:, :], in0=fr[:, :], scalar1=0.5,
                                       scalar2=None, op0=ALU.is_gt), 1)
                ev.inc(V.tensor_scalar(out=ee[:, :], in0=fr[:, :], scalar1=0.5,
                                       scalar2=None, op0=ALU.is_equal), 1)
                ev.inc(V.scalar_tensor_tensor(out=oo[:, :], in0=hi_[:, :], scalar=-2.0,
                                              in1=ri[:, :], op0=ALU.mult,
                                              op1=ALU.add), 1)
                V.wait_ge(ev.h, ev.n)
                ev.inc(V.tensor_tensor(out=ee[:, :], in0=ee[:, :], in1=oo[:, :],
                                       op=ALU.mult), 1)
                V.wait_ge(ev.h, ev.n)
                ev.inc(V.tensor_tensor(out=qa[:, :], in0=ri[:, :], in1=aa[:, :],
                                       op=ALU.add), 1)
                V.wait_ge(ev.h, ev.n)
                ev.inc(V.tensor_tensor(out=qa[:, :], in0=qa[:, :], in1=ee[:, :],
                                       op=ALU.add), 1)
                V.wait_ge(ev.h, ev.n)
                ev.inc(V.tensor_tensor(out=qa[:, :], in0=qa[:, :], in1=sgn[:, :],
                                       op=ALU.mult), 1)
                V.wait_ge(ev.h, ev.n)
                ev.inc(V.tensor_scalar(out=qa[:, :], in0=qa[:, :],
                                       scalar1=-1.0, scalar2=1.0,
                                       op0=ALU.max, op1=ALU.min), 1)
                V.wait_ge(ev.h, ev.n)
                ev.inc(V.tensor_tensor(out=gv(ob[p][:, :]), in0=gv(qa[:, :]),
                                       in1=bscl, op=ALU.mult), 1)
                used_ev[p] = ev.n
                SY.wait_ge(ev.h, ev.n)
                sto[p].inc(SY.dma_start(out=dst_ap, in_=ob[p][:, :]), 16)
            for p in range(2):
                G.wait_ge(sto[p].h, sto[p].n)

    return nc


def build_l1tail():
    nc = bass.Bass("TRN2", target_bir_lowering=False)
    avi = nc.dram_tensor("avi", [D, TPC], BF16, kind="ExternalInput").ap()
    x2i = nc.dram_tensor("x2i", [D, TPC], F32, kind="ExternalInput").ap()
    wo = nc.dram_tensor("wo", [D, D], F32, kind="ExternalInput").ap()
    guqT = nc.dram_tensor("guqT", [D, 8 * D], BF16, kind="ExternalInput").ap()
    wpqT = nc.dram_tensor("wpqT", [4 * D, D], BF16, kind="ExternalInput").ap()
    x4T = nc.dram_tensor("x4T", [D, TPC], F32, kind="ExternalOutput").ap()
    z_scr = nc.dram_tensor("z_scr", [D, TPC], F32).ap()
    x3_scr = nc.dram_tensor("x3_scr", [D, TPC], F32).ap()
    h_scr = nc.dram_tensor("h_scr", [4 * D, TPC], BF16).ap()
    z2_scr = nc.dram_tensor("z2_scr", [D, TPC], F32).ap()

    EPS1 = 1e-5
    EPSR = float(np.finfo(np.float32).eps)
    H4 = 4 * D // 128

    with ExitStack() as ctx:
        e = ctx.enter_context
        ones_c = e(nc.sbuf_tensor("ones_c", [128, 1], F32))
        onesb = e(nc.sbuf_tensor("onesb", [128, 1], BF16))
        ones_r = e(nc.sbuf_tensor("ones_r", [1, 128], F32))
        murow = e(nc.sbuf_tensor("murow", [1, TPC], F32))
        q2row = e(nc.sbuf_tensor("q2row", [1, TPC], F32))
        srow = e(nc.sbuf_tensor("srow", [1, TPC], F32))
        Asb = e(nc.sbuf_tensor("Asb", [128, TPC], F32))
        Bsb = e(nc.sbuf_tensor("Bsb", [128, TPC], F32))
        atile = [e(nc.sbuf_tensor(f"atile{p}", [128, TPC], F32)) for p in range(2)]
        ctile = [e(nc.sbuf_tensor(f"ctile{p}", [128, TPC], F32)) for p in range(2)]
        sqt = [e(nc.sbuf_tensor(f"sqt{p}", [128, TPC], BF16)) for p in range(2)]
        wbuf = [e(nc.sbuf_tensor(f"wbuf{p}", [128, KT, 128], BF16)) for p in range(2)]

        ld = SC(ctx, nc, "ld")
        ald = [SC(ctx, nc, f"ald{p}") for p in range(2)]
        bld = [SC(ctx, nc, f"bld{p}") for p in range(2)]
        wc = [SC(ctx, nc, f"wc{p}") for p in range(2)]
        mm = SC(ctx, nc, "mm")
        ev = SC(ctx, nc, "ev")
        sc = SC(ctx, nc, "sc")
        stz = [SC(ctx, nc, f"stz{p}") for p in range(2)]
        stx = [SC(ctx, nc, f"stx{p}") for p in range(2)]
        sth = [SC(ctx, nc, f"sth{p}") for p in range(2)]
        block = e(nc.Block())

        @block.gpsimd
        def _(g):
            T, V, SCA, G, SY = nc.tensor, nc.vector, nc.scalar, nc.gpsimd, nc.sync

            ev.inc(V.memset(ones_c[:, :], 1.0), 1)
            ev.inc(V.memset(onesb[:, :], 1.0), 1)
            ev.inc(V.memset(ones_r[:, :], 1.0), 1)
            ev_ones = ev.n

            def pbar():
                for eng in (SY, V, T, SCA, G):
                    eng.wait_ge(ev.h, ev.n)
                    eng.wait_ge(mm.h, mm.n)
                    eng.wait_ge(sc.h, sc.n)
                for eng in (SY, V, T, G):
                    for s_ in (stz, stx, sth, ald, bld, wc):
                        for p in range(2):
                            eng.wait_ge(s_[p].h, s_[p].n)

            def stats_to_AB(stats, statq, eps, nmean):
                V.wait_ge(mm.h, mm.n)
                for c in range(NCH):
                    ev.inc(V.tensor_scalar(out=murow[:, c * CH:(c + 1) * CH],
                                           in0=stats[c][:, :], scalar1=1.0 / nmean,
                                           scalar2=None, op0=ALU.mult), 1)
                    ev.inc(V.tensor_scalar(out=q2row[:, c * CH:(c + 1) * CH],
                                           in0=statq[c][:, :], scalar1=1.0 / nmean,
                                           scalar2=None, op0=ALU.mult), 1)
                V.wait_ge(ev.h, ev.n)
                ev.inc(V.tensor_tensor(out=Bsb[0:1, 0:TPC], in0=murow[:, :],
                                       in1=murow[:, :], op=ALU.mult), 1)
                V.wait_ge(ev.h, ev.n)
                ev.inc(V.scalar_tensor_tensor(out=q2row[:, :], in0=q2row[:, :],
                                              scalar=eps, in1=Bsb[0:1, 0:TPC],
                                              op0=ALU.add, op1=ALU.subtract), 1)
                SCA.wait_ge(ev.h, ev.n)
                SCA.wait_ge(sc.h, sc.n)
                sc.inc(SCA.activation(out=q2row[:, :], in_=q2row[:, :], func=AF.Sqrt), 1)
                V.wait_ge(sc.h, sc.n)
                ev.inc(V.reciprocal(out=q2row[:, :], in_=q2row[:, :]), 1)
                V.wait_ge(ev.h, ev.n)
                ev.inc(V.scalar_tensor_tensor(out=murow[:, :], in0=murow[:, :],
                                              scalar=-1.0, in1=q2row[:, :],
                                              op0=ALU.mult, op1=ALU.mult), 1)
                return ev.n

            def bcast_AB(bps):
                for c in range(NCH):
                    T.wait_ge(ev.h, ev.n)
                    T.matmul(bps[0][:, :], ones_r[:, :], q2row[:, c * CH:(c + 1) * CH],
                             start=True, stop=True)
                    ins = T.matmul(bps[1][:, :], ones_r[:, :],
                                   murow[:, c * CH:(c + 1) * CH], start=True, stop=True)
                    mm.inc(ins, 1)
                    V.wait_ge(mm.h, mm.n)
                    ev.inc(V.tensor_copy(Asb[:, c * CH:(c + 1) * CH], bps[0][:, :]), 1)
                    ev.inc(V.tensor_copy(Bsb[:, c * CH:(c + 1) * CH], bps[1][:, :]), 1)
                return ev.n

            with ExitStack() as cA:
                avsb = cA.enter_context(nc.sbuf_tensor("avsb", [128, KT, TPC], BF16))
                mps = [cA.enter_context(nc.psum_tensor(f"mps{i}", [128, CH], F32))
                       for i in range(2)]
                ld.inc(SY.dma_start(out=avsb[:, :, :], in_=_r16(avi)), 16)
                wcast = []
                for i in range(2):
                    wc[i % 2].inc(G.dma_start(out=wbuf[i % 2][:, :, :],
                                              in_=_r16(wo)[:, :, i * 128:(i + 1) * 128]), 16)
                    wcast.append((i % 2, wc[i % 2].n))
                T.wait_ge(ld.h, ld.n)
                T.wait_ge(ev.h, ev_ones)
                bank_ev = [0, 0]
                x2_used = [0, 0]
                for oc in range(KT):
                    p = oc % 2
                    wp = oc % 2
                    T.wait_ge(wc[wcast[oc][0]].h, wcast[oc][1])
                    if oc >= 2:
                        SY.wait_ge(ev.h, x2_used[p])
                        SY.wait_ge(stz[p].h, stz[p].n)
                        SY.wait_ge(ald[p].h, ald[p].n)
                    ald[p].inc(SY.dma_start(out=atile[p][:, :],
                                            in_=x2i[oc * 128:(oc + 1) * 128, :]), 16)
                    for c in range(NCH):
                        bk = (oc * NCH + c) % 2
                        if bank_ev[bk]:
                            T.wait_ge(ev.h, bank_ev[bk])
                        for kt in range(KT):
                            ins = T.matmul(mps[bk][:, :], wbuf[wp][:, kt, :],
                                           avsb[:, kt, c * CH:(c + 1) * CH],
                                           start=(kt == 0), stop=(kt == KT - 1))
                        mm.inc(ins, 1)
                        V.wait_ge(mm.h, mm.n)
                        V.wait_ge(ald[p].h, ald[p].n)
                        if oc >= 2 and c == 0:
                            V.wait_ge(stz[p].h, stz[p].n)
                        ev.inc(V.tensor_tensor(out=ctile[p][:, c * CH:(c + 1) * CH],
                                               in0=mps[bk][:, :],
                                               in1=atile[p][:, c * CH:(c + 1) * CH],
                                               op=ALU.add), 1)
                        bank_ev[bk] = ev.n
                    x2_used[p] = ev.n
                    ev_z = ev.n
                    SY.wait_ge(ev.h, ev_z)
                    stz[p].inc(SY.dma_start(out=z_scr[oc * 128:(oc + 1) * 128, :],
                                            in_=ctile[p][:, :]), 16)
                    if oc + 2 < KT:
                        G.wait_ge(mm.h, mm.n)
                        wc[oc % 2].inc(G.dma_start(
                            out=wbuf[oc % 2][:, :, :],
                            in_=_r16(wo)[:, :, (oc + 2) * 128:(oc + 3) * 128]), 16)
                        wcast.append((oc % 2, wc[oc % 2].n))

            pbar()
            with ExitStack() as cS1:
                s1 = [cS1.enter_context(nc.psum_tensor(f"s1_{c}", [1, CH], F32))
                      for c in range(NCH)]
                q1 = [cS1.enter_context(nc.psum_tensor(f"q1_{c}", [1, CH], F32))
                      for c in range(NCH)]
                zu1 = [[0, 0], [0, 0]]
                for j in range(KT):
                    p = j % 2
                    if j >= 2:
                        SY.wait_ge(mm.h, zu1[p][0])
                        SY.wait_ge(sc.h, zu1[p][1])
                        SY.wait_ge(ald[p].h, ald[p].n)
                    ald[p].inc(SY.dma_start(out=atile[p][:, :],
                                            in_=z_scr[j * 128:(j + 1) * 128, :]), 16)
                    SCA.wait_ge(ald[p].h, ald[p].n)
                    SCA.wait_ge(sc.h, sc.n)
                    if j >= 2:
                        SCA.wait_ge(mm.h, zu1[p][0])
                    sc.inc(SCA.activation(out=sqt[p][:, :], in_=atile[p][:, :],
                                          func=AF.Square), 1)
                    T.wait_ge(ald[p].h, ald[p].n)
                    T.wait_ge(sc.h, sc.n)
                    for c in range(NCH):
                        T.matmul(s1[c][:, :], ones_c[:, :],
                                 atile[p][:, c * CH:(c + 1) * CH],
                                 start=(j == 0), stop=(j == KT - 1))
                        ins = T.matmul(q1[c][:, :], onesb[:, :],
                                       sqt[p][:, c * CH:(c + 1) * CH],
                                       start=(j == 0), stop=(j == KT - 1))
                    mm.inc(ins, 1)
                    zu1[p] = [mm.n, sc.n]
                stats_to_AB(s1, q1, EPS1, D)
            with nc.psum_tensor("bpA", [128, CH], F32) as b0, \
                 nc.psum_tensor("bpB", [128, CH], F32) as b1:
                ev_ab1 = bcast_AB([b0, b1])

            pbar()
            with ExitStack() as cB:
                x3bf = cB.enter_context(nc.sbuf_tensor("x3bf", [128, KT, TPC], BF16))
                x3_used = [0, 0]
                for j in range(KT):
                    p = j % 2
                    if j >= 2:
                        SY.wait_ge(ev.h, x3_used[p])
                        SY.wait_ge(stx[p].h, stx[p].n)
                        SY.wait_ge(ald[p].h, ald[p].n)
                    ald[p].inc(SY.dma_start(out=atile[p][:, :],
                                            in_=z_scr[j * 128:(j + 1) * 128, :]), 16)
                    V.wait_ge(ald[p].h, ald[p].n)
                    V.wait_ge(ev.h, ev_ab1)
                    if j >= 2:
                        V.wait_ge(stx[p].h, stx[p].n)
                    ev.inc(V.tensor_tensor(out=ctile[p][:, :], in0=atile[p][:, :],
                                           in1=Asb[:, :], op=ALU.mult), 1)
                    V.wait_ge(ev.h, ev.n)
                    ev.inc(V.tensor_tensor(out=ctile[p][:, :], in0=ctile[p][:, :],
                                           in1=Bsb[:, :], op=ALU.add), 1)
                    V.wait_ge(ev.h, ev.n)
                    ev.inc(V.tensor_copy(x3bf[:, j, :], ctile[p][:, :]), 1)
                    x3_used[p] = ev.n
                    SY.wait_ge(ev.h, ev.n)
                    stx[p].inc(SY.dma_start(out=x3_scr[j * 128:(j + 1) * 128, :],
                                            in_=ctile[p][:, :]), 16)

                pbar()
                with ExitStack() as cC:
                    s2 = [cC.enter_context(nc.psum_tensor(f"s2_{c}", [1, CH], F32))
                          for c in range(NCH)]
                    gp_ = [cC.enter_context(nc.psum_tensor(f"gp{i}", [128, CH], F32))
                           for i in range(4)]
                    hst = [cC.enter_context(nc.sbuf_tensor(f"hst{i}", [128, TPC], BF16))
                           for i in range(2)]
                    gf = [cC.enter_context(nc.sbuf_tensor(f"gf{i}", [128, CH], F32))
                          for i in range(2)]
                    hsq2 = [cC.enter_context(nc.sbuf_tensor(f"hsq2_{i}", [128, CH], BF16))
                            for i in range(2)]
                    sq_acc = cC.enter_context(nc.sbuf_tensor("sq_acc", [128, TPC], F32))
                    wg = [cC.enter_context(nc.sbuf_tensor(f"wg{i}", [128, KT, 128], BF16))
                          for i in range(4)]
                    wgc = [SC(ctx, nc, f"wgc{i}") for i in range(4)]
                    wgcast = []
                    for i in range(2):
                        G.wait_ge(mm.h, mm.n)
                        wgc[2 * (i % 2)].inc(G.dma_start(
                            out=wg[2 * (i % 2)][:, :, :],
                            in_=_r16(guqT)[:, :, i * 128:(i + 1) * 128]), 16)
                        wgc[2 * (i % 2) + 1].inc(G.dma_start(
                            out=wg[2 * (i % 2) + 1][:, :, :],
                            in_=_r16(guqT)[:, :, (i + 64) * 128:(i + 65) * 128]), 16)
                        wgcast.append((2 * (i % 2), wgc[2 * (i % 2)].n,
                                       wgc[2 * (i % 2) + 1].n))
                    bank_sc2 = [0, 0]
                    V.wait_ge(ev.h, ev.n)
                    ev.inc(V.memset(sq_acc[:, :], 0.0), 1)
                    ev_sqz = ev.n
                    for ht in range(H4):
                        wpi, wv0, wv1 = wgcast[ht]
                        T.wait_ge(wgc[wpi].h, wv0)
                        T.wait_ge(wgc[wpi + 1].h, wv1)
                        p = ht % 2
                        if ht >= 2:
                            V.wait_ge(sth[p].h, sth[p].n)
                        for c in range(NCH):
                            bk = 2 * (c % 2)
                            if bank_sc2[c % 2]:
                                T.wait_ge(ev.h, bank_sc2[c % 2])
                            for kt in range(KT):
                                T.matmul(gp_[bk][:, :], wg[wpi][:, kt, :],
                                         x3bf[:, kt, c * CH:(c + 1) * CH],
                                         start=(kt == 0), stop=(kt == KT - 1))
                                ins = T.matmul(gp_[bk + 1][:, :], wg[wpi + 1][:, kt, :],
                                               x3bf[:, kt, c * CH:(c + 1) * CH],
                                               start=(kt == 0), stop=(kt == KT - 1))
                            mm.inc(ins, 2)
                            SCA.wait_ge(mm.h, mm.n)
                            SCA.wait_ge(sc.h, sc.n)
                            sc.inc(SCA.activation(out=gf[c % 2][:, :], in_=gp_[bk][:, :],
                                                  func=AF.Silu), 1)
                            V.wait_ge(sc.h, sc.n)
                            ev.inc(V.tensor_tensor(out=hst[p][:, c * CH:(c + 1) * CH],
                                                   in0=gf[c % 2][:, :], in1=gp_[bk + 1][:, :],
                                                   op=ALU.mult), 1)
                            ev_h = ev.n
                            SCA.wait_ge(ev.h, ev_h)
                            SCA.wait_ge(sc.h, sc.n)
                            sc.inc(SCA.activation(out=hsq2[c % 2][:, :],
                                                  in_=hst[p][:, c * CH:(c + 1) * CH],
                                                  func=AF.Square), 1)
                            V.wait_ge(sc.h, sc.n)
                            V.wait_ge(ev.h, ev_sqz)
                            ev.inc(V.tensor_tensor(
                                out=sq_acc[:, c * CH:(c + 1) * CH],
                                in0=sq_acc[:, c * CH:(c + 1) * CH],
                                in1=hsq2[c % 2][:, :], op=ALU.add), 1)
                            bank_sc2[c % 2] = ev_h
                        SY.wait_ge(ev.h, ev.n)
                        sth[p].inc(SY.dma_start(out=h_scr[ht * 128:(ht + 1) * 128, :],
                                                in_=hst[p][:, :]), 16)
                        if ht + 2 < H4:
                            G.wait_ge(mm.h, mm.n)
                            i2 = ht + 2
                            wgc[2 * (i2 % 2)].inc(G.dma_start(
                                out=wg[2 * (i2 % 2)][:, :, :],
                                in_=_r16(guqT)[:, :, i2 * 128:(i2 + 1) * 128]), 16)
                            wgc[2 * (i2 % 2) + 1].inc(G.dma_start(
                                out=wg[2 * (i2 % 2) + 1][:, :, :],
                                in_=_r16(guqT)[:, :, (i2 + 64) * 128:(i2 + 65) * 128]), 16)
                            wgcast.append((2 * (i2 % 2), wgc[2 * (i2 % 2)].n,
                                           wgc[2 * (i2 % 2) + 1].n))
                    T.wait_ge(ev.h, ev.n)
                    for c in range(NCH):
                        ins = T.matmul(s2[c][:, :], ones_c[:, :],
                                       sq_acc[:, c * CH:(c + 1) * CH],
                                       start=True, stop=True)
                    mm.inc(ins, 1)
                    V.wait_ge(mm.h, mm.n)
                    for c in range(NCH):
                        ev.inc(V.tensor_scalar(out=srow[:, c * CH:(c + 1) * CH],
                                               in0=s2[c][:, :], scalar1=1.0 / (4 * D),
                                               scalar2=EPSR, op0=ALU.mult,
                                               op1=ALU.add), 1)
                    SCA.wait_ge(ev.h, ev.n)
                    SCA.wait_ge(sc.h, sc.n)
                    sc.inc(SCA.activation(out=srow[:, :], in_=srow[:, :], func=AF.Sqrt), 1)
                    V.wait_ge(sc.h, sc.n)
                    ev.inc(V.reciprocal(out=srow[:, :], in_=srow[:, :]), 1)

            pbar()
            PG = TPC // 4
            NSUB = PG // CH
            with ExitStack() as cD:
                hch = cD.enter_context(nc.sbuf_tensor("hch", [128, H4, PG], BF16))
                wp_ = [cD.enter_context(nc.sbuf_tensor(f"wp{i}", [128, H4, 128], BF16))
                       for i in range(2)]
                sbb = cD.enter_context(nc.sbuf_tensor("sbb", [128, PG], F32))
                z2sb = [cD.enter_context(nc.sbuf_tensor(f"z2sb{i}", [128, PG], F32))
                        for i in range(2)]
                pps = [cD.enter_context(nc.psum_tensor(f"pps{i}", [128, CH], F32))
                       for i in range(2)]
                bps2 = [cD.enter_context(nc.psum_tensor(f"bps2_{i}", [128, CH], F32))
                        for i in range(2)]
                hl = SC(ctx, nc, "hl")
                wpc = [SC(ctx, nc, f"wpc{i}") for i in range(2)]
                z2_used = [0, 0]
                for grp_i in range(4):
                    tk0 = grp_i * PG
                    pbar()
                    hl.inc(SY.dma_start(
                        out=hch[:, :, :],
                        in_=_r16(h_scr)[:, :, tk0:tk0 + PG]), 16)
                    T.wait_ge(ev.h, ev.n)
                    for sub in range(NSUB):
                        ins = T.matmul(bps2[sub][:, :], ones_r[:, :],
                                       srow[:, tk0 + sub * CH:tk0 + (sub + 1) * CH],
                                       start=True, stop=True)
                    mm.inc(ins, 1)
                    V.wait_ge(mm.h, mm.n)
                    for sub in range(NSUB):
                        ev.inc(V.tensor_copy(sbb[:, sub * CH:(sub + 1) * CH],
                                             bps2[sub][:, :]), 1)
                    ev_sbb = ev.n
                    wpcast = []
                    for i in range(2):
                        G.wait_ge(mm.h, mm.n)
                        wpc[i].inc(G.dma_start(
                            out=wp_[i][:, :, :],
                            in_=_r16(wpqT)[:, :, i * 128:(i + 1) * 128]), 16)
                        wpcast.append(wpc[i].n)
                    T.wait_ge(hl.h, hl.n)
                    bank_ev3 = [0, 0]
                    for oc in range(KT):
                        p = oc % 2
                        T.wait_ge(wpc[p].h, wpcast[oc])
                        if oc >= 2:
                            SY.wait_ge(ev.h, z2_used[p])
                            SY.wait_ge(stz[p].h, stz[p].n)
                            SY.wait_ge(bld[p].h, bld[p].n)
                        bld[p].inc(SY.dma_start(
                            out=atile[p][:, 0:PG],
                            in_=x3_scr[oc * 128:(oc + 1) * 128, tk0:tk0 + PG]), 16)
                        for sub in range(NSUB):
                            bk = sub
                            if bank_ev3[bk]:
                                T.wait_ge(ev.h, bank_ev3[bk])
                            for kt in range(H4):
                                ins = T.matmul(pps[bk][:, :], wp_[p][:, kt, :],
                                               hch[:, kt, sub * CH:(sub + 1) * CH],
                                               start=(kt == 0), stop=(kt == H4 - 1))
                            mm.inc(ins, 1)
                            V.wait_ge(mm.h, mm.n)
                            V.wait_ge(ev.h, ev_sbb)
                            V.wait_ge(bld[p].h, bld[p].n)
                            if oc >= 2 and sub == 0:
                                V.wait_ge(stz[p].h, stz[p].n)
                            ev.inc(V.tensor_tensor(out=z2sb[p][:, sub * CH:(sub + 1) * CH],
                                                   in0=pps[bk][:, :],
                                                   in1=sbb[:, sub * CH:(sub + 1) * CH],
                                                   op=ALU.mult), 1)
                            V.wait_ge(ev.h, ev.n)
                            ev.inc(V.tensor_tensor(out=z2sb[p][:, sub * CH:(sub + 1) * CH],
                                                   in0=z2sb[p][:, sub * CH:(sub + 1) * CH],
                                                   in1=atile[p][:, sub * CH:(sub + 1) * CH],
                                                   op=ALU.add), 1)
                            bank_ev3[bk] = ev.n
                        z2_used[p] = ev.n
                        SY.wait_ge(ev.h, ev.n)
                        stz[p].inc(SY.dma_start(
                            out=z2_scr[oc * 128:(oc + 1) * 128, tk0:tk0 + PG],
                            in_=z2sb[p][:, :]), 16)
                        if oc + 2 < KT:
                            G.wait_ge(mm.h, mm.n)
                            wpc[p].inc(G.dma_start(
                                out=wp_[p][:, :, :],
                                in_=_r16(wpqT)[:, :, (oc + 2) * 128:(oc + 3) * 128]), 16)
                            wpcast.append(wpc[p].n)

            pbar()
            with ExitStack() as cE:
                s3 = [cE.enter_context(nc.psum_tensor(f"s3_{c}", [1, CH], F32))
                      for c in range(NCH)]
                q3 = [cE.enter_context(nc.psum_tensor(f"q3_{c}", [1, CH], F32))
                      for c in range(NCH)]
                zu = [[0, 0], [0, 0]]
                for j in range(KT):
                    p = j % 2
                    if j >= 2:
                        SY.wait_ge(mm.h, zu[p][0])
                        SY.wait_ge(sc.h, zu[p][1])
                        SY.wait_ge(ald[p].h, ald[p].n)
                    ald[p].inc(SY.dma_start(out=atile[p][:, :],
                                            in_=z2_scr[j * 128:(j + 1) * 128, :]), 16)
                    SCA.wait_ge(ald[p].h, ald[p].n)
                    SCA.wait_ge(sc.h, sc.n)
                    if j >= 2:
                        SCA.wait_ge(mm.h, zu[p][0])
                    sc.inc(SCA.activation(out=sqt[p][:, :], in_=atile[p][:, :],
                                          func=AF.Square), 1)
                    T.wait_ge(ald[p].h, ald[p].n)
                    T.wait_ge(sc.h, sc.n)
                    for c in range(NCH):
                        T.matmul(s3[c][:, :], ones_c[:, :],
                                 atile[p][:, c * CH:(c + 1) * CH],
                                 start=(j == 0), stop=(j == KT - 1))
                        ins = T.matmul(q3[c][:, :], onesb[:, :],
                                       sqt[p][:, c * CH:(c + 1) * CH],
                                       start=(j == 0), stop=(j == KT - 1))
                    mm.inc(ins, 1)
                    zu[p] = [mm.n, sc.n]
                stats_to_AB(s3, q3, EPS1, D)
            with nc.psum_tensor("bpC", [128, CH], F32) as b0, \
                 nc.psum_tensor("bpD", [128, CH], F32) as b1:
                ev_ab2 = bcast_AB([b0, b1])
            pbar()
            x4u = [0, 0]
            for j in range(KT):
                p = j % 2
                if j >= 2:
                    SY.wait_ge(ev.h, x4u[p])
                    SY.wait_ge(stx[p].h, stx[p].n)
                    SY.wait_ge(ald[p].h, ald[p].n)
                ald[p].inc(SY.dma_start(out=atile[p][:, :],
                                        in_=z2_scr[j * 128:(j + 1) * 128, :]), 16)
                V.wait_ge(ald[p].h, ald[p].n)
                V.wait_ge(ev.h, ev_ab2)
                if j >= 2:
                    V.wait_ge(stx[p].h, stx[p].n)
                ev.inc(V.tensor_tensor(out=ctile[p][:, :], in0=atile[p][:, :],
                                       in1=Asb[:, :], op=ALU.mult), 1)
                V.wait_ge(ev.h, ev.n)
                ev.inc(V.tensor_tensor(out=ctile[p][:, :], in0=ctile[p][:, :],
                                       in1=Bsb[:, :], op=ALU.add), 1)
                x4u[p] = ev.n
                SY.wait_ge(ev.h, ev.n)
                stx[p].inc(SY.dma_start(out=x4T[j * 128:(j + 1) * 128, :],
                                        in_=ctile[p][:, :]), 16)
            for p in range(2):
                G.wait_ge(stx[p].h, stx[p].n)
                G.wait_ge(stz[p].h, stz[p].n)
                G.wait_ge(sth[p].h, sth[p].n)

    return nc


def _run(nc, in_maps):
    res = run_bass_kernel_spmd(nc, in_maps, core_ids=list(range(NCORE)))
    return res.results


def kernel(**inputs):
    ii = {k: np.asarray(v) for k, v in inputs.items()}

    x = ii["L0_attn_x"].astype(np.float32, copy=False)
    xT = np.ascontiguousarray(x.transpose(2, 0, 1).reshape(D, 2 * S))

    def tok_shard(full):
        out = []
        for c in range(NCORE):
            out.append(np.ascontiguousarray(np.concatenate(
                [full[:, c * HB:(c + 1) * HB],
                 full[:, S + c * HB:S + (c + 1) * HB]], axis=1)))
        return out

    def tok_assemble(shards, dtype):
        full = np.empty((shards[0].shape[0], 2 * S), dtype=dtype)
        for c in range(NCORE):
            full[:, c * HB:(c + 1) * HB] = shards[c][:, :HB]
            full[:, S + c * HB:S + (c + 1) * HB] = shards[c][:, HB:]
        return full

    xs = tok_shard(xT)
    row = lambda v: np.ascontiguousarray(v.astype(np.float32, copy=False).reshape(1, -1))

    wgu = ii["L1_ffn_w_gate_up"].astype(np.float32, copy=False)
    wpj = ii["L1_ffn_w_proj"].astype(np.float32, copy=False)
    r0 = _run(build_quant(), [
        {"gur": np.ascontiguousarray(wgu[c * 2048:(c + 1) * 2048, :]),
         "wpr": np.ascontiguousarray(wpj[c * 256:(c + 1) * 256, :])}
        for c in range(NCORE)])
    guq = np.concatenate([np.asarray(r0[c]["guq"]) for c in range(NCORE)], axis=0)
    wpq = np.concatenate([np.asarray(r0[c]["wpq"]) for c in range(NCORE)], axis=0)
    guqT = np.ascontiguousarray(guq.T)
    wpqT = np.ascontiguousarray(wpq.T)

    w = np.ascontiguousarray(ii["L0_attn_W"].astype(np.float32, copy=False))
    bv = row(ii["L0_attn_b"])
    r1 = _run(build_l0qkv(), [{"xs": xs[c], "w": w, "bvec": bv} for c in range(NCORE)])
    parts = np.ascontiguousarray(np.stack([np.asarray(r1[c]["part"]) for c in range(NCORE)]))

    gw = np.ascontiguousarray(ii["L0_ffn_gate_w"].astype(np.float32, copy=False))
    r2 = _run(build_l0x1(), [
        {"xs": xs[c], "qTi": np.asarray(r1[c]["qT"]), "parts": parts, "gw": gw,
         "gb": row(ii["L0_ffn_gate_b"]), "ing": row(ii["L0_ffn_in_gamma"]),
         "inb": row(ii["L0_ffn_in_beta"])} for c in range(NCORE)])

    xnl = np.ascontiguousarray(np.asarray(r2[NCORE - 1]["xnl"]))
    d1w = ii["L0_ffn_d1_w"].astype(np.float32, copy=False)
    d2w = ii["L0_ffn_d2_w"].astype(np.float32, copy=False)
    png_, pnb_ = row(ii["L0_ffn_pn_gamma"]), row(ii["L0_ffn_pn_beta"])
    d1b_ = row(ii["L0_ffn_d1_b"])
    d2b_ = row(ii["L0_ffn_d2_b"])
    tmaps = [{"png": png_, "pnb": pnb_,
              "d1ws": np.ascontiguousarray(d1w[:, c * 1024:(c + 1) * 1024]),
              "d1bs": np.ascontiguousarray(d1b_[:, c * 1024:(c + 1) * 1024]),
              "d2ws": np.ascontiguousarray(d2w[c * 1024:(c + 1) * 1024, :])}
             for c in range(NCORE)]

    def pack_upds(rs):
        a = np.stack([np.asarray(rs[c]["updp"]) for c in range(NCORE)])
        return np.ascontiguousarray(
            a.reshape(NCORE, KT, 128, 2).transpose(2, 0, 1, 3).reshape(128, NCORE * KT * 2))

    nc_first = build_tstep(True)
    nc_next = build_tstep(False)
    rt = _run(nc_first, [
        {**tmaps[c], "xnli": xnl,
         "aggw": np.ascontiguousarray(ii["L0_ffn_agg_w"].astype(np.float32, copy=False)),
         "aggb": row(ii["L0_ffn_agg_b"])} for c in range(NCORE)])
    for _step in range(3):
        thv = np.ascontiguousarray(np.asarray(rt[0]["thv"]))
        rt = _run(nc_next, [
            {**tmaps[c], "thpi": thv, "updsi": pack_upds(rt), "d2br": d2b_}
            for c in range(NCORE)])

    thv = np.ascontiguousarray(np.asarray(rt[0]["thv"]))
    upds = pack_upds(rt)
    r4 = _run(build_l0tail(), [
        {"x1i": np.asarray(r2[c]["x1T"]), "gti": np.asarray(r2[c]["gateT"]),
         "thpi": thv, "updsi": upds, "d2br": d2b_,
         "bcw": np.ascontiguousarray(ii["L0_ffn_bc_w"].astype(np.float32, copy=False)),
         "bcbr": row(ii["L0_ffn_bc_b"]),
         "outg": row(ii["L0_ffn_out_gamma"]),
         "outb": row(ii["L0_ffn_out_beta"])} for c in range(NCORE)])
    x2bf_full = tok_assemble([np.asarray(r4[c]["x2bf"]) for c in range(NCORE)], BF)

    wq = ii["L1_attn_Wq"].astype(np.float32, copy=False)
    wk = ii["L1_attn_Wk"].astype(np.float32, copy=False)
    wv = ii["L1_attn_Wv"].astype(np.float32, copy=False)
    wqi = ii["L1_attn_Wq_ind"].astype(np.float32, copy=False)
    wki = ii["L1_attn_Wk_ind"].astype(np.float32, copy=False)
    hw = ii["L1_attn_head_weights"].astype(np.float32, copy=False).reshape(-1)
    temp = ii["L1_attn_temperature_param"].astype(np.float32, copy=False).reshape(1, 1)
    r5 = _run(build_l1a(), [
        {"x2f": x2bf_full,
         "wvs": np.ascontiguousarray(wv[:, c * 256:(c + 1) * 256]),
         "wqss": np.ascontiguousarray(wq[:, c * 256:(c + 1) * 256]),
         "wkss": np.ascontiguousarray(wk[:, c * 256:(c + 1) * 256])}
        for c in range(NCORE)])

    qlf = np.concatenate([np.asarray(r5[c]["qTs"]) for c in range(NCORE)], axis=0)
    klf = np.concatenate([np.asarray(r5[c]["kTs"]) for c in range(NCORE)], axis=0)
    r5b = _run(build_l1ind(), [
        {"qlf": qlf, "klf": klf,
         "wqis": np.ascontiguousarray(wqi[:, c * 256:(c + 1) * 256]),
         "wkis": np.ascontiguousarray(wki[:, c * 256:(c + 1) * 256]),
         "hwme": np.ascontiguousarray(hw[c].reshape(1, 1)),
         "temp": np.ascontiguousarray(temp)} for c in range(NCORE)])
    relqs = np.ascontiguousarray(np.stack([np.asarray(r5b[c]["relq"]) for c in range(NCORE)]))

    r6 = _run(build_l1b(), [
        {"relqs": relqs, "qTs": np.asarray(r5[c]["qTs"]),
         "kTs": np.asarray(r5[c]["kTs"]), "vs": np.asarray(r5[c]["vs"])}
        for c in range(NCORE)])
    avn_full = np.concatenate([np.asarray(r6[c]["avnT"]) for c in range(NCORE)], axis=0)
    avi = tok_shard(avn_full)

    wo = np.ascontiguousarray(ii["L1_attn_Wo"].astype(np.float32, copy=False))
    r7 = _run(build_l1tail(), [
        {"avi": avi[c], "x2i": np.asarray(r4[c]["x2T"]), "wo": wo,
         "guqT": guqT, "wpqT": wpqT} for c in range(NCORE)])

    outT = tok_assemble([np.asarray(r7[c]["x4T"]) for c in range(NCORE)], np.float32)
    return np.ascontiguousarray(outT.reshape(D, 2, S).transpose(1, 2, 0))
